# version 07
# baseline (speedup 1.0000x reference)
import contextlib

import numpy as np
import ml_dtypes

import concourse.bass as bass
import concourse.mybir as mybir
from concourse.bass_utils import run_bass_kernel_spmd

F32 = mybir.dt.float32
BF16 = mybir.dt.bfloat16
AF = mybir.ActivationFunctionType
ALU = mybir.AluOpType
AX = mybir.AxisListType.X
BFDT = ml_dtypes.bfloat16

B, S, D = 4, 2048, 4096
H, KVH, G, DH = 32, 8, 4, 128
HID = 11008
NO = HID // 128
CB = 16
NBLK = S // CB
NCK = NBLK + 1
WIN = 64
NWTOK = 1152
QH = 1024
NT = QH // 128
SCALE = DH ** -0.5
NEG = float(np.float32(BFDT(-1e30)))
V30 = -NEG
LN1E10 = float(np.log(1e-10))
LORA_SCALE = 2.0
NCORES = 8



def _tables_for_core(h):
    qoff = QH * h
    f32 = np.float32
    inv = 1.0 / 10000.0 ** (np.arange(0, DH, 2, dtype=np.float64) / DH)

    def rope_tabs(positions):
        ang = positions[None, :] * inv[:, None]
        cosf = np.repeat(np.cos(ang), 2, axis=0).astype(f32)
        sinf = np.repeat(np.sin(ang), 2, axis=0).astype(f32)
        sinf[0::2, :] *= -1.0
        return np.ascontiguousarray(cosf), np.ascontiguousarray(sinf)

    cosk, sink = rope_tabs(np.arange(S, dtype=np.float64))
    cosq, sinq = rope_tabs(np.arange(qoff, qoff + QH, dtype=np.float64))
    cosw, sinw = rope_tabs(np.arange(qoff - WIN, qoff - WIN + NWTOK,
                                     dtype=np.float64))

    CM = np.zeros((NT, 128, NCK), dtype=f32)
    qpos = qoff + 128 * np.arange(NT)[:, None] + np.arange(128)[None, :]
    ends = 16 * np.arange(NBLK) + 15
    CM[:, :, :NBLK] = np.where(ends[None, None, :] < qpos[:, :, None], 0.0, NEG)

    SM = np.full((NT, 128, 192), NEG, dtype=f32)
    r = np.arange(128)[None, :, None]
    c = np.arange(192)[None, None, :]
    keypos = (qoff + 128 * np.arange(NT))[:, None, None] - WIN + c
    SM[(c >= r) & (c <= r + WIN) & (keypos >= 0)] = 0.0

    T2 = np.zeros((128, 3200), dtype=BFDT)
    for p in range(128):
        T2[p, qoff + 1024 + p] = BFDT(1.0)
    return dict(cosk=cosk, sink=sink, cosq=cosq, sinq=sinq, cosw=cosw, sinw=sinw,
                CM=np.ascontiguousarray(CM), SM=np.ascontiguousarray(SM), T2=T2)


def _shared_consts():
    ident = np.eye(128, dtype=BFDT)
    psw = np.zeros((128, 128), dtype=BFDT)
    for k in range(128):
        psw[k, k ^ 1] = BFDT(1.0)
    onesb = np.ones((1, 128), dtype=BFDT)
    EL = np.zeros((32, 512), dtype=BFDT)
    for bb in range(32):
        EL[bb, bb * 16:(bb + 1) * 16] = BFDT(1.0)
    DL = np.zeros((128, 128), dtype=BFDT)
    v30b = BFDT(V30)
    p = np.arange(128)
    rr = np.arange(128)
    m = (rr[None, :] // 16 == p[:, None] // 16) & (rr[None, :] % 16 >= p[:, None] % 16)
    DL[m] = v30b
    return dict(ident=ident, psw=psw, onesb=onesb, EL=EL, DL=DL)


def _stage_weights(I):
    f32 = np.float32
    ac = np.ascontiguousarray
    w = {}
    w["wqkvT"] = ac(I["L0_attn_w_qkv"].T.reshape(D, 48, 128).transpose(1, 0, 2))
    w["woT"] = ac(I["L0_attn_w_o"].T.reshape(32, 128, 8, 512).transpose(2, 0, 1, 3))
    w["w1T"] = ac(I["L0_ffn_w1"].T.reshape(D, NO, 128).transpose(1, 0, 2))
    w["w3T"] = ac(I["L0_ffn_w3"].T.reshape(D, NO, 128).transpose(1, 0, 2))
    w["w2T"] = ac(I["L0_ffn_w2"].T.reshape(NO, 128, 8, 512).transpose(2, 0, 1, 3))
    w["kw1T"] = ac(I["L0_attn_k_w1"].T.reshape(16, 128, 16, 128).transpose(2, 0, 1, 3))
    w["vw1T"] = ac(I["L0_attn_v_w1"].T.reshape(16, 128, 16, 128).transpose(2, 0, 1, 3))
    w["kw2T"] = ac(I["L0_attn_k_w2"].T.reshape(16, 128, 128))
    w["vw2T"] = ac(I["L0_attn_v_w2"].T.reshape(16, 128, 128))
    w["kb1"] = ac(I["L0_attn_k_b1"].reshape(16, 128).T.astype(f32))
    w["vb1"] = ac(I["L0_attn_v_b1"].reshape(16, 128).T.astype(f32))
    w["kb2"] = ac(I["L0_attn_k_b2"].reshape(128, 1).astype(f32))
    w["vb2"] = ac(I["L0_attn_v_b2"].reshape(128, 1).astype(f32))
    w["kposT"] = ac(I["L0_attn_k_pos"].transpose(2, 0, 1).reshape(128, 128))
    w["vposT"] = ac(I["L0_attn_v_pos"].transpose(2, 0, 1).reshape(128, 128))
    w["memk"] = ac(I["L0_attn_mem_kv"][0, :, 0, :].reshape(KVH, 128, 1))
    w["memv"] = ac(I["L0_attn_mem_kv"][1, :, 0, :])
    w["wgT"] = ac(I["L0_attn_w_gate"].T.reshape(32, 128, 96))
    w["bg"] = ac(I["L0_attn_b_gate"].reshape(1, 96))
    w["A1T"] = ac(I["L0_ffn_lora_A1"].T.reshape(32, 128, 16))
    w["A3T"] = ac(I["L0_ffn_lora_A3"].T.reshape(32, 128, 16))
    w["A2T"] = ac(I["L0_ffn_lora_A2"].T.reshape(NO, 128, 16))
    w["B1T"] = ac(I["L0_ffn_lora_B1"].T)
    w["B3T"] = ac(I["L0_ffn_lora_B3"].T)
    w["B2T"] = ac(I["L0_ffn_lora_B2"].T)
    return w


def stage_inputs(inputs):
    x = np.asarray(inputs["L0_attn_hidden_states"], dtype=np.float32)
    w = _stage_weights(inputs)
    consts = _shared_consts()
    tabs = [_tables_for_core(h) for h in (0, 1)]
    in_maps = []
    for core in range(NCORES):
        bb, h = core // 2, core % 2
        qoff = QH * h
        xw = np.zeros((NWTOK, D), dtype=np.float32)
        lo, hi = qoff - WIN, qoff - WIN + NWTOK
        slo, shi = max(lo, 0), min(hi, S)
        xw[slo - lo: shi - lo, :] = x[bb, slo:shi, :]
        m = dict(w)
        m.update(consts)
        m.update(tabs[h])
        m["xf"] = np.ascontiguousarray(x[bb])
        m["xo"] = np.ascontiguousarray(x[bb, qoff:qoff + QH])
        m["xw"] = xw
        in_maps.append(m)
    return in_maps



ENGS = ("te", "ve", "se", "gp", "sy")
DMA_ENGS = ("gp", "sy")


class Sched:

    def __init__(self):
        self.q = {e: [] for e in ENGS}
        self.cnt = {}
        for e in ENGS:
            self.cnt[e] = 0
        self.sem = {}
        self.tags = []
        self.buf = {}
        self.hwm = {e: {} for e in ENGS}

    def _need_to_waits(self, eng, need):
        out = []
        for s2, n in sorted(need.items()):
            if n > self.hwm[eng].get(s2, 0):
                self.hwm[eng][s2] = n
                out.append((s2, n))
        return out

    def _waits(self, eng, reads, writes, sreads):
        need = {}

        def add(sname, n):
            if sname.startswith("dma_"):
                n = self.cnt[sname]
            need[sname] = max(need.get(sname, 0), n)

        strict = eng in ("ve", "se", "gp")
        for key in list(reads) + list(sreads):
            ent = self.buf.get(key)
            if ent is None:
                continue
            we, wn, haz = ent["w"]
            if we is None:
                continue
            if we != eng or haz or strict or key in sreads:
                add(we, wn)
        for key in writes:
            ent = self.buf.get(key)
            if ent is None:
                continue
            we, wn, haz = ent["w"]
            if we is not None and (we != eng or strict):
                add(we, wn)
            for re_, rn in ent["r"]:
                if re_ != eng or strict:
                    add(re_, rn)
        return self._need_to_waits(eng, need)

    def op(self, eng, build, reads=(), writes=(), sreads=(), hazard=False,
           tag=None):
        builds = build if isinstance(build, (list, tuple)) else [build]
        is_dma = tag is not None
        assert not (eng == "sy" and tag is None), "sync DMA op needs tag"
        if is_dma:
            sname = "dma_" + tag
            if sname not in self.cnt:
                self.cnt[sname] = 0
                self.tags.append(sname)
        else:
            sname = eng
        waits = self._waits(eng, reads, writes, sreads)
        self.cnt[sname] += 16 if is_dma else 1
        n = self.cnt[sname]

        def fn(h, _w=waits, _b=builds, _s=sname, _dma=is_dma):
            for s2, v in _w:
                h.wait_ge(self.sem[s2], v)
            ins = None
            for bb in _b:
                ins = bb(h)
            ins.then_inc(self.sem[_s], 16 if _dma else 1)
        self.q[eng].append(fn)
        for key in writes:
            self.buf[key] = {"w": (sname, n, hazard), "r": []}
        for key in reads:
            ent = self.buf.setdefault(key, {"w": (None, 0, False), "r": []})
            ent["r"].append((sname, n))
        return n

    def raw_wait(self, eng, s2, n=None):
        v = self.cnt[s2] if n is None else n
        if v > self.hwm[eng].get(s2, 0):
            self.hwm[eng][s2] = v
            self.q[eng].append(lambda h, _s2=s2, _v=v: h.wait_ge(self.sem[_s2], _v))

    def barrier(self, engs=ENGS):
        on = [e for e in ENGS if e != "sy"] + self.tags
        for e in engs:
            for s2 in on:
                if s2 != e and self.cnt.get(s2, 0) > 0:
                    self.raw_wait(e, s2)



def _dram_inputs(nc):
    t = {}

    def di(name, shape, dtype=F32):
        t[name] = nc.dram_tensor(name, list(shape), dtype, kind="ExternalInput").ap()

    di("xf", (S, D)); di("xo", (QH, D)); di("xw", (NWTOK, D))
    di("wqkvT", (48, D, 128)); di("woT", (8, 32, 128, 512))
    di("w1T", (NO, D, 128)); di("w3T", (NO, D, 128)); di("w2T", (8, NO, 128, 512))
    di("kw1T", (16, 16, 128, 128)); di("vw1T", (16, 16, 128, 128))
    di("kw2T", (16, 128, 128)); di("vw2T", (16, 128, 128))
    di("kb1", (128, 16)); di("vb1", (128, 16))
    di("kb2", (128, 1)); di("vb2", (128, 1))
    di("kposT", (128, 128)); di("vposT", (128, 128))
    di("memk", (KVH, 128, 1)); di("memv", (8, 128))
    di("wgT", (32, 128, 96)); di("bg", (1, 96))
    di("A1T", (32, 128, 16)); di("A3T", (32, 128, 16)); di("A2T", (NO, 128, 16))
    di("B1T", (16, HID)); di("B3T", (16, HID)); di("B2T", (16, D))
    di("cosk", (128, S)); di("sink", (128, S))
    di("cosq", (128, QH)); di("sinq", (128, QH))
    di("cosw", (128, NWTOK)); di("sinw", (128, NWTOK))
    di("CM", (NT, 128, NCK)); di("SM", (NT, 128, 192))
    di("T2", (128, 3200), BF16)
    di("ident", (128, 128), BF16); di("psw", (128, 128), BF16)
    di("onesb", (1, 128), BF16); di("EL", (32, 512), BF16); di("DL", (128, 128), BF16)
    return t


class Bld:
    pass


def build_program(upto=9, debug=()):
    nc = bass.Bass("TRN2", target_bir_lowering=False, debug=False)
    b = Bld()
    b.nc = nc
    b.s = Sched()
    b.di = _dram_inputs(nc)
    b.out = nc.dram_tensor("out", [QH, D], F32, kind="ExternalOutput").ap()
    b.q_scr = nc.dram_tensor("q_scr", [KVH, 128, G * QH], BF16, kind="Internal").ap()
    b.g_scr = nc.dram_tensor("g_scr", [NO, 128, QH], BF16, kind="Internal").ap()
    b.a_scr = nc.dram_tensor("a_scr", [KVH, 128, G * QH], BF16, kind="Internal").ap()

    es = contextlib.ExitStack()
    with es:
        sbuf = nc.sbuf_tensor
        b.ckf = es.enter_context(sbuf("ckf", [128, KVH * NCK], BF16))
        b.cvt1 = es.enter_context(sbuf("cvt1", [128, KVH * 128], BF16))
        b.cvt2 = es.enter_context(sbuf("cvt2", [1, KVH * 128], BF16))
        b.gates = es.enter_context(sbuf("gates", [128, NT * 96], F32))
        b.t2b = es.enter_context(sbuf("t2b", [16, QH], BF16))
        b.ident = es.enter_context(sbuf("sb_idn", [128, 128], BF16))
        b.psw = es.enter_context(sbuf("sb_psw", [128, 128], BF16))
        b.onesb = es.enter_context(sbuf("sb_onesb", [1, 128], BF16))
        b.el = es.enter_context(sbuf("sb_el", [32, 512], BF16))
        b.dl = es.enter_context(sbuf("sb_dl", [128, 128], BF16))
        b.stat = es.enter_context(sbuf("stat", [128, 64], F32))

        b.dumps = dict(debug) if not isinstance(debug, dict) else debug
        _p_consts(b)
        with contextlib.ExitStack() as xs:
            b.xn = xs.enter_context(sbuf("xn", [128, 32 * QH], BF16))
            with contextlib.ExitStack() as ps:
                _p0(b, ps)
            if upto >= 1:
                b.s.barrier()
                with contextlib.ExitStack() as ps:
                    _p4a(b, ps)
            if upto >= 2:
                b.s.barrier()
                with contextlib.ExitStack() as ps:
                    _p4b(b, ps)
            _emit_dumps(b, only_scope="p04")
        if upto >= 3:
            b.s.barrier()
            with contextlib.ExitStack() as kvs:
                b.kf = kvs.enter_context(sbuf("kf", [128, KVH * S], BF16))
                b.vf = kvs.enter_context(sbuf("vf", [128, KVH * S], BF16))
                b.kwin = kvs.enter_context(sbuf("kwin", [128, KVH * NWTOK], BF16))
                b.vwin = kvs.enter_context(sbuf("vwin", [128, KVH * NWTOK], BF16))
                with contextlib.ExitStack() as ps:
                    _p1(b, ps)
                if upto >= 4:
                    b.s.barrier()
                    with contextlib.ExitStack() as ps:
                        _p2(b, ps)
                _emit_dumps(b, only_scope="p12")
                if upto >= 5:
                    b.s.barrier()
                    with contextlib.ExitStack() as ps:
                        _p3(b, ps)
                        _emit_dumps(b, only_scope="p3")
        if upto >= 6:
            b.s.barrier()
            with contextlib.ExitStack() as ps:
                _p5(b, ps)
        _emit_dumps(b, only_scope="dram")
        _replay(b)
    return nc


def _emit_dumps(b, only_scope):
    s = b.s
    todo = [(n, sp) for n, (sc, sp) in b.dumps.items() if sc == only_scope]
    if not todo:
        return
    s.barrier(engs=("sy",))
    for name, src_fn in todo:
        ap, shape, dt = src_fn(b)
        dst = b.nc.dram_tensor("dump_" + name, list(shape), dt,
                               kind="ExternalOutput").ap()
        s.op("sy", lambda h, ap=ap, dst=dst: h.dma_start(out=dst, in_=ap),
             reads=(), writes=(), tag="dump")


def _replay(b):
    s = b.s
    with contextlib.ExitStack() as es:
        for e in ENGS:
            s.sem[e] = es.enter_context(b.nc.semaphore(f"sem_{e}"))
        for t in s.tags:
            s.sem[t] = es.enter_context(b.nc.semaphore(t))
        block = es.enter_context(b.nc.Block())

        @block.tensor
        def _(h):
            for fn in s.q["te"]:
                fn(h)

        @block.vector
        def _(h):
            for fn in s.q["ve"]:
                fn(h)

        @block.scalar
        def _(h):
            for fn in s.q["se"]:
                fn(h)

        @block.gpsimd
        def _(h):
            for fn in s.q["gp"]:
                fn(h)

        @block.sync
        def _(h):
            for fn in s.q["sy"]:
                fn(h)
            for t in s.tags:
                h.wait_ge(s.sem[t], s.cnt[t])


def _p_consts(b):
    s, di = b.s, b.di
    for dst, src in ((b.ident, "ident"), (b.psw, "psw"), (b.onesb, "onesb"),
                     (b.el, "EL"), (b.dl, "DL")):
        s.op("sy", lambda h, d=dst, sr=src: h.dma_start(out=d[:], in_=di[sr]),
             writes=(src,), tag="consts")
    s.op("ve", lambda h: h.memset(b.stat[:, 60:61], -1000.0), writes=("c1000",))



def _ln_tiles(b, src_ap, ntile, sink_fn, xbuf, xnt):
    s, st = b.s, b.stat
    for i in range(ntile):
        sl = (i % 2) * D
        xk = f"lnx{i % 2}"
        s.op("sy", lambda h, i=i, sl=sl: h.dma_start(
            out=xbuf[:, sl:sl + D], in_=src_ap[i * 128:(i + 1) * 128, :]),
            reads=(), writes=(xk,), tag=xk)
        xs = xbuf[:, sl:sl + D]
        s.op("se", lambda h, xs=xs: h.activation(xnt[:], xs, AF.Square,
                                                 accum_out=st[:, 1:2]),
             reads=(xk,), writes=("xnt", "ss"), hazard=True)
        s.op("ve", lambda h, xs=xs: h.reduce_sum(st[:, 0:1], xs, axis=AX),
             reads=(xk,), writes=("s1",), hazard=True)
        s.op("ve", lambda h: h.tensor_scalar(st[:, 2:3], st[:, 0:1], 1.0 / D,
                                             None, ALU.mult),
             reads=("s1",), writes=("mean",))
        s.op("ve", lambda h: h.tensor_scalar(st[:, 4:5], st[:, 1:2], 1.0 / D,
                                             None, ALU.mult),
             reads=("ss",), writes=("ex2",))
        s.op("ve", lambda h: h.tensor_tensor(st[:, 3:4], st[:, 2:3], st[:, 2:3],
                                             ALU.mult),
             reads=("mean",), writes=("m2",))
        s.op("ve", lambda h: h.tensor_tensor(st[:, 5:6], st[:, 4:5], st[:, 3:4],
                                             ALU.subtract),
             reads=("ex2", "m2"), writes=("var",))
        s.op("ve", lambda h: h.tensor_scalar(st[:, 6:7], st[:, 5:6], 1e-5, None,
                                             ALU.add),
             reads=("var",), writes=("vpe",))
        s.op("se", lambda h: h.activation(st[:, 7:8], st[:, 6:7], AF.Sqrt),
             reads=("vpe",), writes=("std",))
        s.op("ve", lambda h: h.reciprocal(st[:, 8:9], st[:, 7:8]),
             reads=("std",), writes=("rstd",), hazard=True)
        s.op("ve", lambda h, xs=xs: h.tensor_scalar(xnt[:], xs, st[:, 2:3],
                                                    st[:, 8:9], ALU.subtract,
                                                    ALU.mult),
             reads=(xk, "mean", "rstd"), writes=("xnt",),
             sreads=("mean", "rstd"))
        sink_fn(i)


def _xn_transpose_sink(b, xnt, pst, dst, dst_stride, dstkey, base_col=0):
    s = b.s

    def sink(i):
        for jj in range(8):
            pk = f"pst{jj % 2}"
            builds = []
            for cc in range(4):
                c = jj * 4 + cc
                builds.append(lambda h, c=c, cc=cc, jj=jj: h.transpose(
                    pst[jj % 2][:, cc * 128:(cc + 1) * 128],
                    xnt[:, c * 128:(c + 1) * 128], b.ident[:]))
            s.op("te", builds, reads=("xnt", "ident"), writes=(pk,))
            dv = dst[:].rearrange("p (c t) -> p c t", t=dst_stride)
            s.op("ve", lambda h, jj=jj, i=i, dv=dv: h.tensor_copy(
                dv[:, jj * 4:(jj + 1) * 4,
                   base_col + i * 128: base_col + (i + 1) * 128],
                pst[jj % 2][:].rearrange("p (c t) -> p c t", t=128)),
                reads=(pk,), writes=(dstkey,))
    return sink



def _p0(b, ps):
    nc, s = b.nc, b.s
    xbuf = ps.enter_context(nc.sbuf_tensor("p0xb", [128, 2 * D], F32))
    xnt = ps.enter_context(nc.sbuf_tensor("p0xnt", [128, D], BF16))
    pst = [ps.enter_context(nc.psum_tensor(f"p0pst{i}", [128, 512], BF16))
           for i in range(2)]
    sink = _xn_transpose_sink(b, xnt, pst, b.xn, QH, "xndst")
    _ln_tiles(b, b.di["xo"], NT, sink, xbuf, xnt)



def _p4a(b, ps):
    nc, s, di = b.nc, b.s, b.di
    wq = ps.enter_context(nc.sbuf_tensor("wq", [128, 2 * D], BF16))
    qo = ps.enter_context(nc.sbuf_tensor("qo", [128, 2 * QH], BF16))
    wg = ps.enter_context(nc.sbuf_tensor("wg", [128, 32 * 96], BF16))
    bgb = ps.enter_context(nc.sbuf_tensor("bgb", [1, 96], BF16))
    psq = [ps.enter_context(nc.psum_tensor(f"psq{i}", [128, 512], F32))
           for i in range(2)]
    psg = ps.enter_context(nc.psum_tensor("psg", [128, 96], F32))

    s.op("gp", lambda h: h.dma_start(
        out=wg[:].rearrange("p (c n) -> p c n", n=96),
        in_=di["wgT"].rearrange("c p n -> p c n")),
        writes=("wg",), tag="p4aw")
    s.op("gp", lambda h: h.dma_start(out=bgb[:], in_=di["bg"]), writes=("bgb",),
         tag="p4aw")
    for o in range(32):
        wk = f"wq{o % 2}"
        s.op("gp", lambda h, o=o: h.dma_start(
            out=wq[:, (o % 2) * D:(o % 2) * D + D].rearrange(
                "p (c n) -> p c n", n=128),
            in_=di["wqkvT"][o].rearrange("(c p) n -> p c n", p=128)),
            writes=(wk,), tag=wk)
        wqs = wq[:, (o % 2) * D:(o % 2) * D + D].rearrange("p (c n) -> p c n", n=128)
        for half in range(2):
            pk = f"psq{half}"
            builds = [lambda h, c=c, half=half, wqs=wqs: h.matmul(
                psq[half][:], wqs[:, c, :],
                b.xn[:, c * QH + half * 512: c * QH + half * 512 + 512],
                start=(c == 0), stop=(c == 31)) for c in range(32)]
            s.op("te", builds, reads=(wk, "xndst"), writes=(pk,))
            qk = f"qo{o % 2}"
            s.op("se", lambda h, o=o, half=half: h.activation(
                qo[:, (o % 2) * QH + half * 512:(o % 2) * QH + half * 512 + 512],
                psq[half][:], AF.Copy, bias=0.0, scale=SCALE),
                reads=(pk,), writes=(qk + f"h{half}",))
        s.op("sy", lambda h, o=o: h.dma_start(
            out=b.q_scr[o // 4][:, (o % 4) * QH:(o % 4 + 1) * QH],
            in_=qo[:, (o % 2) * QH:(o % 2) * QH + QH]),
            reads=(f"qo{o % 2}h0", f"qo{o % 2}h1"), writes=(),
            tag=f"qsp{o % 2}")
    for tt in range(NT):
        builds = [lambda h, c=c, tt=tt: h.matmul(
            psg[:], b.xn[:, c * QH + tt * 128: c * QH + (tt + 1) * 128],
            wg[:, c * 96:(c + 1) * 96], start=(c == 0), stop=False)
            for c in range(32)]
        builds.append(lambda h: h.matmul(psg[:], b.onesb[:, 0:128], bgb[:],
                                         start=False, stop=True))
        s.op("te", builds, reads=("wg", "bgb", "xndst", "onesb"), writes=("psg",))
        s.op("se", lambda h, tt=tt: h.activation(
            b.gates[:, tt * 96:(tt + 1) * 96], psg[:], AF.Sigmoid),
            reads=("psg",), writes=("gates",))



def _p4b(b, ps):
    nc, s, di = b.nc, b.s, b.di
    wf = ps.enter_context(nc.sbuf_tensor("wf", [128, 8 * D], BF16))
    gob = ps.enter_context(nc.sbuf_tensor("gob", [128, 4 * QH], BF16))
    a1s = ps.enter_context(nc.sbuf_tensor("a1s", [128, 32 * 16], BF16))
    a3s = ps.enter_context(nc.sbuf_tensor("a3s", [128, 32 * 16], BF16))
    a2s = ps.enter_context(nc.sbuf_tensor("a2s", [128, NO * 16], BF16))
    b1s = ps.enter_context(nc.sbuf_tensor("b1s", [16, HID], BF16))
    b3s = ps.enter_context(nc.sbuf_tensor("b3s", [16, HID], BF16))
    t1b = ps.enter_context(nc.sbuf_tensor("t1b", [16, QH], BF16))
    t3b = ps.enter_context(nc.sbuf_tensor("t3b", [16, QH], BF16))
    sil = ps.enter_context(nc.sbuf_tensor("sil", [128, QH], F32))
    ps1 = [ps.enter_context(nc.psum_tensor(f"ph1{i}", [128, 512], F32))
           for i in range(2)]
    ps3 = [ps.enter_context(nc.psum_tensor(f"ph3{i}", [128, 512], F32))
           for i in range(2)]
    pt2 = ps.enter_context(nc.psum_tensor("pt2", [16, QH], F32))

    for nm, dst in (("A1T", a1s), ("A3T", a3s), ("A2T", a2s)):
        s.op("gp", lambda h, nm=nm, dst=dst: h.dma_start(
            out=dst[:].rearrange("p (c n) -> p c n", n=16),
            in_=di[nm].rearrange("c p n -> p c n")),
            writes=(nm,), tag="p4bw")
    for nm, dst in (("B1T", b1s), ("B3T", b3s)):
        s.op("gp", lambda h, nm=nm, dst=dst: h.dma_start(out=dst[:], in_=di[nm]),
             writes=(nm,), tag="p4bw")
    for a_s, akey, tb, tk, pp in ((a1s, "A1T", t1b, "t1b", ps1),
                                  (a3s, "A3T", t3b, "t3b", ps3)):
        builds = []
        for hh in range(2):
            for c in range(32):
                builds.append(lambda h, c=c, hh=hh, a_s=a_s, pp=pp: h.matmul(
                    pp[hh][0:16, :], a_s[:, c * 16:(c + 1) * 16],
                    b.xn[:, c * QH + hh * 512: c * QH + hh * 512 + 512],
                    start=(c == 0), stop=(c == 31)))
        s.op("te", builds, reads=(akey, "xndst"), writes=("ps10", "ps11")
             if pp is ps1 else ("ps30", "ps31"))
        for hh in range(2):
            s.op("ve", lambda h, tb=tb, hh=hh, pp=pp: h.tensor_scalar(
                tb[:, hh * 512: hh * 512 + 512], pp[hh][0:16, :], LORA_SCALE,
                None, ALU.mult),
                reads=(("ps10", "ps11") if pp is ps1 else ("ps30", "ps31"))[hh:hh + 1],
                writes=(tk + str(hh),))

    for o in range(NO):
        for wsl, nm in ((0, "w1T"), (1, "w3T")):
            wk = f"wf{wsl}{o % 4}"
            s.op("gp", lambda h, o=o, wsl=wsl, nm=nm: h.dma_start(
                out=wf[:, (4 * wsl + o % 4) * D:(4 * wsl + o % 4) * D + D]
                    .rearrange("p (c n) -> p c n", n=128),
                in_=di[nm][o].rearrange("(c p) n -> p c n", p=128)),
            writes=(wk,), tag=wk)
        w1s = wf[:, (o % 4) * D:(o % 4) * D + D].rearrange("p (c n) -> p c n", n=128)
        w3s = wf[:, (4 + o % 4) * D:(4 + o % 4) * D + D].rearrange(
            "p (c n) -> p c n", n=128)
        for half in range(2):
            hs = half * 512
            bl1 = [lambda h, c=c, half=half, w1s=w1s: h.matmul(
                ps1[half][:], w1s[:, c, :],
                b.xn[:, c * QH + half * 512: c * QH + half * 512 + 512],
                start=(c == 0), stop=False) for c in range(32)]
            bl1.append(lambda h, o=o, half=half: h.matmul(
                ps1[half][:], b1s[:, o * 128:(o + 1) * 128],
                t1b[:, half * 512: half * 512 + 512], start=False, stop=True))
            s.op("te", bl1, reads=(f"wf0{o % 4}", "xndst", "B1T", "t1b0", "t1b1"),
                 writes=(f"ps1{half}",))
            bl3 = [lambda h, c=c, half=half, w3s=w3s: h.matmul(
                ps3[half][:], w3s[:, c, :],
                b.xn[:, c * QH + half * 512: c * QH + half * 512 + 512],
                start=(c == 0), stop=False) for c in range(32)]
            bl3.append(lambda h, o=o, half=half: h.matmul(
                ps3[half][:], b3s[:, o * 128:(o + 1) * 128],
                t3b[:, half * 512: half * 512 + 512], start=False, stop=True))
            s.op("te", bl3, reads=(f"wf1{o % 4}", "xndst", "B3T", "t3b0", "t3b1"),
                 writes=(f"ps3{half}",))
            s.op("se", lambda h, half=half: h.activation(
                sil[:, half * 512: half * 512 + 512], ps1[half][:], AF.Silu),
                reads=(f"ps1{half}",), writes=(f"sil{half}",))
            gk = f"gob{o % 4}"
            s.op("ve", lambda h, o=o, half=half: h.tensor_tensor(
                gob[:, (o % 4) * QH + half * 512:(o % 4) * QH + half * 512 + 512],
                sil[:, half * 512: half * 512 + 512], ps3[half][:], ALU.mult),
                reads=(f"sil{half}", f"ps3{half}"), writes=(gk + f"h{half}",))
        s.op("sy", lambda h, o=o: h.dma_start(
            out=b.g_scr[o], in_=gob[:, (o % 4) * QH:(o % 4) * QH + QH]),
            reads=(f"gob{o % 4}h0", f"gob{o % 4}h1"), writes=(),
            tag=f"gsp{o % 2}")
        if o >= 2:
            od = o - 2
            bt2 = [lambda h, od=od, hh=hh: h.matmul(
                pt2[:, hh * 512: hh * 512 + 512], a2s[:, od * 16:(od + 1) * 16],
                gob[:, (od % 4) * QH + hh * 512:(od % 4) * QH + hh * 512 + 512],
                start=(od == 0), stop=False) for hh in range(2)]
            s.op("te", bt2, reads=("A2T", f"gob{od % 4}h0", f"gob{od % 4}h1"),
                 writes=())
    for od in (NO - 2, NO - 1):
        bt2 = [lambda h, od=od, hh=hh: h.matmul(
            pt2[:, hh * 512: hh * 512 + 512], a2s[:, od * 16:(od + 1) * 16],
            gob[:, (od % 4) * QH + hh * 512:(od % 4) * QH + hh * 512 + 512],
            start=False, stop=(od == NO - 1)) for hh in range(2)]
        s.op("te", bt2, reads=("A2T", f"gob{od % 4}h0", f"gob{od % 4}h1"),
             writes=() if od < NO - 1 else ("pt2",))
    s.op("ve", lambda h: h.tensor_scalar(b.t2b[:], pt2[:], LORA_SCALE, None,
                                         ALU.mult),
         reads=("pt2",), writes=("t2b",))



def _p1(b, ps):
    nc, s, di = b.nc, b.s, b.di
    xbuf = ps.enter_context(nc.sbuf_tensor("p1xb", [128, 2 * D], F32))
    xnt = ps.enter_context(nc.sbuf_tensor("p1xnt", [128, D], BF16))
    xnkv = ps.enter_context(nc.sbuf_tensor("xnkv", [128, 32 * 512], BF16))
    wkv = ps.enter_context(nc.sbuf_tensor("wkv", [128, 2 * D], BF16))
    pst = [ps.enter_context(nc.psum_tensor(f"p1pst{i}", [128, 512], BF16))
           for i in range(2)]
    psp = [ps.enter_context(nc.psum_tensor(f"p1psp{i}", [128, 512], F32))
           for i in range(2)]
    groups = [(di["xf"], gg * 512, 512, "glob") for gg in range(4)]
    groups += [(di["xw"], gg * 512, 512, "win") for gg in range(2)]
    groups += [(di["xw"], 1024, 128, "win")]
    for gi, (src, tok0, ntok, kind) in enumerate(groups):
        nt = ntok // 128
        sink = _xn_transpose_sink(b, xnt, pst, xnkv, 512, "xnkvdst")
        _ln_tiles(b, src[tok0:tok0 + ntok], nt, sink, xbuf, xnt)
        for o in range(16):
            wk = f"wkv{o % 2}"
            s.op("gp", lambda h, o=o: h.dma_start(
                out=wkv[:, (o % 2) * D:(o % 2) * D + D].rearrange(
                    "p (c n) -> p c n", n=128),
                in_=di["wqkvT"][32 + o].rearrange("(c p) n -> p c n", p=128)),
                writes=(wk,), tag=wk)
            wks = wkv[:, (o % 2) * D:(o % 2) * D + D].rearrange(
                "p (c n) -> p c n", n=128)
            builds = [lambda h, c=c, wks=wks, ntok=ntok, o=o: h.matmul(
                psp[o % 2][:, 0:ntok],
                wks[:, c, :], xnkv[:, c * 512: c * 512 + ntok],
                start=(c == 0), stop=(c == 31)) for c in range(32)]
            s.op("te", builds, reads=(wk, "xnkvdst"), writes=(f"pkv{o % 2}",))
            kvh = o % 8
            if kind == "glob":
                dst = (b.kf if o < 8 else b.vf)[:, kvh * S + tok0: kvh * S + tok0 + ntok]
                dkey = "kf" if o < 8 else "vf"
            else:
                dst = (b.kwin if o < 8 else b.vwin)[
                    :, kvh * NWTOK + tok0: kvh * NWTOK + tok0 + ntok]
                dkey = "kwin" if o < 8 else "vwin"
            s.op("se", lambda h, dst=dst, o=o, ntok=ntok: h.activation(
                dst, psp[o % 2][:, 0:ntok], AF.Copy),
                reads=(f"pkv{o % 2}",), writes=(dkey,))



def _rope_inplace(b, buf, bufkey, ncol_per_head, cos_sb, coskey, sin_sb,
                  psr, f32a, f32b):
    s = b.s
    ntile = (ncol_per_head + 511) // 512
    for k in range(KVH):
        for u in range(ntile):
            w = min(512, ncol_per_head - u * 512)
            c0 = k * ncol_per_head + u * 512
            t0 = u * 512
            sl = buf[:, c0:c0 + w]
            pk = f"ropeps{u % 2}"
            s.op("te", lambda h, sl=sl, u=u, w=w: h.matmul(
                psr[u % 2][:, 0:w], b.psw[:], sl, start=True, stop=True),
                reads=(bufkey, "psw"), writes=(pk,))
            s.op("se", lambda h, sl=sl, w=w: h.activation(
                f32a[:, 0:w], sl, AF.Copy),
                reads=(bufkey,), writes=("ropea",))
            s.op("ve", lambda h, w=w, t0=t0: h.tensor_tensor(
                f32a[:, 0:w], f32a[:, 0:w], cos_sb[:, t0:t0 + w], ALU.mult),
                reads=("ropea", coskey), writes=("ropea",))
            s.op("ve", lambda h, w=w, u=u, t0=t0: h.tensor_tensor(
                f32b[:, 0:w], psr[u % 2][:, 0:w], sin_sb[:, t0:t0 + w],
                ALU.mult),
                reads=(pk, coskey), writes=("ropeb",))
            s.op("ve", lambda h, sl=sl, w=w: h.tensor_tensor(
                sl, f32a[:, 0:w], f32b[:, 0:w], ALU.add),
                reads=("ropea", "ropeb"), writes=(bufkey,))


def _p2(b, ps):
    nc, s, di = b.nc, b.s, b.di
    cosk = ps.enter_context(nc.sbuf_tensor("coskb", [128, S], F32))
    sinkb = ps.enter_context(nc.sbuf_tensor("sinkb", [128, S], F32))
    cosw = ps.enter_context(nc.sbuf_tensor("coswb", [128, NWTOK], F32))
    sinw = ps.enter_context(nc.sbuf_tensor("sinwb", [128, NWTOK], F32))
    posb = ps.enter_context(nc.sbuf_tensor("posb", [128, 2 * 128], F32))
    b1b = ps.enter_context(nc.sbuf_tensor("b1b", [128, 2 * 16], F32))
    b2b = ps.enter_context(nc.sbuf_tensor("b2b", [128, 2], F32))
    w1b = ps.enter_context(nc.sbuf_tensor("w1b", [128, 2 * 2048], BF16))
    w2b = ps.enter_context(nc.sbuf_tensor("w2b", [128, 2 * 2048], BF16))
    cki = ps.enter_context(nc.sbuf_tensor("cki", [128, 4 * 16 * 128], BF16))
    hid = ps.enter_context(nc.sbuf_tensor("hid", [128, 4 * 16 * 128], BF16))
    f32a = ps.enter_context(nc.sbuf_tensor("rpa", [128, 512], F32))
    f32b = ps.enter_context(nc.sbuf_tensor("rpb", [128, 512], F32))
    cvfb = ps.enter_context(nc.sbuf_tensor("cvfb", [128, 128], BF16))
    psc = [ps.enter_context(nc.psum_tensor(f"p2psc{i}", [128, 512], F32))
           for i in range(2)]
    psc2 = ps.enter_context(nc.psum_tensor("p2psc2", [128, 128], F32))
    psr = [ps.enter_context(nc.psum_tensor(f"p2psr{i}", [128, 512], F32))
           for i in range(2)]
    pst = [ps.enter_context(nc.psum_tensor(f"p2pst{i}", [128, 512], BF16))
           for i in range(2)]

    for nm, dst in (("cosk", cosk), ("sink", sinkb), ("cosw", cosw),
                    ("sinw", sinw)):
        s.op("sy", lambda h, nm=nm, dst=dst: h.dma_start(out=dst[:], in_=di[nm]),
             writes=(nm,), tag="p2tab")
    for bld, wkey in (
        (lambda h: h.dma_start(out=posb[:, 0:128], in_=di["kposT"]), "posb"),
        (lambda h: h.dma_start(out=posb[:, 128:256], in_=di["vposT"]), "posb"),
        (lambda h: h.dma_start(out=b1b[:, 0:16], in_=di["kb1"]), "b1b"),
        (lambda h: h.dma_start(out=b1b[:, 16:32], in_=di["vb1"]), "b1b"),
        (lambda h: h.dma_start(out=b2b[:, 0:1], in_=di["kb2"]), "b2b"),
        (lambda h: h.dma_start(out=b2b[:, 1:2], in_=di["vb2"]), "b2b"),
    ):
        s.op("sy", bld, writes=(wkey,), tag="p2tab")
    for side, nm in ((0, "kw2T"), (1, "vw2T")):
        s.op("gp", lambda h, side=side, nm=nm: h.dma_start(
            out=w2b[:, side * 2048: side * 2048 + 2048].rearrange(
                "p (o n) -> p o n", n=128),
            in_=di[nm].rearrange("o p n -> p o n")),
            writes=("w2b",), tag="p2w2")
    for k in range(KVH):
        s.op("gp", lambda h, k=k: h.dma_start(
            out=b.ckf[:, k * NCK + 128: k * NCK + 129],
            in_=di["memk"][k]),
            writes=("ckf",), tag="p2w2")
        s.op("gp", lambda h, k=k: h.dma_start(
            out=b.cvt2[:, k * 128:(k + 1) * 128], in_=di["memv"][k:k + 1, :]),
            writes=("cvt2",), tag="p2w2")

    for side in range(2):
        src_ = b.kf if side == 0 else b.vf
        skey = "kf" if side == 0 else "vf"
        w1nm = "kw1T" if side == 0 else "vw1T"
        for kh in range(2):
            for ki in range(4):
                k = 4 * kh + ki
                for c in range(16):
                    s.op("ve", lambda h, k=k, c=c, ki=ki, side=side,
                         src_=src_: h.tensor_scalar(
                        cki[:, ki * 2048 + c * 128: ki * 2048 + (c + 1) * 128],
                        src_[:, k * S + c: k * S + S: 16],
                        posb[:, side * 128 + k * 16 + c:
                             side * 128 + k * 16 + c + 1],
                        None, ALU.add),
                        reads=(skey, "posb"), writes=(f"cki{ki}",))
            if side == 0 and kh == 0 and getattr(b, "dbg_cki", None) is not None:
                s.barrier(engs=("sy",))
                s.op("sy", lambda h: h.dma_start(out=b.dbg_cki, in_=cki[:]),
                     reads=("cki0", "cki1", "cki2", "cki3"), writes=(),
                     tag="dump")
            for o in range(16):
                wk = f"w1b{o % 2}"
                s.op("gp", lambda h, o=o, w1nm=w1nm: h.dma_start(
                    out=w1b[:, (o % 2) * 2048:(o % 2) * 2048 + 2048].rearrange(
                        "p (c n) -> p c n", n=128),
                    in_=di[w1nm][o].rearrange("c p n -> p c n")),
                    writes=(wk,), tag=wk)
                for ki in range(4):
                    builds = [lambda h, c=c, o=o, ki=ki: h.matmul(
                        psc[ki % 2][:, 0:128],
                        w1b[:, (o % 2) * 2048 + c * 128:
                            (o % 2) * 2048 + (c + 1) * 128],
                        cki[:, ki * 2048 + c * 128: ki * 2048 + (c + 1) * 128],
                        start=(c == 0), stop=(c == 15)) for c in range(16)]
                    s.op("te", builds, reads=(wk, f"cki{ki}"),
                         writes=(f"p2c{ki % 2}",))
                    s.op("se", lambda h, o=o, ki=ki, side=side: h.activation(
                        hid[:, ki * 2048 + o * 128: ki * 2048 + (o + 1) * 128],
                        psc[ki % 2][:, 0:128], AF.Relu,
                        bias=b1b[:, side * 16 + o: side * 16 + o + 1]),
                        reads=(f"p2c{ki % 2}", "b1b"), writes=(f"hid{ki}",))
            if side == 0 and kh == 0 and getattr(b, "dbg_hid", None) is not None:
                s.barrier(engs=("sy",))
                s.op("sy", lambda h: h.dma_start(out=b.dbg_hid, in_=hid[:]),
                     reads=("hid0", "hid1", "hid2", "hid3"), writes=(),
                     tag="dump")
            for ki in range(4):
                k = 4 * kh + ki
                builds = [lambda h, o=o, ki=ki, side=side: h.matmul(
                    psc2[:, 0:128],
                    w2b[:, side * 2048 + o * 128: side * 2048 + (o + 1) * 128],
                    hid[:, ki * 2048 + o * 128: ki * 2048 + (o + 1) * 128],
                    start=(o == 0), stop=(o == 15)) for o in range(16)]
                s.op("te", builds, reads=("w2b", f"hid{ki}"), writes=("p2out",))
                if side == 0:
                    s.op("ve", lambda h, k=k: h.tensor_scalar(
                        b.ckf[:, k * NCK: k * NCK + 128], psc2[:, 0:128],
                        b2b[:, 0:1], None, ALU.add),
                        reads=("p2out", "b2b"), writes=("ckf",), sreads=("b2b",))
                else:
                    s.op("ve", lambda h: h.tensor_scalar(
                        cvfb[:], psc2[:, 0:128],
                        b2b[:, 1:2], None, ALU.add),
                        reads=("p2out", "b2b"), writes=("cvfb",), sreads=("b2b",))
                    s.op("te", lambda h: h.transpose(
                        pst[0][:, 0:128], cvfb[:], b.ident[:]),
                        reads=("cvfb", "ident"), writes=("p2t0",))
                    s.op("ve", lambda h, k=k: h.tensor_copy(
                        b.cvt1[:, k * 128:(k + 1) * 128], pst[0][:, 0:128]),
                        reads=("p2t0",), writes=("cvt1",))

    _rope_inplace(b, b.kf, "kf", S, cosk, "cosk", sinkb, psr, f32a, f32b)
    _rope_inplace(b, b.kwin, "kwin", NWTOK, cosw, "cosw", sinw, psr, f32a, f32b)

    for buf, bufkey, ncol, ntile in ((b.vf, "vf", S, 16),
                                     (b.vwin, "vwin", NWTOK, 9)):
        for k in range(KVH):
            for u in range(ntile):
                c0 = k * ncol + u * 128
                pk = f"p2t{u % 2}"
                s.op("te", lambda h, c0=c0, u=u, buf=buf: h.transpose(
                    pst[u % 2][:, 0:128], buf[:, c0:c0 + 128], b.ident[:]),
                    reads=(bufkey, "ident"), writes=(pk,))
                s.op("ve", lambda h, c0=c0, u=u, buf=buf: h.tensor_copy(
                    buf[:, c0:c0 + 128], pst[u % 2][:, 0:128]),
                    reads=(pk,), writes=(bufkey,))


def _p3(b, ps):
    nc, s, di = b.nc, b.s, b.di
    st = b.stat
    sb = nc.sbuf_tensor
    qb = ps.enter_context(sb("qb", [128, G * QH], BF16))
    rq = ps.enter_context(sb("rq", [128, G * QH], BF16))
    pf = ps.enter_context(sb("pf", [128, G * 2 * 2048], BF16))
    pc = ps.enter_context(sb("pcb", [128, G * NCK], BF16))
    psl = ps.enter_context(sb("pslb", [128, G * 192], BF16))
    smb = ps.enter_context(sb("smb", [128, G * NCK], F32))
    ssb = ps.enter_context(sb("ssb", [128, 192], F32))
    imps = ps.enter_context(sb("imps", [128, 128], F32))
    ya = ps.enter_context(sb("ya", [128, 128], F32))
    yb = ps.enter_context(sb("yb", [128, 128], F32))
    geb = ps.enter_context(sb("geb", [128, 128], F32))
    fm = ps.enter_context(sb("fm", [128, 128], BF16))
    fmq = ps.enter_context(sb("fmq", [32, 512], BF16))
    zdump = ps.enter_context(sb("zdump", [128, 128], BF16))
    ptA = ps.enter_context(sb("ptA", [128, 2 * 512], BF16))
    ptM = ps.enter_context(sb("ptM", [1, 512], BF16))
    ptB = ps.enter_context(sb("ptB", [64, 512], BF16))
    asp = ps.enter_context(sb("asp", [128, 2 * 512], BF16))
    cosq = ps.enter_context(sb("cosqb", [128, QH], F32))
    sinq = ps.enter_context(sb("sinqb", [128, QH], F32))
    cmb = ps.enter_context(sb("cmb", [128, NT * NCK], F32))
    smvb = ps.enter_context(sb("smvb", [128, NT * 192], F32))
    t2t = ps.enter_context(sb("t2tb", [128, 3200], BF16))
    rpa = ps.enter_context(sb("rpa3", [128, 512], F32))
    rpb = ps.enter_context(sb("rpb3", [128, 512], F32))
    psC = ps.enter_context(nc.psum_tensor("p3psC", [128, 2048], F32))
    psO = ps.enter_context(nc.psum_tensor("p3psO", [128, 512], F32))
    psT = [ps.enter_context(nc.psum_tensor(f"p3psT{i}", [128, 1024], BF16))
           for i in range(2)]
    psR = ps.enter_context(nc.psum_tensor("p3psR", [128, 512], F32))

    for bld, wkey in (
        (lambda h: h.dma_start(out=cosq[:], in_=di["cosq"]), "cosq"),
        (lambda h: h.dma_start(out=sinq[:], in_=di["sinq"]), "cosq"),
        (lambda h: h.dma_start(out=cmb[:].rearrange("p (t n) -> p t n", n=NCK),
                               in_=di["CM"].rearrange("t p n -> p t n")), "cm"),
        (lambda h: h.dma_start(out=smvb[:].rearrange("p (t n) -> p t n", n=192),
                               in_=di["SM"].rearrange("t p n -> p t n")), "smv"),
        (lambda h: h.dma_start(out=t2t[:], in_=di["T2"]), "t2t"),
    ):
        s.op("sy", bld, writes=(wkey,), tag="p3tab")

    PSCK = ("psCg0", "psCg1", "psCg2", "psCg3")
    for kvh in range(KVH):
        s.op("sy", lambda h, kvh=kvh: h.dma_start(out=qb[:], in_=b.q_scr[kvh]),
             reads=(), writes=("qld",), tag="qld")
        for g in range(G):
            for uu in range(2):
                c0 = g * QH + uu * 512
                t0 = uu * 512
                pk = "ropeps0" if uu == 0 else "ropeps1"
                pp = psR[:, 0:512]
                s.op("te", lambda h, c0=c0: h.matmul(
                    psR[:, 0:512], b.psw[:], qb[:, c0:c0 + 512],
                    start=True, stop=True),
                    reads=("qld", "psw"), writes=("psR",))
                s.op("se", lambda h, c0=c0: h.activation(
                    rpa[:], qb[:, c0:c0 + 512], AF.Copy),
                    reads=("qld",), writes=("ropea",))
                s.op("ve", lambda h, t0=t0: h.tensor_tensor(
                    rpa[:], rpa[:], cosq[:, t0:t0 + 512], ALU.mult),
                    reads=("ropea", "cosq"), writes=("ropea",))
                s.op("ve", lambda h, t0=t0: h.tensor_tensor(
                    rpb[:], psR[:, 0:512], sinq[:, t0:t0 + 512], ALU.mult),
                    reads=("psR", "cosq"), writes=("ropeb",))
                s.op("ve", lambda h, c0=c0: h.tensor_tensor(
                    rq[:, c0:c0 + 512], rpa[:], rpb[:], ALU.add),
                    reads=("ropea", "ropeb"), writes=("rq",))

        PSCKALL = ("psCg0", "psCg1", "psCg2", "psCg3")

        def pfbase(g, qt):
            return (g * 2 + qt % 2) * 2048

        def emit_comp(qt):
            for g in range(G):
                s.op("te", lambda h, kvh=kvh, g=g, qt=qt: h.matmul(
                    psC[:, g * 512: g * 512 + NCK],
                    qb[:, g * QH + qt * 128: g * QH + (qt + 1) * 128],
                    b.ckf[:, kvh * NCK: (kvh + 1) * NCK],
                    start=True, stop=True),
                    reads=("qld", "ckf"), writes=(f"psCg{g}",))
                s.op("ve", lambda h, g=g, qt=qt: h.tensor_tensor(
                    smb[:, g * NCK:(g + 1) * NCK],
                    psC[:, g * 512: g * 512 + NCK],
                    cmb[:].rearrange("p (t n) -> p t n", n=NCK)[:, qt, :],
                    ALU.add),
                    reads=(f"psCg{g}", "cm"), writes=(f"smb{g}",))
                s.op("ve", lambda h, g=g: h.reduce_max(
                    st[:, 16 + g:17 + g], smb[:, g * NCK:(g + 1) * NCK],
                    axis=AX, negate=True),
                    reads=(f"smb{g}",), writes=(f"nm{g}",), hazard=True)
                s.op("se", lambda h, g=g: h.activation(
                    pc[:, g * NCK:(g + 1) * NCK], smb[:, g * NCK:(g + 1) * NCK],
                    AF.Exp, bias=st[:, 16 + g:17 + g],
                    accum_out=st[:, 20 + g:21 + g]),
                    reads=(f"smb{g}", f"nm{g}"), writes=(f"pc{g}", f"csum{g}"))
                s.op("ve", lambda h, g=g: h.reciprocal(
                    st[:, 24 + g:25 + g], st[:, 20 + g:21 + g]),
                    reads=(f"csum{g}",), writes=(f"rcp{g}",), hazard=True)
                gcol = qt * 96 + (4 * kvh + g) * 3
                s.op("ve", lambda h, g=g, gcol=gcol: h.tensor_tensor(
                    st[:, 28 + g:29 + g], st[:, 24 + g:25 + g],
                    b.gates[:, gcol:gcol + 1], ALU.mult),
                    reads=(f"rcp{g}", "gates"), writes=(f"rs{g}",))
                s.op("se", lambda h, g=g: h.activation(
                    pc[:, g * NCK:(g + 1) * NCK], pc[:, g * NCK:(g + 1) * NCK],
                    AF.Copy, bias=0.0, scale=st[:, 28 + g:29 + g]),
                    reads=(f"pc{g}", f"rs{g}"), writes=(f"pc{g}",))

        def emit_slide(qt):
            for g in range(G):
                s.op("te", lambda h, kvh=kvh, g=g, qt=qt: h.matmul(
                    psC[:, g * 512: g * 512 + 192],
                    rq[:, g * QH + qt * 128: g * QH + (qt + 1) * 128],
                    b.kwin[:, kvh * NWTOK + qt * 128:
                           kvh * NWTOK + qt * 128 + 192],
                    start=True, stop=True),
                    reads=("rq", "kwin"), writes=(f"psCg{g}",))
                s.op("ve", lambda h, g=g, qt=qt: h.tensor_tensor(
                    ssb[:], psC[:, g * 512: g * 512 + 192],
                    smvb[:].rearrange("p (t n) -> p t n", n=192)[:, qt, :],
                    ALU.add),
                    reads=(f"psCg{g}", "smv"), writes=("ssb",))
                s.op("ve", lambda h: h.reduce_max(
                    st[:, 48:49], ssb[:], axis=AX, negate=True),
                    reads=("ssb",), writes=("snm",), hazard=True)
                s.op("se", lambda h, g=g: h.activation(
                    psl[:, g * 192:(g + 1) * 192], ssb[:], AF.Exp,
                    bias=st[:, 48:49], accum_out=st[:, 49:50]),
                    reads=("ssb", "snm"), writes=(f"psl{g}", "ssum"))
                s.op("ve", lambda h: h.reciprocal(st[:, 50:51], st[:, 49:50]),
                     reads=("ssum",), writes=("srcp",), hazard=True)
                gcol = qt * 96 + (4 * kvh + g) * 3 + 2
                s.op("ve", lambda h, gcol=gcol: h.tensor_tensor(
                    st[:, 51:52], st[:, 50:51], b.gates[:, gcol:gcol + 1],
                    ALU.mult),
                    reads=("srcp", "gates"), writes=("srs",))
                s.op("se", lambda h, g=g: h.activation(
                    psl[:, g * 192:(g + 1) * 192],
                    psl[:, g * 192:(g + 1) * 192],
                    AF.Copy, bias=0.0, scale=st[:, 51:52]),
                    reads=(f"psl{g}", "srs"), writes=(f"psl{g}",))

        def emit_qkpre(qt):
            Lt = 1152 + 128 * qt
            nch = (Lt + 511) // 512
            for c in range(nch):
                N = min(512, Lt - 512 * c)
                s.op("te", [lambda h, qt=qt, c=c, N=N, kvh=kvh: h.matmul(
                    psC[:, c * 512: c * 512 + N],
                    rq[:, qt * 128: (qt + 1) * 128],
                    b.kf[:, kvh * S + c * 512: kvh * S + c * 512 + N],
                    start=True, stop=False)],
                    reads=("rq", "kf"), writes=(f"psCg{c}",))

        def emit_sel(qt):
                s.op("gp", lambda h: h.tensor_tensor(
                    ya[:], smb[:, 0:128], smb[:, NCK:NCK + 128], ALU.add),
                    reads=("smb0", "smb1"), writes=("ya",))
                s.op("gp", lambda h: h.tensor_tensor(
                    yb[:], smb[:, 2 * NCK:2 * NCK + 128],
                    smb[:, 3 * NCK:3 * NCK + 128], ALU.add),
                    reads=("smb2", "smb3"), writes=("yb",))
                s.op("gp", lambda h: h.tensor_tensor(imps[:], ya[:], yb[:], ALU.add),
                     reads=("ya", "yb"), writes=("imps",))
                s.op("ve", lambda h: h.reduce_max(st[:, 32:33], imps[:], axis=AX),
                     reads=("imps",), writes=("mx",), hazard=True)
                s.op("gp", lambda h: h.tensor_scalar(
                    st[:, 33:34], st[:, 32:33], 0.25, -1000.0, ALU.mult, ALU.max),
                    reads=("mx",), sreads=("mx",), writes=("M",))
                s.op("gp", lambda h: h.tensor_scalar(
                    st[:, 34:35], st[:, 33:34], -1.0, None, ALU.mult),
                    reads=("M",), sreads=("M",), writes=("nM",))
                s.op("se", lambda h: h.activation(
                    zdump[:], imps[:], AF.Exp, bias=st[:, 34:35], scale=0.25,
                    accum_out=st[:, 35:36]),
                    reads=("imps", "nM"), writes=("zs", "zdumpb"))
                s.op("se", lambda h: h.activation(
                    st[:, 36:37], st[:, 33:34], AF.Exp, bias=st[:, 60:61],
                    scale=-1.0),
                    reads=("M", "c1000"), writes=("e2",))
                s.op("gp", lambda h: h.tensor_tensor(
                    st[:, 37:38], st[:, 35:36], st[:, 36:37], ALU.add),
                    reads=("zs", "e2"), writes=("z",))
                s.op("se", lambda h: h.activation(st[:, 38:39], st[:, 37:38], AF.Ln),
                     reads=("z",), writes=("lnz",))
                s.op("gp", lambda h: h.tensor_tensor(
                    st[:, 39:40], st[:, 33:34], st[:, 38:39], ALU.add),
                    reads=("M", "lnz"), writes=("mlz",))
                s.op("gp", lambda h: h.tensor_scalar(
                    st[:, 40:41], st[:, 39:40], LN1E10, 4.0, ALU.add, ALU.mult),
                    reads=("mlz",), sreads=("mlz",), writes=("t2c",))
                cur, nxt = imps, ya
                curk, nxtk = "imps", "ya"
                for it in range(3):
                    s.op("ve", lambda h, cur=cur: h.reduce_max(
                        st[:, 41:42], cur[:], axis=AX),
                        reads=(curk,), writes=("mi",), hazard=True)
                    s.op("gp", lambda h, cur=cur: h.tensor_scalar(
                        geb[:], cur[:], st[:, 41:42], None, ALU.is_ge),
                        reads=(curk,), sreads=("mi",), writes=("geb",))
                    s.op("ve", lambda h, cur=cur, nxt=nxt: h.scalar_tensor_tensor(
                        nxt[:], geb[:], -1e33, cur[:], ALU.mult, ALU.add),
                        reads=("geb", curk), writes=(nxtk,))
                    cur, nxt = nxt, (yb if nxt is ya else ya)
                    curk, nxtk = nxtk, ("yb" if nxtk == "ya" else "ya")
                s.op("ve", lambda h, cur=cur: h.reduce_max(
                    st[:, 42:43], cur[:], axis=AX),
                    reads=(curk,), writes=("q4",), hazard=True)
                s.op("gp", lambda h: h.tensor_scalar(
                    geb[:], imps[:], st[:, 42:43], None, ALU.is_ge),
                    reads=("imps",), sreads=("q4",), writes=("geb",))
                s.op("gp", lambda h: h.tensor_scalar(
                    ya[:], imps[:], st[:, 40:41], None, ALU.is_gt),
                    reads=("imps",), sreads=("t2c",), writes=("ya",))
                s.op("gp", lambda h: h.tensor_tensor(yb[:], geb[:], ya[:], ALU.mult),
                     reads=("geb", "ya"), writes=("yb",))
                s.op("gp", lambda h: h.tensor_scalar(
                    fm[:], yb[:], 1.0, V30, ALU.subtract, ALU.mult),
                    reads=("yb",), writes=("fm",))
                builds = [lambda h, c=c: h.transpose(
                    psT[0][0:32, c * 128:(c + 1) * 128], fm[:, c * 32:(c + 1) * 32],
                    b.ident[:]) for c in range(4)]
                s.op("te", builds, reads=("fm", "ident"), writes=("psT0",))
                s.op("ve", lambda h: h.tensor_copy(fmq[:], psT[0][0:32, 0:512]),
                     reads=("psT0",), writes=("fmq",))

        def emit_fine_g(qt, g):
            Lt = 1152 + 128 * qt
            nch = (Lt + 511) // 512
            STT = 1024 - 128 * qt
            pb = pfbase(g, qt)
            for c in range(nch):
                N = min(512, Lt - 512 * c)
                builds = []
                if g > 0:
                    builds.append(
                        lambda h, g=g, qt=qt, c=c, N=N, kvh=kvh: h.matmul(
                            psC[:, c * 512: c * 512 + N],
                            rq[:, g * QH + qt * 128: g * QH + (qt + 1) * 128],
                            b.kf[:, kvh * S + c * 512: kvh * S + c * 512 + N],
                            start=True, stop=False))
                builds += [
                    lambda h, g=g, c=c, N=N: h.matmul(
                        psC[:, c * 512: c * 512 + N],
                        fmq[:, c * 128:(c + 1) * 128], b.el[:, 0:N],
                        start=False, stop=False),
                    lambda h, c=c, N=N, STT=STT: h.matmul(
                        psC[:, c * 512: c * 512 + N], b.dl[:],
                        t2t[:, STT + 512 * c: STT + 512 * c + N],
                        start=False, stop=True),
                ]
                s.op("te", builds, reads=("rq", "kf", "fmq", "el", "dl", "t2t"),
                     writes=(f"psCg{c}",))
                s.op("ve", lambda h, c=c, N=N: h.reduce_max(
                    st[:, 52 + c:53 + c], psC[:, c * 512: c * 512 + N], axis=AX),
                    reads=(f"psCg{c}",), writes=(f"pm{c}",), hazard=True)
            s.op("ve", lambda h, nch=nch: h.reduce_max(
                st[:, 44:45], st[:, 52:52 + nch], axis=AX, negate=True),
                reads=tuple(f"pm{c}" for c in range(nch)), writes=("nmf",),
                hazard=True)
            for c in range(nch):
                N = min(512, Lt - 512 * c)
                s.op("se", lambda h, c=c, N=N, pb=pb: h.activation(
                    pf[:, pb + c * 512: pb + c * 512 + N],
                    psC[:, c * 512: c * 512 + N], AF.Exp,
                    bias=st[:, 44:45], accum_out=st[:, 56 + c:57 + c]),
                    reads=(f"psCg{c}", "nmf"),
                    writes=(f"pf{g}{qt % 2}", f"es{c}"))
            s.op("ve", lambda h, nch=nch: h.reduce_sum(
                st[:, 45:46], st[:, 56:56 + nch], axis=AX),
                reads=tuple(f"es{c}" for c in range(nch)), writes=("fsum",),
                hazard=True)
            s.op("ve", lambda h: h.reciprocal(st[:, 46:47], st[:, 45:46]),
                 reads=("fsum",), writes=("frcp",), hazard=True)
            gcol = qt * 96 + (4 * kvh + g) * 3 + 1
            s.op("ve", lambda h, gcol=gcol: h.tensor_tensor(
                st[:, 47:48], st[:, 46:47], b.gates[:, gcol:gcol + 1],
                ALU.mult),
                reads=("frcp", "gates"), writes=("frs",))
            s.op("se", lambda h, g=g, Lt=Lt, pb=pb: h.activation(
                pf[:, pb: pb + Lt], pf[:, pb: pb + Lt],
                AF.Copy, bias=0.0, scale=st[:, 47:48]),
                reads=(f"pf{g}{qt % 2}", "frs"), writes=(f"pf{g}{qt % 2}",))

        def emit_fineav_chunks(qt, us):
            par = qt % 2
            pfk = tuple(f"pf{g}{par}" for g in range(G))
            for u in us:
                builds = [lambda h, g=g, u=u, qt=qt: h.transpose(
                    psT[u % 2][:, g * 128:(g + 1) * 128],
                    pf[:, pfbase(g, qt) + u * 128: pfbase(g, qt) + (u + 1) * 128],
                    b.ident[:]) for g in range(G)]
                s.op("te", builds, reads=pfk + ("ident",),
                     writes=(f"psT{u % 2}",))
                if u % 2 == 0:
                    s.op("ve", lambda h, u=u: h.tensor_copy(
                        ptA[:, (u % 2) * 512:(u % 2) * 512 + 512],
                        psT[u % 2][:, 0:512]),
                        reads=(f"psT{u % 2}",), writes=(f"ptA{u % 2}",))
                else:
                    s.op("se", lambda h, u=u: h.activation(
                        ptA[:, (u % 2) * 512:(u % 2) * 512 + 512],
                        psT[u % 2][:, 0:512], AF.Copy),
                        reads=(f"psT{u % 2}",), writes=(f"ptA{u % 2}",))
                s.op("te", lambda h, kvh=kvh, u=u: h.matmul(
                    psO[:], b.vf[:, kvh * S + u * 128: kvh * S + (u + 1) * 128],
                    ptA[:, (u % 2) * 512:(u % 2) * 512 + 512],
                    start=False, stop=False),
                    reads=(f"ptA{u % 2}", "vf"), writes=("psO",))

        def emit_slide_av(qt):
            builds = [lambda h, g=g: h.transpose(
                psT[0][:, g * 128:(g + 1) * 128],
                psl[:, g * 192: g * 192 + 128], b.ident[:]) for g in range(G)]
            s.op("te", builds, reads=("psl0", "psl1", "psl2", "psl3", "ident"),
                 writes=("psT0",))
            builds = [lambda h, g=g: h.transpose(
                psT[1][0:64, g * 128:(g + 1) * 128],
                psl[:, g * 192 + 128: g * 192 + 192], b.ident[:])
                for g in range(G)]
            s.op("te", builds, reads=("psl0", "psl1", "psl2", "psl3", "ident"),
                 writes=("psT1",))
            s.op("ve", lambda h: h.tensor_copy(ptA[:, 0:512], psT[0][:, 0:512]),
                 reads=("psT0",), writes=("ptA0",))
            s.op("se", lambda h: h.activation(ptB[:], psT[1][0:64, 0:512],
                                              AF.Copy),
                 reads=("psT1",), writes=("ptB",))
            s.op("te", [lambda h, kvh=kvh, qt=qt: h.matmul(
                psO[:],
                b.vwin[:, kvh * NWTOK + qt * 128: kvh * NWTOK + (qt + 1) * 128],
                ptA[:, 0:512], start=False, stop=False),
                lambda h, kvh=kvh, qt=qt: h.matmul(
                psO[:],
                b.vwin[0:64, kvh * NWTOK + (qt + 1) * 128:
                       kvh * NWTOK + (qt + 2) * 128],
                ptB[:], start=False, stop=True)],
                reads=("ptA0", "ptB", "vwin"), writes=("psO",))
            s.op("ve", lambda h, qt=qt: h.tensor_copy(
                asp[:, (qt % 2) * 512:(qt % 2) * 512 + 512], psO[:]),
                reads=("psO",), writes=(f"asp{qt % 2}",))
            s.op("sy", lambda h, kvh=kvh, qt=qt: h.dma_start(
                out=b.a_scr[kvh].rearrange("p (g t) -> p g t", t=QH)
                [:, :, qt * 128:(qt + 1) * 128],
                in_=asp[:, (qt % 2) * 512:(qt % 2) * 512 + 512].rearrange(
                    "p (g t) -> p g t", t=128)),
                reads=(f"asp{qt % 2}",), writes=(), tag=f"asp{qt % 2}")

        def emit_compav(qt):
            builds = [lambda h, g=g: h.transpose(
                psT[0][:, g * 128:(g + 1) * 128], pc[:, g * NCK: g * NCK + 128],
                b.ident[:]) for g in range(G)]
            s.op("te", builds, reads=("pc0", "pc1", "pc2", "pc3", "ident"),
                 writes=("psT0",))
            builds = [lambda h, g=g: h.transpose(
                psT[1][0:1, g * 128:(g + 1) * 128],
                pc[:, g * NCK + 128: g * NCK + 129], b.ident[:])
                for g in range(G)]
            s.op("te", builds, reads=("pc0", "pc1", "pc2", "pc3", "ident"),
                 writes=("psT1",))
            s.op("ve", lambda h: h.tensor_copy(ptA[:, 0:512], psT[0][:, 0:512]),
                 reads=("psT0",), writes=("ptA0",))
            s.op("se", lambda h: h.activation(ptM[:], psT[1][0:1, 0:512],
                                              AF.Copy),
                 reads=("psT1",), writes=("ptM",))
            s.op("te", [lambda h, kvh=kvh: h.matmul(
                psO[:], b.cvt1[:, kvh * 128:(kvh + 1) * 128], ptA[:, 0:512],
                start=True, stop=False),
                lambda h, kvh=kvh: h.matmul(
                psO[:], b.cvt2[:, kvh * 128:(kvh + 1) * 128], ptM[:],
                start=False, stop=False)],
                reads=("ptA0", "ptM", "cvt1", "cvt2"), writes=("psO",))

        for qt in range(NT):
            Ltp = 1152 + 128 * (qt - 1)
            nup = Ltp // 128 if qt > 0 else 0
            emit_comp(qt)
            if qt > 0:
                emit_slide(qt - 1)
            emit_qkpre(qt)
            emit_sel(qt)
            cuts = [int(nup * f) for f in (0.4, 0.55, 0.7, 0.85)] + [nup]
            if qt > 0:
                emit_fineav_chunks(qt - 1, range(0, cuts[0]))
            emit_fine_g(qt, 0)
            if qt > 0:
                emit_fineav_chunks(qt - 1, range(cuts[0], cuts[1]))
            emit_fine_g(qt, 1)
            if qt > 0:
                emit_fineav_chunks(qt - 1, range(cuts[1], cuts[2]))
            emit_fine_g(qt, 2)
            if qt > 0:
                emit_fineav_chunks(qt - 1, range(cuts[2], cuts[3]))
            emit_fine_g(qt, 3)
            if qt > 0:
                emit_fineav_chunks(qt - 1, range(cuts[3], nup))
                emit_slide_av(qt - 1)
            emit_compav(qt)
        emit_slide(NT - 1)
        emit_fineav_chunks(NT - 1, range(0, (1152 + 128 * (NT - 1)) // 128))
        emit_slide_av(NT - 1)



def _p5(b, ps):
    nc, s, di = b.nc, b.s, b.di
    sb = nc.sbuf_tensor
    attn = ps.enter_context(sb("attnsb", [128, H * QH], BF16))
    wo = ps.enter_context(sb("wob", [128, 2 * 16 * 512], BF16))
    w2s = ps.enter_context(sb("w2sb", [128, 4 * 512], BF16))
    gs = ps.enter_context(sb("gsb", [128, 4 * QH], BF16))
    Fb = ps.enter_context(sb("Fb", [128, 8 * 512], F32))
    xb = ps.enter_context(sb("xb5", [128, 2 * 512], F32))
    ob = ps.enter_context(sb("ob5", [128, 2 * 512], F32))
    rc = ps.enter_context(sb("rc5", [128, 512], F32))
    B2s = ps.enter_context(sb("B2sb", [16, D], BF16))
    psD = [ps.enter_context(nc.psum_tensor(f"p5psD{i}", [128, 512], F32))
           for i in range(8)]

    s.op("gp", lambda h: h.dma_start(out=B2s[:], in_=di["B2T"]),
         writes=("B2s",), tag="p5w")
    for kvh in range(KVH):
        s.op("sy", lambda h, kvh=kvh: h.dma_start(
            out=attn[:, kvh * G * QH:(kvh + 1) * G * QH], in_=b.a_scr[kvh]),
            writes=("attn",), tag="p5attn")
    for o in range(8):
        for hh in range(NO):
            wk = f"w2l{hh % 4}"
            gk = f"gl{hh % 4}"
            s.op("gp", lambda h, o=o, hh=hh: h.dma_start(
                out=w2s[:, (hh % 4) * 512:(hh % 4) * 512 + 512],
                in_=di["w2T"][o][hh]),
                writes=(wk,), tag=wk)
            s.op("sy", lambda h, hh=hh: h.dma_start(
                out=gs[:, (hh % 4) * QH:(hh % 4) * QH + QH], in_=b.g_scr[hh]),
                writes=(gk,), tag=gk)
            builds = [lambda h, hh=hh, tt=tt: h.matmul(
                psD[tt][:], gs[:, (hh % 4) * QH + tt * 128:
                               (hh % 4) * QH + (tt + 1) * 128],
                w2s[:, (hh % 4) * 512:(hh % 4) * 512 + 512],
                start=(hh == 0), stop=False) for tt in range(NT)]
            s.op("te", builds, reads=(wk, gk),
                 writes=tuple(f"psD{tt}" for tt in range(NT)))
        builds = [lambda h, o=o, tt=tt: h.matmul(
            psD[tt][:], b.t2b[:, tt * 128:(tt + 1) * 128],
            B2s[:, o * 512:(o + 1) * 512], start=False, stop=True)
            for tt in range(NT)]
        s.op("te", builds, reads=("t2b", "B2s"),
             writes=tuple(f"psD{tt}" for tt in range(NT)))
        for tt in range(NT):
            s.op("se", lambda h, tt=tt: h.activation(
                Fb[:, tt * 512:(tt + 1) * 512], psD[tt][:], AF.Silu),
                reads=(f"psD{tt}",), writes=(f"F{tt}",))
        for ah in range(2):
            wk = f"wo{ah}"
            s.op("gp", lambda h, o=o, ah=ah: h.dma_start(
                out=wo[:, ah * 8192: ah * 8192 + 8192].rearrange(
                    "p (a n) -> p a n", n=512),
                in_=di["woT"][o][ah * 16:(ah + 1) * 16].rearrange(
                    "a p n -> p a n")),
                writes=(wk,), tag=wk)
            for tt in range(NT):
                builds = [lambda h, a=a, ah=ah, tt=tt: h.matmul(
                    psD[tt][:],
                    attn[:, (ah * 16 + a) * QH + tt * 128:
                         (ah * 16 + a) * QH + (tt + 1) * 128],
                    wo[:, ah * 8192 + a * 512: ah * 8192 + (a + 1) * 512],
                    start=(ah == 0 and a == 0), stop=(ah == 1 and a == 15))
                    for a in range(16)]
                s.op("te", builds, reads=(wk, "attn", f"F{tt}"),
                     writes=(f"psD{tt}",))
        for tt in range(NT):
            xk = f"x5{tt % 2}"
            s.op("sy", lambda h, o=o, tt=tt: h.dma_start(
                out=xb[:, (tt % 2) * 512:(tt % 2) * 512 + 512],
                in_=di["xo"][tt * 128:(tt + 1) * 128, o * 512:(o + 1) * 512]),
                writes=(xk,), tag=xk)
            s.op("ve", lambda h, tt=tt: h.tensor_tensor(
                rc[:], psD[tt][:], Fb[:, tt * 512:(tt + 1) * 512], ALU.mult),
                reads=(f"psD{tt}", f"F{tt}"), writes=("rc",))
            s.op("ve", lambda h, tt=tt: h.tensor_tensor(
                ob[:, (tt % 2) * 512:(tt % 2) * 512 + 512], rc[:],
                xb[:, (tt % 2) * 512:(tt % 2) * 512 + 512], ALU.add),
                reads=("rc", xk), writes=(f"ob{tt % 2}",))
            s.op("sy", lambda h, o=o, tt=tt: h.dma_start(
                out=b.out[tt * 128:(tt + 1) * 128, o * 512:(o + 1) * 512],
                in_=ob[:, (tt % 2) * 512:(tt % 2) * 512 + 512]),
                reads=(f"ob{tt % 2}",), writes=(), tag=f"o5{tt % 2}")


def _emit_debug(b):
    s = b.s
    for name, ap in b.dbg.items():
        src = getattr(b, "dbgsrc", {}).get(name)
        if src is None:
            continue
        sap, reads = src
        s.barrier(engs=("sy",))
        s.op("sy", lambda h, ap=ap, sap=sap: h.dma_start(out=ap, in_=sap),
             reads=reads, writes=())



def kernel(**inputs):
    in_maps = stage_inputs(inputs)
    nc = build_program()
    res = run_bass_kernel_spmd(nc, in_maps, core_ids=list(range(NCORES)))
    out = np.empty((B, S, D), dtype=np.float32)
    for core in range(NCORES):
        bb, h = core // 2, core % 2
        out[bb, h * QH:(h + 1) * QH, :] = res.results[core]["out"]
    return out


# version 08
# speedup vs baseline: 1.0549x; 1.0549x over previous
import contextlib

import numpy as np
import ml_dtypes

import concourse.bass as bass
import concourse.mybir as mybir
from concourse.bass_utils import run_bass_kernel_spmd

F32 = mybir.dt.float32
BF16 = mybir.dt.bfloat16
AF = mybir.ActivationFunctionType
ALU = mybir.AluOpType
AX = mybir.AxisListType.X
BFDT = ml_dtypes.bfloat16

B, S, D = 4, 2048, 4096
H, KVH, G, DH = 32, 8, 4, 128
HID = 11008
NO = HID // 128
CB = 16
NBLK = S // CB
NCK = NBLK + 1
WIN = 64
NWTOK = 1152
QH = 1024
NT = QH // 128
SCALE = DH ** -0.5
NEG = float(np.float32(BFDT(-1e30)))
V30 = -NEG
LN1E10 = float(np.log(1e-10))
LORA_SCALE = 2.0
NCORES = 8



def _tables_for_core(h):
    qoff = QH * h
    f32 = np.float32
    inv = 1.0 / 10000.0 ** (np.arange(0, DH, 2, dtype=np.float64) / DH)

    def rope_tabs(positions):
        ang = positions[None, :] * inv[:, None]
        cosf = np.repeat(np.cos(ang), 2, axis=0).astype(f32)
        sinf = np.repeat(np.sin(ang), 2, axis=0).astype(f32)
        sinf[0::2, :] *= -1.0
        return np.ascontiguousarray(cosf), np.ascontiguousarray(sinf)

    cosk, sink = rope_tabs(np.arange(S, dtype=np.float64))
    cosq, sinq = rope_tabs(np.arange(qoff, qoff + QH, dtype=np.float64))
    cosw, sinw = rope_tabs(np.arange(qoff - WIN, qoff - WIN + NWTOK,
                                     dtype=np.float64))

    CM = np.zeros((NT, 128, NCK), dtype=f32)
    qpos = qoff + 128 * np.arange(NT)[:, None] + np.arange(128)[None, :]
    ends = 16 * np.arange(NBLK) + 15
    CM[:, :, :NBLK] = np.where(ends[None, None, :] < qpos[:, :, None], 0.0, NEG)

    SM = np.full((NT, 128, 192), NEG, dtype=f32)
    r = np.arange(128)[None, :, None]
    c = np.arange(192)[None, None, :]
    keypos = (qoff + 128 * np.arange(NT))[:, None, None] - WIN + c
    SM[(c >= r) & (c <= r + WIN) & (keypos >= 0)] = 0.0

    T2 = np.zeros((128, 3200), dtype=BFDT)
    for p in range(128):
        T2[p, qoff + 1024 + p] = BFDT(1.0)
    return dict(cosk=cosk, sink=sink, cosq=cosq, sinq=sinq, cosw=cosw, sinw=sinw,
                CM=np.ascontiguousarray(CM), SM=np.ascontiguousarray(SM), T2=T2)


def _shared_consts():
    ident = np.eye(128, dtype=BFDT)
    psw = np.zeros((128, 128), dtype=BFDT)
    for k in range(128):
        psw[k, k ^ 1] = BFDT(1.0)
    onesb = np.ones((1, 128), dtype=BFDT)
    EL = np.zeros((32, 512), dtype=BFDT)
    for bb in range(32):
        EL[bb, bb * 16:(bb + 1) * 16] = BFDT(1.0)
    DL = np.zeros((128, 128), dtype=BFDT)
    v30b = BFDT(V30)
    p = np.arange(128)
    rr = np.arange(128)
    m = (rr[None, :] // 16 == p[:, None] // 16) & (rr[None, :] % 16 >= p[:, None] % 16)
    DL[m] = v30b
    return dict(ident=ident, psw=psw, onesb=onesb, EL=EL, DL=DL)


def _stage_weights(I):
    f32 = np.float32
    ac = np.ascontiguousarray
    w = {}
    w["wqkvT"] = ac(I["L0_attn_w_qkv"].T.reshape(D, 48, 128).transpose(1, 0, 2))
    w["woT"] = ac(I["L0_attn_w_o"].T.reshape(32, 128, 8, 512).transpose(2, 0, 1, 3))
    w["w1T"] = ac(I["L0_ffn_w1"].T.reshape(D, NO, 128).transpose(1, 0, 2))
    w["w3T"] = ac(I["L0_ffn_w3"].T.reshape(D, NO, 128).transpose(1, 0, 2))
    w["w2T"] = ac(I["L0_ffn_w2"].T.reshape(NO, 128, 8, 512).transpose(2, 0, 1, 3))
    w["kw1T"] = ac(I["L0_attn_k_w1"].T.reshape(16, 128, 16, 128).transpose(2, 0, 1, 3))
    w["vw1T"] = ac(I["L0_attn_v_w1"].T.reshape(16, 128, 16, 128).transpose(2, 0, 1, 3))
    w["kw2T"] = ac(I["L0_attn_k_w2"].T.reshape(16, 128, 128))
    w["vw2T"] = ac(I["L0_attn_v_w2"].T.reshape(16, 128, 128))
    w["kb1"] = ac(I["L0_attn_k_b1"].reshape(16, 128).T.astype(f32))
    w["vb1"] = ac(I["L0_attn_v_b1"].reshape(16, 128).T.astype(f32))
    w["kb2"] = ac(I["L0_attn_k_b2"].reshape(128, 1).astype(f32))
    w["vb2"] = ac(I["L0_attn_v_b2"].reshape(128, 1).astype(f32))
    w["kposT"] = ac(I["L0_attn_k_pos"].transpose(2, 0, 1).reshape(128, 128))
    w["vposT"] = ac(I["L0_attn_v_pos"].transpose(2, 0, 1).reshape(128, 128))
    w["memk"] = ac(I["L0_attn_mem_kv"][0, :, 0, :].reshape(KVH, 128, 1))
    w["memv"] = ac(I["L0_attn_mem_kv"][1, :, 0, :])
    w["wgT"] = ac(I["L0_attn_w_gate"].T.reshape(32, 128, 96))
    w["bg"] = ac(I["L0_attn_b_gate"].reshape(1, 96))
    w["A1T"] = ac(I["L0_ffn_lora_A1"].T.reshape(32, 128, 16))
    w["A3T"] = ac(I["L0_ffn_lora_A3"].T.reshape(32, 128, 16))
    w["A2T"] = ac(I["L0_ffn_lora_A2"].T.reshape(NO, 128, 16))
    w["B1T"] = ac(I["L0_ffn_lora_B1"].T)
    w["B3T"] = ac(I["L0_ffn_lora_B3"].T)
    w["B2T"] = ac(I["L0_ffn_lora_B2"].T)
    return w


def stage_inputs(inputs):
    x = np.asarray(inputs["L0_attn_hidden_states"], dtype=np.float32)
    w = _stage_weights(inputs)
    consts = _shared_consts()
    tabs = [_tables_for_core(h) for h in (0, 1)]
    in_maps = []
    for core in range(NCORES):
        bb, h = core // 2, core % 2
        qoff = QH * h
        xw = np.zeros((NWTOK, D), dtype=np.float32)
        lo, hi = qoff - WIN, qoff - WIN + NWTOK
        slo, shi = max(lo, 0), min(hi, S)
        xw[slo - lo: shi - lo, :] = x[bb, slo:shi, :]
        m = dict(w)
        m.update(consts)
        m.update(tabs[h])
        m["xf"] = np.ascontiguousarray(x[bb])
        m["xo"] = np.ascontiguousarray(x[bb, qoff:qoff + QH])
        m["xw"] = xw
        in_maps.append(m)
    return in_maps



ENGS = ("te", "ve", "se", "gp", "sy")
DMA_ENGS = ("gp", "sy")


class Sched:

    def __init__(self):
        self.q = {e: [] for e in ENGS}
        self.cnt = {}
        for e in ENGS:
            self.cnt[e] = 0
        self.sem = {}
        self.tags = []
        self.buf = {}
        self.hwm = {e: {} for e in ENGS}

    def _need_to_waits(self, eng, need):
        out = []
        for s2, n in sorted(need.items()):
            if n > self.hwm[eng].get(s2, 0):
                self.hwm[eng][s2] = n
                out.append((s2, n))
        return out

    def _waits(self, eng, reads, writes, sreads):
        need = {}

        def add(sname, n):
            if sname.startswith("dma_"):
                n = self.cnt[sname]
            need[sname] = max(need.get(sname, 0), n)

        strict = eng in ("ve", "se")
        for key in list(reads) + list(sreads):
            ent = self.buf.get(key)
            if ent is None:
                continue
            we, wn, haz = ent["w"]
            if we is None:
                continue
            if we != eng or haz or strict or key in sreads:
                add(we, wn)
        for key in writes:
            ent = self.buf.get(key)
            if ent is None:
                continue
            we, wn, haz = ent["w"]
            if we is not None and (we != eng or strict):
                add(we, wn)
            for re_, rn in ent["r"]:
                if re_ != eng or strict:
                    add(re_, rn)
        return self._need_to_waits(eng, need)

    def op(self, eng, build, reads=(), writes=(), sreads=(), hazard=False,
           tag=None):
        builds = build if isinstance(build, (list, tuple)) else [build]
        is_dma = eng in DMA_ENGS
        if is_dma:
            assert tag is not None, "DMA op needs tag"
            sname = "dma_" + tag
            if sname not in self.cnt:
                self.cnt[sname] = 0
                self.tags.append(sname)
        else:
            sname = eng
        waits = self._waits(eng, reads, writes, sreads)
        self.cnt[sname] += 16 if is_dma else 1
        n = self.cnt[sname]

        def fn(h, _w=waits, _b=builds, _s=sname, _dma=is_dma):
            for s2, v in _w:
                h.wait_ge(self.sem[s2], v)
            ins = None
            for bb in _b:
                ins = bb(h)
            ins.then_inc(self.sem[_s], 16 if _dma else 1)
        self.q[eng].append(fn)
        for key in writes:
            self.buf[key] = {"w": (sname, n, hazard), "r": []}
        for key in reads:
            ent = self.buf.setdefault(key, {"w": (None, 0, False), "r": []})
            ent["r"].append((sname, n))
        return n

    def raw_wait(self, eng, s2, n=None):
        v = self.cnt[s2] if n is None else n
        if v > self.hwm[eng].get(s2, 0):
            self.hwm[eng][s2] = v
            self.q[eng].append(lambda h, _s2=s2, _v=v: h.wait_ge(self.sem[_s2], _v))

    def barrier(self, engs=ENGS):
        on = [e for e in ENGS if e not in DMA_ENGS] + self.tags
        for e in engs:
            for s2 in on:
                if s2 != e and self.cnt.get(s2, 0) > 0:
                    self.raw_wait(e, s2)



def _dram_inputs(nc):
    t = {}

    def di(name, shape, dtype=F32):
        t[name] = nc.dram_tensor(name, list(shape), dtype, kind="ExternalInput").ap()

    di("xf", (S, D)); di("xo", (QH, D)); di("xw", (NWTOK, D))
    di("wqkvT", (48, D, 128)); di("woT", (8, 32, 128, 512))
    di("w1T", (NO, D, 128)); di("w3T", (NO, D, 128)); di("w2T", (8, NO, 128, 512))
    di("kw1T", (16, 16, 128, 128)); di("vw1T", (16, 16, 128, 128))
    di("kw2T", (16, 128, 128)); di("vw2T", (16, 128, 128))
    di("kb1", (128, 16)); di("vb1", (128, 16))
    di("kb2", (128, 1)); di("vb2", (128, 1))
    di("kposT", (128, 128)); di("vposT", (128, 128))
    di("memk", (KVH, 128, 1)); di("memv", (8, 128))
    di("wgT", (32, 128, 96)); di("bg", (1, 96))
    di("A1T", (32, 128, 16)); di("A3T", (32, 128, 16)); di("A2T", (NO, 128, 16))
    di("B1T", (16, HID)); di("B3T", (16, HID)); di("B2T", (16, D))
    di("cosk", (128, S)); di("sink", (128, S))
    di("cosq", (128, QH)); di("sinq", (128, QH))
    di("cosw", (128, NWTOK)); di("sinw", (128, NWTOK))
    di("CM", (NT, 128, NCK)); di("SM", (NT, 128, 192))
    di("T2", (128, 3200), BF16)
    di("ident", (128, 128), BF16); di("psw", (128, 128), BF16)
    di("onesb", (1, 128), BF16); di("EL", (32, 512), BF16); di("DL", (128, 128), BF16)
    return t


class Bld:
    pass


def build_program(upto=9, debug=()):
    nc = bass.Bass("TRN2", target_bir_lowering=False, debug=False)
    b = Bld()
    b.nc = nc
    b.s = Sched()
    b.di = _dram_inputs(nc)
    b.out = nc.dram_tensor("out", [QH, D], F32, kind="ExternalOutput").ap()
    b.q_scr = nc.dram_tensor("q_scr", [KVH, 128, G * QH], BF16, kind="Internal").ap()
    b.g_scr = nc.dram_tensor("g_scr", [NO, 128, QH], BF16, kind="Internal").ap()
    b.a_scr = nc.dram_tensor("a_scr", [KVH, 128, G * QH], BF16, kind="Internal").ap()

    es = contextlib.ExitStack()
    with es:
        sbuf = nc.sbuf_tensor
        b.ckf = es.enter_context(sbuf("ckf", [128, KVH * NCK], BF16))
        b.cvt1 = es.enter_context(sbuf("cvt1", [128, KVH * 128], BF16))
        b.cvt2 = es.enter_context(sbuf("cvt2", [1, KVH * 128], BF16))
        b.gates = es.enter_context(sbuf("gates", [128, NT * 96], F32))
        b.t2b = es.enter_context(sbuf("t2b", [16, QH], BF16))
        b.ident = es.enter_context(sbuf("sb_idn", [128, 128], BF16))
        b.psw = es.enter_context(sbuf("sb_psw", [128, 128], BF16))
        b.onesb = es.enter_context(sbuf("sb_onesb", [1, 128], BF16))
        b.el = es.enter_context(sbuf("sb_el", [32, 512], BF16))
        b.dl = es.enter_context(sbuf("sb_dl", [128, 128], BF16))
        b.stat = es.enter_context(sbuf("stat", [128, 64], F32))

        b.dumps = dict(debug) if not isinstance(debug, dict) else debug
        _p_consts(b)
        with contextlib.ExitStack() as xs:
            b.xn = xs.enter_context(sbuf("xn", [128, 32 * QH], BF16))
            with contextlib.ExitStack() as ps:
                _p0(b, ps)
            if upto >= 1:
                b.s.barrier()
                with contextlib.ExitStack() as ps:
                    _p4a(b, ps)
            if upto >= 2:
                b.s.barrier()
                with contextlib.ExitStack() as ps:
                    _p4b(b, ps)
            _emit_dumps(b, only_scope="p04")
        if upto >= 3:
            b.s.barrier()
            with contextlib.ExitStack() as kvs:
                b.kf = kvs.enter_context(sbuf("kf", [128, KVH * S], BF16))
                b.vf = kvs.enter_context(sbuf("vf", [128, KVH * S], BF16))
                b.kwin = kvs.enter_context(sbuf("kwin", [128, KVH * NWTOK], BF16))
                b.vwin = kvs.enter_context(sbuf("vwin", [128, KVH * NWTOK], BF16))
                with contextlib.ExitStack() as ps:
                    _p1(b, ps)
                if upto >= 4:
                    b.s.barrier()
                    with contextlib.ExitStack() as ps:
                        _p2(b, ps)
                _emit_dumps(b, only_scope="p12")
                if upto >= 5:
                    b.s.barrier()
                    with contextlib.ExitStack() as ps:
                        _p3(b, ps)
                        _emit_dumps(b, only_scope="p3")
        if upto >= 6:
            b.s.barrier()
            with contextlib.ExitStack() as ps:
                _p5(b, ps)
        _emit_dumps(b, only_scope="dram")
        _replay(b)
    return nc


def _emit_dumps(b, only_scope):
    s = b.s
    todo = [(n, sp) for n, (sc, sp) in b.dumps.items() if sc == only_scope]
    if not todo:
        return
    s.barrier(engs=("sy",))
    for name, src_fn in todo:
        ap, shape, dt = src_fn(b)
        dst = b.nc.dram_tensor("dump_" + name, list(shape), dt,
                               kind="ExternalOutput").ap()
        s.op("sy", lambda h, ap=ap, dst=dst: h.dma_start(out=dst, in_=ap),
             reads=(), writes=(), tag="dump")


def _replay(b):
    s = b.s
    with contextlib.ExitStack() as es:
        for e in ENGS:
            s.sem[e] = es.enter_context(b.nc.semaphore(f"sem_{e}"))
        for t in s.tags:
            s.sem[t] = es.enter_context(b.nc.semaphore(t))
        block = es.enter_context(b.nc.Block())

        @block.tensor
        def _(h):
            for fn in s.q["te"]:
                fn(h)

        @block.vector
        def _(h):
            for fn in s.q["ve"]:
                fn(h)

        @block.scalar
        def _(h):
            for fn in s.q["se"]:
                fn(h)

        @block.gpsimd
        def _(h):
            for fn in s.q["gp"]:
                fn(h)

        @block.sync
        def _(h):
            for fn in s.q["sy"]:
                fn(h)
            for t in s.tags:
                h.wait_ge(s.sem[t], s.cnt[t])


def _p_consts(b):
    s, di = b.s, b.di
    for dst, src in ((b.ident, "ident"), (b.psw, "psw"), (b.onesb, "onesb"),
                     (b.el, "EL"), (b.dl, "DL")):
        s.op("sy", lambda h, d=dst, sr=src: h.dma_start(out=d[:], in_=di[sr]),
             writes=(src,), tag="consts")
    s.op("ve", lambda h: h.memset(b.stat[:, 60:61], -1000.0), writes=("c1000",))



def _ln_tiles(b, src_ap, ntile, sink_fn, xbuf, xnt):
    s, st = b.s, b.stat
    for i in range(ntile):
        sl = (i % 2) * D
        xk = f"lnx{i % 2}"
        s.op("sy", lambda h, i=i, sl=sl: h.dma_start(
            out=xbuf[:, sl:sl + D], in_=src_ap[i * 128:(i + 1) * 128, :]),
            reads=(), writes=(xk,), tag=xk)
        xs = xbuf[:, sl:sl + D]
        s.op("se", lambda h, xs=xs: h.activation(xnt[:], xs, AF.Square,
                                                 accum_out=st[:, 1:2]),
             reads=(xk,), writes=("xnt", "ss"), hazard=True)
        s.op("ve", lambda h, xs=xs: h.reduce_sum(st[:, 0:1], xs, axis=AX),
             reads=(xk,), writes=("s1",), hazard=True)
        s.op("ve", lambda h: h.tensor_scalar(st[:, 2:3], st[:, 0:1], 1.0 / D,
                                             None, ALU.mult),
             reads=("s1",), writes=("mean",))
        s.op("ve", lambda h: h.tensor_scalar(st[:, 4:5], st[:, 1:2], 1.0 / D,
                                             None, ALU.mult),
             reads=("ss",), writes=("ex2",))
        s.op("ve", lambda h: h.tensor_tensor(st[:, 3:4], st[:, 2:3], st[:, 2:3],
                                             ALU.mult),
             reads=("mean",), writes=("m2",))
        s.op("ve", lambda h: h.tensor_tensor(st[:, 5:6], st[:, 4:5], st[:, 3:4],
                                             ALU.subtract),
             reads=("ex2", "m2"), writes=("var",))
        s.op("ve", lambda h: h.tensor_scalar(st[:, 6:7], st[:, 5:6], 1e-5, None,
                                             ALU.add),
             reads=("var",), writes=("vpe",))
        s.op("se", lambda h: h.activation(st[:, 7:8], st[:, 6:7], AF.Sqrt),
             reads=("vpe",), writes=("std",))
        s.op("ve", lambda h: h.reciprocal(st[:, 8:9], st[:, 7:8]),
             reads=("std",), writes=("rstd",), hazard=True)
        s.op("ve", lambda h, xs=xs: h.tensor_scalar(xnt[:], xs, st[:, 2:3],
                                                    st[:, 8:9], ALU.subtract,
                                                    ALU.mult),
             reads=(xk, "mean", "rstd"), writes=("xnt",),
             sreads=("mean", "rstd"))
        sink_fn(i)


def _xn_transpose_sink(b, xnt, pst, dst, dst_stride, dstkey, base_col=0):
    s = b.s

    def sink(i):
        for jj in range(8):
            pk = f"pst{jj % 2}"
            builds = []
            for cc in range(4):
                c = jj * 4 + cc
                builds.append(lambda h, c=c, cc=cc, jj=jj: h.transpose(
                    pst[jj % 2][:, cc * 128:(cc + 1) * 128],
                    xnt[:, c * 128:(c + 1) * 128], b.ident[:]))
            s.op("te", builds, reads=("xnt", "ident"), writes=(pk,))
            dv = dst[:].rearrange("p (c t) -> p c t", t=dst_stride)
            s.op("ve", lambda h, jj=jj, i=i, dv=dv: h.tensor_copy(
                dv[:, jj * 4:(jj + 1) * 4,
                   base_col + i * 128: base_col + (i + 1) * 128],
                pst[jj % 2][:].rearrange("p (c t) -> p c t", t=128)),
                reads=(pk,), writes=(dstkey,))
    return sink



def _p0(b, ps):
    nc, s = b.nc, b.s
    xbuf = ps.enter_context(nc.sbuf_tensor("p0xb", [128, 2 * D], F32))
    xnt = ps.enter_context(nc.sbuf_tensor("p0xnt", [128, D], BF16))
    pst = [ps.enter_context(nc.psum_tensor(f"p0pst{i}", [128, 512], BF16))
           for i in range(2)]
    sink = _xn_transpose_sink(b, xnt, pst, b.xn, QH, "xndst")
    _ln_tiles(b, b.di["xo"], NT, sink, xbuf, xnt)



def _p4a(b, ps):
    nc, s, di = b.nc, b.s, b.di
    wq = ps.enter_context(nc.sbuf_tensor("wq", [128, 2 * D], BF16))
    qo = ps.enter_context(nc.sbuf_tensor("qo", [128, 2 * QH], BF16))
    wg = ps.enter_context(nc.sbuf_tensor("wg", [128, 32 * 96], BF16))
    bgb = ps.enter_context(nc.sbuf_tensor("bgb", [1, 96], BF16))
    psq = [ps.enter_context(nc.psum_tensor(f"psq{i}", [128, 512], F32))
           for i in range(2)]
    psg = ps.enter_context(nc.psum_tensor("psg", [128, 96], F32))

    s.op("gp", lambda h: h.dma_start(
        out=wg[:].rearrange("p (c n) -> p c n", n=96),
        in_=di["wgT"].rearrange("c p n -> p c n")),
        writes=("wg",), tag="p4aw")
    s.op("gp", lambda h: h.dma_start(out=bgb[:], in_=di["bg"]), writes=("bgb",),
         tag="p4aw")
    for o in range(32):
        wk = f"wq{o % 2}"
        s.op("gp", lambda h, o=o: h.dma_start(
            out=wq[:, (o % 2) * D:(o % 2) * D + D].rearrange(
                "p (c n) -> p c n", n=128),
            in_=di["wqkvT"][o].rearrange("(c p) n -> p c n", p=128)),
            writes=(wk,), tag=wk)
        wqs = wq[:, (o % 2) * D:(o % 2) * D + D].rearrange("p (c n) -> p c n", n=128)
        for half in range(2):
            pk = f"psq{half}"
            builds = [lambda h, c=c, half=half, wqs=wqs: h.matmul(
                psq[half][:], wqs[:, c, :],
                b.xn[:, c * QH + half * 512: c * QH + half * 512 + 512],
                start=(c == 0), stop=(c == 31)) for c in range(32)]
            s.op("te", builds, reads=(wk, "xndst"), writes=(pk,))
            qk = f"qo{o % 2}"
            s.op("se", lambda h, o=o, half=half: h.activation(
                qo[:, (o % 2) * QH + half * 512:(o % 2) * QH + half * 512 + 512],
                psq[half][:], AF.Copy, bias=0.0, scale=SCALE),
                reads=(pk,), writes=(qk + f"h{half}",))
        s.op("sy", lambda h, o=o: h.dma_start(
            out=b.q_scr[o // 4][:, (o % 4) * QH:(o % 4 + 1) * QH],
            in_=qo[:, (o % 2) * QH:(o % 2) * QH + QH]),
            reads=(f"qo{o % 2}h0", f"qo{o % 2}h1"), writes=(),
            tag=f"qsp{o % 2}")
    for tt in range(NT):
        builds = [lambda h, c=c, tt=tt: h.matmul(
            psg[:], b.xn[:, c * QH + tt * 128: c * QH + (tt + 1) * 128],
            wg[:, c * 96:(c + 1) * 96], start=(c == 0), stop=False)
            for c in range(32)]
        builds.append(lambda h: h.matmul(psg[:], b.onesb[:, 0:128], bgb[:],
                                         start=False, stop=True))
        s.op("te", builds, reads=("wg", "bgb", "xndst", "onesb"), writes=("psg",))
        s.op("se", lambda h, tt=tt: h.activation(
            b.gates[:, tt * 96:(tt + 1) * 96], psg[:], AF.Sigmoid),
            reads=("psg",), writes=("gates",))



def _p4b(b, ps):
    nc, s, di = b.nc, b.s, b.di
    wf = ps.enter_context(nc.sbuf_tensor("wf", [128, 8 * D], BF16))
    gob = ps.enter_context(nc.sbuf_tensor("gob", [128, 4 * QH], BF16))
    a1s = ps.enter_context(nc.sbuf_tensor("a1s", [128, 32 * 16], BF16))
    a3s = ps.enter_context(nc.sbuf_tensor("a3s", [128, 32 * 16], BF16))
    a2s = ps.enter_context(nc.sbuf_tensor("a2s", [128, NO * 16], BF16))
    b1s = ps.enter_context(nc.sbuf_tensor("b1s", [16, HID], BF16))
    b3s = ps.enter_context(nc.sbuf_tensor("b3s", [16, HID], BF16))
    t1b = ps.enter_context(nc.sbuf_tensor("t1b", [16, QH], BF16))
    t3b = ps.enter_context(nc.sbuf_tensor("t3b", [16, QH], BF16))
    sil = ps.enter_context(nc.sbuf_tensor("sil", [128, QH], F32))
    ps1 = [ps.enter_context(nc.psum_tensor(f"ph1{i}", [128, 512], F32))
           for i in range(2)]
    ps3 = [ps.enter_context(nc.psum_tensor(f"ph3{i}", [128, 512], F32))
           for i in range(2)]
    pt2 = ps.enter_context(nc.psum_tensor("pt2", [16, QH], F32))

    for nm, dst in (("A1T", a1s), ("A3T", a3s), ("A2T", a2s)):
        s.op("gp", lambda h, nm=nm, dst=dst: h.dma_start(
            out=dst[:].rearrange("p (c n) -> p c n", n=16),
            in_=di[nm].rearrange("c p n -> p c n")),
            writes=(nm,), tag="p4bw")
    for nm, dst in (("B1T", b1s), ("B3T", b3s)):
        s.op("gp", lambda h, nm=nm, dst=dst: h.dma_start(out=dst[:], in_=di[nm]),
             writes=(nm,), tag="p4bw")
    for a_s, akey, tb, tk, pp in ((a1s, "A1T", t1b, "t1b", ps1),
                                  (a3s, "A3T", t3b, "t3b", ps3)):
        builds = []
        for hh in range(2):
            for c in range(32):
                builds.append(lambda h, c=c, hh=hh, a_s=a_s, pp=pp: h.matmul(
                    pp[hh][0:16, :], a_s[:, c * 16:(c + 1) * 16],
                    b.xn[:, c * QH + hh * 512: c * QH + hh * 512 + 512],
                    start=(c == 0), stop=(c == 31)))
        s.op("te", builds, reads=(akey, "xndst"), writes=("ps10", "ps11")
             if pp is ps1 else ("ps30", "ps31"))
        for hh in range(2):
            s.op("ve", lambda h, tb=tb, hh=hh, pp=pp: h.tensor_scalar(
                tb[:, hh * 512: hh * 512 + 512], pp[hh][0:16, :], LORA_SCALE,
                None, ALU.mult),
                reads=(("ps10", "ps11") if pp is ps1 else ("ps30", "ps31"))[hh:hh + 1],
                writes=(tk + str(hh),))

    for o in range(NO):
        for wsl, nm in ((0, "w1T"), (1, "w3T")):
            wk = f"wf{wsl}{o % 4}"
            s.op("gp", lambda h, o=o, wsl=wsl, nm=nm: h.dma_start(
                out=wf[:, (4 * wsl + o % 4) * D:(4 * wsl + o % 4) * D + D]
                    .rearrange("p (c n) -> p c n", n=128),
                in_=di[nm][o].rearrange("(c p) n -> p c n", p=128)),
            writes=(wk,), tag=wk)
        w1s = wf[:, (o % 4) * D:(o % 4) * D + D].rearrange("p (c n) -> p c n", n=128)
        w3s = wf[:, (4 + o % 4) * D:(4 + o % 4) * D + D].rearrange(
            "p (c n) -> p c n", n=128)
        for half in range(2):
            hs = half * 512
            bl1 = [lambda h, c=c, half=half, w1s=w1s: h.matmul(
                ps1[half][:], w1s[:, c, :],
                b.xn[:, c * QH + half * 512: c * QH + half * 512 + 512],
                start=(c == 0), stop=False) for c in range(32)]
            bl1.append(lambda h, o=o, half=half: h.matmul(
                ps1[half][:], b1s[:, o * 128:(o + 1) * 128],
                t1b[:, half * 512: half * 512 + 512], start=False, stop=True))
            s.op("te", bl1, reads=(f"wf0{o % 4}", "xndst", "B1T", "t1b0", "t1b1"),
                 writes=(f"ps1{half}",))
            bl3 = [lambda h, c=c, half=half, w3s=w3s: h.matmul(
                ps3[half][:], w3s[:, c, :],
                b.xn[:, c * QH + half * 512: c * QH + half * 512 + 512],
                start=(c == 0), stop=False) for c in range(32)]
            bl3.append(lambda h, o=o, half=half: h.matmul(
                ps3[half][:], b3s[:, o * 128:(o + 1) * 128],
                t3b[:, half * 512: half * 512 + 512], start=False, stop=True))
            s.op("te", bl3, reads=(f"wf1{o % 4}", "xndst", "B3T", "t3b0", "t3b1"),
                 writes=(f"ps3{half}",))
            s.op("se", lambda h, half=half: h.activation(
                sil[:, half * 512: half * 512 + 512], ps1[half][:], AF.Silu),
                reads=(f"ps1{half}",), writes=(f"sil{half}",))
            gk = f"gob{o % 4}"
            s.op("ve", lambda h, o=o, half=half: h.tensor_tensor(
                gob[:, (o % 4) * QH + half * 512:(o % 4) * QH + half * 512 + 512],
                sil[:, half * 512: half * 512 + 512], ps3[half][:], ALU.mult),
                reads=(f"sil{half}", f"ps3{half}"), writes=(gk + f"h{half}",))
        s.op("sy", lambda h, o=o: h.dma_start(
            out=b.g_scr[o], in_=gob[:, (o % 4) * QH:(o % 4) * QH + QH]),
            reads=(f"gob{o % 4}h0", f"gob{o % 4}h1"), writes=(),
            tag=f"gsp{o % 2}")
        if o >= 2:
            od = o - 2
            bt2 = [lambda h, od=od, hh=hh: h.matmul(
                pt2[:, hh * 512: hh * 512 + 512], a2s[:, od * 16:(od + 1) * 16],
                gob[:, (od % 4) * QH + hh * 512:(od % 4) * QH + hh * 512 + 512],
                start=(od == 0), stop=False) for hh in range(2)]
            s.op("te", bt2, reads=("A2T", f"gob{od % 4}h0", f"gob{od % 4}h1"),
                 writes=())
    for od in (NO - 2, NO - 1):
        bt2 = [lambda h, od=od, hh=hh: h.matmul(
            pt2[:, hh * 512: hh * 512 + 512], a2s[:, od * 16:(od + 1) * 16],
            gob[:, (od % 4) * QH + hh * 512:(od % 4) * QH + hh * 512 + 512],
            start=False, stop=(od == NO - 1)) for hh in range(2)]
        s.op("te", bt2, reads=("A2T", f"gob{od % 4}h0", f"gob{od % 4}h1"),
             writes=() if od < NO - 1 else ("pt2",))
    s.op("ve", lambda h: h.tensor_scalar(b.t2b[:], pt2[:], LORA_SCALE, None,
                                         ALU.mult),
         reads=("pt2",), writes=("t2b",))



def _p1(b, ps):
    nc, s, di = b.nc, b.s, b.di
    xbuf = ps.enter_context(nc.sbuf_tensor("p1xb", [128, 2 * D], F32))
    xnt = ps.enter_context(nc.sbuf_tensor("p1xnt", [128, D], BF16))
    xnkv = ps.enter_context(nc.sbuf_tensor("xnkv", [128, 32 * 512], BF16))
    wkv = ps.enter_context(nc.sbuf_tensor("wkv", [128, 2 * D], BF16))
    pst = [ps.enter_context(nc.psum_tensor(f"p1pst{i}", [128, 512], BF16))
           for i in range(2)]
    psp = [ps.enter_context(nc.psum_tensor(f"p1psp{i}", [128, 512], F32))
           for i in range(2)]
    groups = [(di["xf"], gg * 512, 512, "glob") for gg in range(4)]
    groups += [(di["xw"], gg * 512, 512, "win") for gg in range(2)]
    groups += [(di["xw"], 1024, 128, "win")]
    for gi, (src, tok0, ntok, kind) in enumerate(groups):
        nt = ntok // 128
        sink = _xn_transpose_sink(b, xnt, pst, xnkv, 512, "xnkvdst")
        _ln_tiles(b, src[tok0:tok0 + ntok], nt, sink, xbuf, xnt)
        for o in range(16):
            wk = f"wkv{o % 2}"
            s.op("gp", lambda h, o=o: h.dma_start(
                out=wkv[:, (o % 2) * D:(o % 2) * D + D].rearrange(
                    "p (c n) -> p c n", n=128),
                in_=di["wqkvT"][32 + o].rearrange("(c p) n -> p c n", p=128)),
                writes=(wk,), tag=wk)
            wks = wkv[:, (o % 2) * D:(o % 2) * D + D].rearrange(
                "p (c n) -> p c n", n=128)
            builds = [lambda h, c=c, wks=wks, ntok=ntok, o=o: h.matmul(
                psp[o % 2][:, 0:ntok],
                wks[:, c, :], xnkv[:, c * 512: c * 512 + ntok],
                start=(c == 0), stop=(c == 31)) for c in range(32)]
            s.op("te", builds, reads=(wk, "xnkvdst"), writes=(f"pkv{o % 2}",))
            kvh = o % 8
            if kind == "glob":
                dst = (b.kf if o < 8 else b.vf)[:, kvh * S + tok0: kvh * S + tok0 + ntok]
                dkey = "kf" if o < 8 else "vf"
            else:
                dst = (b.kwin if o < 8 else b.vwin)[
                    :, kvh * NWTOK + tok0: kvh * NWTOK + tok0 + ntok]
                dkey = "kwin" if o < 8 else "vwin"
            s.op("se", lambda h, dst=dst, o=o, ntok=ntok: h.activation(
                dst, psp[o % 2][:, 0:ntok], AF.Copy),
                reads=(f"pkv{o % 2}",), writes=(dkey,))



def _rope_inplace(b, buf, bufkey, ncol_per_head, cos_sb, coskey, sin_sb,
                  psr, f32a, f32b):
    s = b.s
    ntile = (ncol_per_head + 511) // 512
    for k in range(KVH):
        for u in range(ntile):
            w = min(512, ncol_per_head - u * 512)
            c0 = k * ncol_per_head + u * 512
            t0 = u * 512
            sl = buf[:, c0:c0 + w]
            pk = f"ropeps{u % 2}"
            s.op("te", lambda h, sl=sl, u=u, w=w: h.matmul(
                psr[u % 2][:, 0:w], b.psw[:], sl, start=True, stop=True),
                reads=(bufkey, "psw"), writes=(pk,))
            s.op("se", lambda h, sl=sl, w=w: h.activation(
                f32a[:, 0:w], sl, AF.Copy),
                reads=(bufkey,), writes=("ropea",))
            s.op("ve", lambda h, w=w, t0=t0: h.tensor_tensor(
                f32a[:, 0:w], f32a[:, 0:w], cos_sb[:, t0:t0 + w], ALU.mult),
                reads=("ropea", coskey), writes=("ropea",))
            s.op("ve", lambda h, w=w, u=u, t0=t0: h.tensor_tensor(
                f32b[:, 0:w], psr[u % 2][:, 0:w], sin_sb[:, t0:t0 + w],
                ALU.mult),
                reads=(pk, coskey), writes=("ropeb",))
            s.op("ve", lambda h, sl=sl, w=w: h.tensor_tensor(
                sl, f32a[:, 0:w], f32b[:, 0:w], ALU.add),
                reads=("ropea", "ropeb"), writes=(bufkey,))


def _p2(b, ps):
    nc, s, di = b.nc, b.s, b.di
    cosk = ps.enter_context(nc.sbuf_tensor("coskb", [128, S], F32))
    sinkb = ps.enter_context(nc.sbuf_tensor("sinkb", [128, S], F32))
    cosw = ps.enter_context(nc.sbuf_tensor("coswb", [128, NWTOK], F32))
    sinw = ps.enter_context(nc.sbuf_tensor("sinwb", [128, NWTOK], F32))
    posb = ps.enter_context(nc.sbuf_tensor("posb", [128, 2 * 128], F32))
    b1b = ps.enter_context(nc.sbuf_tensor("b1b", [128, 2 * 16], F32))
    b2b = ps.enter_context(nc.sbuf_tensor("b2b", [128, 2], F32))
    w1b = ps.enter_context(nc.sbuf_tensor("w1b", [128, 2 * 2048], BF16))
    w2b = ps.enter_context(nc.sbuf_tensor("w2b", [128, 2 * 2048], BF16))
    cki = ps.enter_context(nc.sbuf_tensor("cki", [128, 4 * 16 * 128], BF16))
    hid = ps.enter_context(nc.sbuf_tensor("hid", [128, 4 * 16 * 128], BF16))
    f32a = ps.enter_context(nc.sbuf_tensor("rpa", [128, 512], F32))
    f32b = ps.enter_context(nc.sbuf_tensor("rpb", [128, 512], F32))
    cvfb = ps.enter_context(nc.sbuf_tensor("cvfb", [128, 128], BF16))
    psc = [ps.enter_context(nc.psum_tensor(f"p2psc{i}", [128, 512], F32))
           for i in range(2)]
    psc2 = ps.enter_context(nc.psum_tensor("p2psc2", [128, 128], F32))
    psr = [ps.enter_context(nc.psum_tensor(f"p2psr{i}", [128, 512], F32))
           for i in range(2)]
    pst = [ps.enter_context(nc.psum_tensor(f"p2pst{i}", [128, 512], BF16))
           for i in range(2)]

    for nm, dst in (("cosk", cosk), ("sink", sinkb), ("cosw", cosw),
                    ("sinw", sinw)):
        s.op("sy", lambda h, nm=nm, dst=dst: h.dma_start(out=dst[:], in_=di[nm]),
             writes=(nm,), tag="p2tab")
    for bld, wkey in (
        (lambda h: h.dma_start(out=posb[:, 0:128], in_=di["kposT"]), "posb"),
        (lambda h: h.dma_start(out=posb[:, 128:256], in_=di["vposT"]), "posb"),
        (lambda h: h.dma_start(out=b1b[:, 0:16], in_=di["kb1"]), "b1b"),
        (lambda h: h.dma_start(out=b1b[:, 16:32], in_=di["vb1"]), "b1b"),
        (lambda h: h.dma_start(out=b2b[:, 0:1], in_=di["kb2"]), "b2b"),
        (lambda h: h.dma_start(out=b2b[:, 1:2], in_=di["vb2"]), "b2b"),
    ):
        s.op("sy", bld, writes=(wkey,), tag="p2tab")
    for side, nm in ((0, "kw2T"), (1, "vw2T")):
        s.op("gp", lambda h, side=side, nm=nm: h.dma_start(
            out=w2b[:, side * 2048: side * 2048 + 2048].rearrange(
                "p (o n) -> p o n", n=128),
            in_=di[nm].rearrange("o p n -> p o n")),
            writes=("w2b",), tag="p2w2")
    for k in range(KVH):
        s.op("gp", lambda h, k=k: h.dma_start(
            out=b.ckf[:, k * NCK + 128: k * NCK + 129],
            in_=di["memk"][k]),
            writes=("ckf",), tag="p2w2")
        s.op("gp", lambda h, k=k: h.dma_start(
            out=b.cvt2[:, k * 128:(k + 1) * 128], in_=di["memv"][k:k + 1, :]),
            writes=("cvt2",), tag="p2w2")

    for side in range(2):
        src_ = b.kf if side == 0 else b.vf
        skey = "kf" if side == 0 else "vf"
        w1nm = "kw1T" if side == 0 else "vw1T"
        for kh in range(2):
            for ki in range(4):
                k = 4 * kh + ki
                for c in range(16):
                    s.op("ve", lambda h, k=k, c=c, ki=ki, side=side,
                         src_=src_: h.tensor_scalar(
                        cki[:, ki * 2048 + c * 128: ki * 2048 + (c + 1) * 128],
                        src_[:, k * S + c: k * S + S: 16],
                        posb[:, side * 128 + k * 16 + c:
                             side * 128 + k * 16 + c + 1],
                        None, ALU.add),
                        reads=(skey, "posb"), writes=(f"cki{ki}",))
            if side == 0 and kh == 0 and getattr(b, "dbg_cki", None) is not None:
                s.barrier(engs=("sy",))
                s.op("sy", lambda h: h.dma_start(out=b.dbg_cki, in_=cki[:]),
                     reads=("cki0", "cki1", "cki2", "cki3"), writes=(),
                     tag="dump")
            for o in range(16):
                wk = f"w1b{o % 2}"
                s.op("gp", lambda h, o=o, w1nm=w1nm: h.dma_start(
                    out=w1b[:, (o % 2) * 2048:(o % 2) * 2048 + 2048].rearrange(
                        "p (c n) -> p c n", n=128),
                    in_=di[w1nm][o].rearrange("c p n -> p c n")),
                    writes=(wk,), tag=wk)
                for ki in range(4):
                    builds = [lambda h, c=c, o=o, ki=ki: h.matmul(
                        psc[ki % 2][:, 0:128],
                        w1b[:, (o % 2) * 2048 + c * 128:
                            (o % 2) * 2048 + (c + 1) * 128],
                        cki[:, ki * 2048 + c * 128: ki * 2048 + (c + 1) * 128],
                        start=(c == 0), stop=(c == 15)) for c in range(16)]
                    s.op("te", builds, reads=(wk, f"cki{ki}"),
                         writes=(f"p2c{ki % 2}",))
                    s.op("se", lambda h, o=o, ki=ki, side=side: h.activation(
                        hid[:, ki * 2048 + o * 128: ki * 2048 + (o + 1) * 128],
                        psc[ki % 2][:, 0:128], AF.Relu,
                        bias=b1b[:, side * 16 + o: side * 16 + o + 1]),
                        reads=(f"p2c{ki % 2}", "b1b"), writes=(f"hid{ki}",))
            if side == 0 and kh == 0 and getattr(b, "dbg_hid", None) is not None:
                s.barrier(engs=("sy",))
                s.op("sy", lambda h: h.dma_start(out=b.dbg_hid, in_=hid[:]),
                     reads=("hid0", "hid1", "hid2", "hid3"), writes=(),
                     tag="dump")
            for ki in range(4):
                k = 4 * kh + ki
                builds = [lambda h, o=o, ki=ki, side=side: h.matmul(
                    psc2[:, 0:128],
                    w2b[:, side * 2048 + o * 128: side * 2048 + (o + 1) * 128],
                    hid[:, ki * 2048 + o * 128: ki * 2048 + (o + 1) * 128],
                    start=(o == 0), stop=(o == 15)) for o in range(16)]
                s.op("te", builds, reads=("w2b", f"hid{ki}"), writes=("p2out",))
                if side == 0:
                    s.op("ve", lambda h, k=k: h.tensor_scalar(
                        b.ckf[:, k * NCK: k * NCK + 128], psc2[:, 0:128],
                        b2b[:, 0:1], None, ALU.add),
                        reads=("p2out", "b2b"), writes=("ckf",), sreads=("b2b",))
                else:
                    s.op("ve", lambda h: h.tensor_scalar(
                        cvfb[:], psc2[:, 0:128],
                        b2b[:, 1:2], None, ALU.add),
                        reads=("p2out", "b2b"), writes=("cvfb",), sreads=("b2b",))
                    s.op("te", lambda h: h.transpose(
                        pst[0][:, 0:128], cvfb[:], b.ident[:]),
                        reads=("cvfb", "ident"), writes=("p2t0",))
                    s.op("ve", lambda h, k=k: h.tensor_copy(
                        b.cvt1[:, k * 128:(k + 1) * 128], pst[0][:, 0:128]),
                        reads=("p2t0",), writes=("cvt1",))

    _rope_inplace(b, b.kf, "kf", S, cosk, "cosk", sinkb, psr, f32a, f32b)
    _rope_inplace(b, b.kwin, "kwin", NWTOK, cosw, "cosw", sinw, psr, f32a, f32b)

    for buf, bufkey, ncol, ntile in ((b.vf, "vf", S, 16),
                                     (b.vwin, "vwin", NWTOK, 9)):
        for k in range(KVH):
            for u in range(ntile):
                c0 = k * ncol + u * 128
                pk = f"p2t{u % 2}"
                s.op("te", lambda h, c0=c0, u=u, buf=buf: h.transpose(
                    pst[u % 2][:, 0:128], buf[:, c0:c0 + 128], b.ident[:]),
                    reads=(bufkey, "ident"), writes=(pk,))
                s.op("ve", lambda h, c0=c0, u=u, buf=buf: h.tensor_copy(
                    buf[:, c0:c0 + 128], pst[u % 2][:, 0:128]),
                    reads=(pk,), writes=(bufkey,))


def _p3(b, ps):
    nc, s, di = b.nc, b.s, b.di
    st = b.stat
    sb = nc.sbuf_tensor
    qb = ps.enter_context(sb("qb", [128, G * QH], BF16))
    rq = ps.enter_context(sb("rq", [128, G * QH], BF16))
    pf = ps.enter_context(sb("pf", [128, G * 2 * 2048], BF16))
    pc = ps.enter_context(sb("pcb", [128, G * NCK], BF16))
    psl = ps.enter_context(sb("pslb", [128, G * 192], BF16))
    smb = ps.enter_context(sb("smb", [128, G * NCK], F32))
    ssb = ps.enter_context(sb("ssb", [128, 192], F32))
    imps = ps.enter_context(sb("imps", [128, 128], F32))
    ya = ps.enter_context(sb("ya", [128, 128], F32))
    yb = ps.enter_context(sb("yb", [128, 128], F32))
    geb = ps.enter_context(sb("geb", [128, 128], F32))
    fm = ps.enter_context(sb("fm", [128, 128], BF16))
    fmq = ps.enter_context(sb("fmq", [32, 512], BF16))
    zdump = ps.enter_context(sb("zdump", [128, 128], BF16))
    ptA = ps.enter_context(sb("ptA", [128, 2 * 512], BF16))
    ptM = ps.enter_context(sb("ptM", [1, 512], BF16))
    ptB = ps.enter_context(sb("ptB", [64, 512], BF16))
    asp = ps.enter_context(sb("asp", [128, 2 * 512], BF16))
    cosq = ps.enter_context(sb("cosqb", [128, QH], F32))
    sinq = ps.enter_context(sb("sinqb", [128, QH], F32))
    cmb = ps.enter_context(sb("cmb", [128, NT * NCK], F32))
    smvb = ps.enter_context(sb("smvb", [128, NT * 192], F32))
    t2t = ps.enter_context(sb("t2tb", [128, 3200], BF16))
    rpa = ps.enter_context(sb("rpa3", [128, 512], F32))
    rpb = ps.enter_context(sb("rpb3", [128, 512], F32))
    psC = ps.enter_context(nc.psum_tensor("p3psC", [128, 2048], F32))
    psO = ps.enter_context(nc.psum_tensor("p3psO", [128, 512], F32))
    psT = [ps.enter_context(nc.psum_tensor(f"p3psT{i}", [128, 1024], BF16))
           for i in range(2)]
    psR = ps.enter_context(nc.psum_tensor("p3psR", [128, 512], F32))

    for bld, wkey in (
        (lambda h: h.dma_start(out=cosq[:], in_=di["cosq"]), "cosq"),
        (lambda h: h.dma_start(out=sinq[:], in_=di["sinq"]), "cosq"),
        (lambda h: h.dma_start(out=cmb[:].rearrange("p (t n) -> p t n", n=NCK),
                               in_=di["CM"].rearrange("t p n -> p t n")), "cm"),
        (lambda h: h.dma_start(out=smvb[:].rearrange("p (t n) -> p t n", n=192),
                               in_=di["SM"].rearrange("t p n -> p t n")), "smv"),
        (lambda h: h.dma_start(out=t2t[:], in_=di["T2"]), "t2t"),
    ):
        s.op("sy", bld, writes=(wkey,), tag="p3tab")

    PSCK = ("psCg0", "psCg1", "psCg2", "psCg3")
    for kvh in range(KVH):
        s.op("sy", lambda h, kvh=kvh: h.dma_start(out=qb[:], in_=b.q_scr[kvh]),
             reads=(), writes=("qld",), tag="qld")
        for g in range(G):
            for uu in range(2):
                c0 = g * QH + uu * 512
                t0 = uu * 512
                pk = "ropeps0" if uu == 0 else "ropeps1"
                pp = psR[:, 0:512]
                s.op("te", lambda h, c0=c0: h.matmul(
                    psR[:, 0:512], b.psw[:], qb[:, c0:c0 + 512],
                    start=True, stop=True),
                    reads=("qld", "psw"), writes=("psR",))
                s.op("se", lambda h, c0=c0: h.activation(
                    rpa[:], qb[:, c0:c0 + 512], AF.Copy),
                    reads=("qld",), writes=("ropea",))
                s.op("ve", lambda h, t0=t0: h.tensor_tensor(
                    rpa[:], rpa[:], cosq[:, t0:t0 + 512], ALU.mult),
                    reads=("ropea", "cosq"), writes=("ropea",))
                s.op("ve", lambda h, t0=t0: h.tensor_tensor(
                    rpb[:], psR[:, 0:512], sinq[:, t0:t0 + 512], ALU.mult),
                    reads=("psR", "cosq"), writes=("ropeb",))
                s.op("ve", lambda h, c0=c0: h.tensor_tensor(
                    rq[:, c0:c0 + 512], rpa[:], rpb[:], ALU.add),
                    reads=("ropea", "ropeb"), writes=("rq",))

        PSCKALL = ("psCg0", "psCg1", "psCg2", "psCg3")

        def pfbase(g, qt):
            return (g * 2 + qt % 2) * 2048

        def emit_comp(qt):
            for g in range(G):
                s.op("te", lambda h, kvh=kvh, g=g, qt=qt: h.matmul(
                    psC[:, g * 512: g * 512 + NCK],
                    qb[:, g * QH + qt * 128: g * QH + (qt + 1) * 128],
                    b.ckf[:, kvh * NCK: (kvh + 1) * NCK],
                    start=True, stop=True),
                    reads=("qld", "ckf"), writes=(f"psCg{g}",))
                s.op("ve", lambda h, g=g, qt=qt: h.tensor_tensor(
                    smb[:, g * NCK:(g + 1) * NCK],
                    psC[:, g * 512: g * 512 + NCK],
                    cmb[:].rearrange("p (t n) -> p t n", n=NCK)[:, qt, :],
                    ALU.add),
                    reads=(f"psCg{g}", "cm"), writes=(f"smb{g}",))
                s.op("ve", lambda h, g=g: h.reduce_max(
                    st[:, 16 + g:17 + g], smb[:, g * NCK:(g + 1) * NCK],
                    axis=AX, negate=True),
                    reads=(f"smb{g}",), writes=(f"nm{g}",), hazard=True)
                s.op("se", lambda h, g=g: h.activation(
                    pc[:, g * NCK:(g + 1) * NCK], smb[:, g * NCK:(g + 1) * NCK],
                    AF.Exp, bias=st[:, 16 + g:17 + g],
                    accum_out=st[:, 20 + g:21 + g]),
                    reads=(f"smb{g}", f"nm{g}"), writes=(f"pc{g}", f"csum{g}"))
                s.op("ve", lambda h, g=g: h.reciprocal(
                    st[:, 24 + g:25 + g], st[:, 20 + g:21 + g]),
                    reads=(f"csum{g}",), writes=(f"rcp{g}",), hazard=True)
                gcol = qt * 96 + (4 * kvh + g) * 3
                s.op("ve", lambda h, g=g, gcol=gcol: h.tensor_tensor(
                    st[:, 28 + g:29 + g], st[:, 24 + g:25 + g],
                    b.gates[:, gcol:gcol + 1], ALU.mult),
                    reads=(f"rcp{g}", "gates"), writes=(f"rs{g}",))
                s.op("ve", lambda h, g=g: h.tensor_scalar(
                    pc[:, g * NCK:(g + 1) * NCK], pc[:, g * NCK:(g + 1) * NCK],
                    st[:, 28 + g:29 + g], None, ALU.mult),
                    reads=(f"pc{g}",), sreads=(f"rs{g}",), writes=(f"pc{g}",))

        def emit_slide(qt):
            for g in range(G):
                s.op("te", lambda h, kvh=kvh, g=g, qt=qt: h.matmul(
                    psC[:, g * 512: g * 512 + 192],
                    rq[:, g * QH + qt * 128: g * QH + (qt + 1) * 128],
                    b.kwin[:, kvh * NWTOK + qt * 128:
                           kvh * NWTOK + qt * 128 + 192],
                    start=True, stop=True),
                    reads=("rq", "kwin"), writes=(f"psCg{g}",))
                s.op("ve", lambda h, g=g, qt=qt: h.tensor_tensor(
                    ssb[:], psC[:, g * 512: g * 512 + 192],
                    smvb[:].rearrange("p (t n) -> p t n", n=192)[:, qt, :],
                    ALU.add),
                    reads=(f"psCg{g}", "smv"), writes=("ssb",))
                s.op("ve", lambda h: h.reduce_max(
                    st[:, 48:49], ssb[:], axis=AX, negate=True),
                    reads=("ssb",), writes=("snm",), hazard=True)
                s.op("se", lambda h, g=g: h.activation(
                    psl[:, g * 192:(g + 1) * 192], ssb[:], AF.Exp,
                    bias=st[:, 48:49], accum_out=st[:, 49:50]),
                    reads=("ssb", "snm"), writes=(f"psl{g}", "ssum"))
                s.op("ve", lambda h: h.reciprocal(st[:, 50:51], st[:, 49:50]),
                     reads=("ssum",), writes=("srcp",), hazard=True)
                gcol = qt * 96 + (4 * kvh + g) * 3 + 2
                s.op("ve", lambda h, gcol=gcol: h.tensor_tensor(
                    st[:, 51:52], st[:, 50:51], b.gates[:, gcol:gcol + 1],
                    ALU.mult),
                    reads=("srcp", "gates"), writes=("srs",))
                s.op("ve", lambda h, g=g: h.tensor_scalar(
                    psl[:, g * 192:(g + 1) * 192],
                    psl[:, g * 192:(g + 1) * 192],
                    st[:, 51:52], None, ALU.mult),
                    reads=(f"psl{g}",), sreads=("srs",), writes=(f"psl{g}",))

        def emit_qkpre(qt):
            Lt = 1152 + 128 * qt
            nch = (Lt + 511) // 512
            for c in range(nch):
                N = min(512, Lt - 512 * c)
                s.op("te", [lambda h, qt=qt, c=c, N=N, kvh=kvh: h.matmul(
                    psC[:, c * 512: c * 512 + N],
                    rq[:, qt * 128: (qt + 1) * 128],
                    b.kf[:, kvh * S + c * 512: kvh * S + c * 512 + N],
                    start=True, stop=False)],
                    reads=("rq", "kf"), writes=(f"psCg{c}",))

        def emit_sel(qt):
                s.op("ve", lambda h: h.tensor_tensor(
                    ya[:], smb[:, 0:128], smb[:, NCK:NCK + 128], ALU.add),
                    reads=("smb0", "smb1"), writes=("ya",))
                s.op("ve", lambda h: h.tensor_tensor(
                    yb[:], smb[:, 2 * NCK:2 * NCK + 128],
                    smb[:, 3 * NCK:3 * NCK + 128], ALU.add),
                    reads=("smb2", "smb3"), writes=("yb",))
                s.op("ve", lambda h: h.tensor_tensor(imps[:], ya[:], yb[:], ALU.add),
                     reads=("ya", "yb"), writes=("imps",))
                s.op("ve", lambda h: h.reduce_max(st[:, 32:33], imps[:], axis=AX),
                     reads=("imps",), writes=("mx",), hazard=True)
                s.op("ve", lambda h: h.tensor_scalar(
                    st[:, 33:34], st[:, 32:33], 0.25, -1000.0, ALU.mult, ALU.max),
                    reads=("mx",), sreads=("mx",), writes=("M",))
                s.op("ve", lambda h: h.tensor_scalar(
                    st[:, 34:35], st[:, 33:34], -1.0, None, ALU.mult),
                    reads=("M",), sreads=("M",), writes=("nM",))
                s.op("se", lambda h: h.activation(
                    zdump[:], imps[:], AF.Exp, bias=st[:, 34:35], scale=0.25,
                    accum_out=st[:, 35:36]),
                    reads=("imps", "nM"), writes=("zs", "zdumpb"))
                s.op("se", lambda h: h.activation(
                    st[:, 36:37], st[:, 33:34], AF.Exp, bias=st[:, 60:61],
                    scale=-1.0),
                    reads=("M", "c1000"), writes=("e2",))
                s.op("ve", lambda h: h.tensor_tensor(
                    st[:, 37:38], st[:, 35:36], st[:, 36:37], ALU.add),
                    reads=("zs", "e2"), writes=("z",))
                s.op("se", lambda h: h.activation(st[:, 38:39], st[:, 37:38], AF.Ln),
                     reads=("z",), writes=("lnz",))
                s.op("ve", lambda h: h.tensor_tensor(
                    st[:, 39:40], st[:, 33:34], st[:, 38:39], ALU.add),
                    reads=("M", "lnz"), writes=("mlz",))
                s.op("ve", lambda h: h.tensor_scalar(
                    st[:, 40:41], st[:, 39:40], LN1E10, 4.0, ALU.add, ALU.mult),
                    reads=("mlz",), sreads=("mlz",), writes=("t2c",))
                cur, nxt = imps, ya
                curk, nxtk = "imps", "ya"
                for it in range(3):
                    s.op("ve", lambda h, cur=cur: h.reduce_max(
                        st[:, 41:42], cur[:], axis=AX),
                        reads=(curk,), writes=("mi",), hazard=True)
                    s.op("ve", lambda h, cur=cur: h.tensor_scalar(
                        geb[:], cur[:], st[:, 41:42], None, ALU.is_ge),
                        reads=(curk,), sreads=("mi",), writes=("geb",))
                    s.op("ve", lambda h, cur=cur, nxt=nxt: h.scalar_tensor_tensor(
                        nxt[:], geb[:], -1e33, cur[:], ALU.mult, ALU.add),
                        reads=("geb", curk), writes=(nxtk,))
                    cur, nxt = nxt, (yb if nxt is ya else ya)
                    curk, nxtk = nxtk, ("yb" if nxtk == "ya" else "ya")
                s.op("ve", lambda h, cur=cur: h.reduce_max(
                    st[:, 42:43], cur[:], axis=AX),
                    reads=(curk,), writes=("q4",), hazard=True)
                s.op("ve", lambda h: h.tensor_scalar(
                    geb[:], imps[:], st[:, 42:43], None, ALU.is_ge),
                    reads=("imps",), sreads=("q4",), writes=("geb",))
                s.op("ve", lambda h: h.tensor_scalar(
                    ya[:], imps[:], st[:, 40:41], None, ALU.is_gt),
                    reads=("imps",), sreads=("t2c",), writes=("ya",))
                s.op("ve", lambda h: h.tensor_tensor(yb[:], geb[:], ya[:], ALU.mult),
                     reads=("geb", "ya"), writes=("yb",))
                s.op("ve", lambda h: h.tensor_scalar(
                    fm[:], yb[:], 1.0, V30, ALU.subtract, ALU.mult),
                    reads=("yb",), writes=("fm",))
                builds = [lambda h, c=c: h.transpose(
                    psT[0][0:32, c * 128:(c + 1) * 128], fm[:, c * 32:(c + 1) * 32],
                    b.ident[:]) for c in range(4)]
                s.op("te", builds, reads=("fm", "ident"), writes=("psT0",))
                s.op("ve", lambda h: h.tensor_copy(fmq[:], psT[0][0:32, 0:512]),
                     reads=("psT0",), writes=("fmq",))

        def emit_fine_g(qt, g):
            Lt = 1152 + 128 * qt
            nch = (Lt + 511) // 512
            STT = 1024 - 128 * qt
            pb = pfbase(g, qt)
            for c in range(nch):
                N = min(512, Lt - 512 * c)
                builds = []
                if g > 0:
                    builds.append(
                        lambda h, g=g, qt=qt, c=c, N=N, kvh=kvh: h.matmul(
                            psC[:, c * 512: c * 512 + N],
                            rq[:, g * QH + qt * 128: g * QH + (qt + 1) * 128],
                            b.kf[:, kvh * S + c * 512: kvh * S + c * 512 + N],
                            start=True, stop=False))
                builds += [
                    lambda h, g=g, c=c, N=N: h.matmul(
                        psC[:, c * 512: c * 512 + N],
                        fmq[:, c * 128:(c + 1) * 128], b.el[:, 0:N],
                        start=False, stop=False),
                    lambda h, c=c, N=N, STT=STT: h.matmul(
                        psC[:, c * 512: c * 512 + N], b.dl[:],
                        t2t[:, STT + 512 * c: STT + 512 * c + N],
                        start=False, stop=True),
                ]
                s.op("te", builds, reads=("rq", "kf", "fmq", "el", "dl", "t2t"),
                     writes=(f"psCg{c}",))
                s.op("ve", lambda h, c=c, N=N: h.reduce_max(
                    st[:, 52 + c:53 + c], psC[:, c * 512: c * 512 + N], axis=AX),
                    reads=(f"psCg{c}",), writes=(f"pm{c}",), hazard=True)
            s.op("ve", lambda h, nch=nch: h.reduce_max(
                st[:, 44:45], st[:, 52:52 + nch], axis=AX, negate=True),
                reads=tuple(f"pm{c}" for c in range(nch)), writes=("nmf",),
                hazard=True)
            for c in range(nch):
                N = min(512, Lt - 512 * c)
                s.op("se", lambda h, c=c, N=N, pb=pb, g=g, qt=qt: h.activation(
                    pf[:, pb + c * 512: pb + c * 512 + N],
                    psC[:, c * 512: c * 512 + N], AF.Exp,
                    bias=st[:, 44:45], accum_out=st[:, 56 + c:57 + c]),
                    reads=(f"psCg{c}", "nmf"),
                    writes=(f"pf{g}{qt % 2}", f"es{c}"))
            s.op("ve", lambda h, nch=nch: h.reduce_sum(
                st[:, 45:46], st[:, 56:56 + nch], axis=AX),
                reads=tuple(f"es{c}" for c in range(nch)), writes=("fsum",),
                hazard=True)
            s.op("ve", lambda h: h.reciprocal(st[:, 46:47], st[:, 45:46]),
                 reads=("fsum",), writes=("frcp",), hazard=True)
            gcol = qt * 96 + (4 * kvh + g) * 3 + 1
            s.op("ve", lambda h, gcol=gcol: h.tensor_tensor(
                st[:, 47:48], st[:, 46:47], b.gates[:, gcol:gcol + 1],
                ALU.mult),
                reads=("frcp", "gates"), writes=("frs",))
            s.op("ve", lambda h, g=g, Lt=Lt, pb=pb: h.tensor_scalar(
                pf[:, pb: pb + Lt], pf[:, pb: pb + Lt],
                st[:, 47:48], None, ALU.mult),
                reads=(f"pf{g}{qt % 2}",), sreads=("frs",),
                writes=(f"pf{g}{qt % 2}",))

        def emit_fineav_chunks(qt, us):
            par = qt % 2
            pfk = tuple(f"pf{g}{par}" for g in range(G))
            for u in us:
                builds = [lambda h, g=g, u=u, qt=qt: h.transpose(
                    psT[u % 2][:, g * 128:(g + 1) * 128],
                    pf[:, pfbase(g, qt) + u * 128: pfbase(g, qt) + (u + 1) * 128],
                    b.ident[:]) for g in range(G)]
                s.op("te", builds, reads=pfk + ("ident",),
                     writes=(f"psT{u % 2}",))
                if u % 2 == 0:
                    s.op("ve", lambda h, u=u: h.tensor_copy(
                        ptA[:, (u % 2) * 512:(u % 2) * 512 + 512],
                        psT[u % 2][:, 0:512]),
                        reads=(f"psT{u % 2}",), writes=(f"ptA{u % 2}",))
                else:
                    s.op("se", lambda h, u=u: h.activation(
                        ptA[:, (u % 2) * 512:(u % 2) * 512 + 512],
                        psT[u % 2][:, 0:512], AF.Copy),
                        reads=(f"psT{u % 2}",), writes=(f"ptA{u % 2}",))
                s.op("te", lambda h, kvh=kvh, u=u: h.matmul(
                    psO[:], b.vf[:, kvh * S + u * 128: kvh * S + (u + 1) * 128],
                    ptA[:, (u % 2) * 512:(u % 2) * 512 + 512],
                    start=False, stop=False),
                    reads=(f"ptA{u % 2}", "vf"), writes=("psO",))

        def emit_slide_av(qt):
            builds = [lambda h, g=g: h.transpose(
                psT[0][:, g * 128:(g + 1) * 128],
                psl[:, g * 192: g * 192 + 128], b.ident[:]) for g in range(G)]
            s.op("te", builds, reads=("psl0", "psl1", "psl2", "psl3", "ident"),
                 writes=("psT0",))
            builds = [lambda h, g=g: h.transpose(
                psT[1][0:64, g * 128:(g + 1) * 128],
                psl[:, g * 192 + 128: g * 192 + 192], b.ident[:])
                for g in range(G)]
            s.op("te", builds, reads=("psl0", "psl1", "psl2", "psl3", "ident"),
                 writes=("psT1",))
            s.op("ve", lambda h: h.tensor_copy(ptA[:, 0:512], psT[0][:, 0:512]),
                 reads=("psT0",), writes=("ptA0",))
            s.op("se", lambda h: h.activation(ptB[:], psT[1][0:64, 0:512],
                                              AF.Copy),
                 reads=("psT1",), writes=("ptB",))
            s.op("te", [lambda h, kvh=kvh, qt=qt: h.matmul(
                psO[:],
                b.vwin[:, kvh * NWTOK + qt * 128: kvh * NWTOK + (qt + 1) * 128],
                ptA[:, 0:512], start=False, stop=False),
                lambda h, kvh=kvh, qt=qt: h.matmul(
                psO[:],
                b.vwin[0:64, kvh * NWTOK + (qt + 1) * 128:
                       kvh * NWTOK + (qt + 2) * 128],
                ptB[:], start=False, stop=True)],
                reads=("ptA0", "ptB", "vwin"), writes=("psO",))
            s.op("ve", lambda h, qt=qt: h.tensor_copy(
                asp[:, (qt % 2) * 512:(qt % 2) * 512 + 512], psO[:]),
                reads=("psO",), writes=(f"asp{qt % 2}",))
            s.op("sy", lambda h, kvh=kvh, qt=qt: h.dma_start(
                out=b.a_scr[kvh].rearrange("p (g t) -> p g t", t=QH)
                [:, :, qt * 128:(qt + 1) * 128],
                in_=asp[:, (qt % 2) * 512:(qt % 2) * 512 + 512].rearrange(
                    "p (g t) -> p g t", t=128)),
                reads=(f"asp{qt % 2}",), writes=(), tag=f"asp{qt % 2}")

        def emit_compav(qt):
            builds = [lambda h, g=g: h.transpose(
                psT[0][:, g * 128:(g + 1) * 128], pc[:, g * NCK: g * NCK + 128],
                b.ident[:]) for g in range(G)]
            s.op("te", builds, reads=("pc0", "pc1", "pc2", "pc3", "ident"),
                 writes=("psT0",))
            builds = [lambda h, g=g: h.transpose(
                psT[1][0:1, g * 128:(g + 1) * 128],
                pc[:, g * NCK + 128: g * NCK + 129], b.ident[:])
                for g in range(G)]
            s.op("te", builds, reads=("pc0", "pc1", "pc2", "pc3", "ident"),
                 writes=("psT1",))
            s.op("ve", lambda h: h.tensor_copy(ptA[:, 0:512], psT[0][:, 0:512]),
                 reads=("psT0",), writes=("ptA0",))
            s.op("se", lambda h: h.activation(ptM[:], psT[1][0:1, 0:512],
                                              AF.Copy),
                 reads=("psT1",), writes=("ptM",))
            s.op("te", [lambda h, kvh=kvh: h.matmul(
                psO[:], b.cvt1[:, kvh * 128:(kvh + 1) * 128], ptA[:, 0:512],
                start=True, stop=False),
                lambda h, kvh=kvh: h.matmul(
                psO[:], b.cvt2[:, kvh * 128:(kvh + 1) * 128], ptM[:],
                start=False, stop=False)],
                reads=("ptA0", "ptM", "cvt1", "cvt2"), writes=("psO",))

        for qt in range(NT):
            Ltp = 1152 + 128 * (qt - 1)
            nup = Ltp // 128 if qt > 0 else 0
            emit_comp(qt)
            if qt > 0:
                emit_slide(qt - 1)
            emit_qkpre(qt)
            emit_sel(qt)
            cuts = [int(nup * f) for f in (0.4, 0.55, 0.7, 0.85)] + [nup]
            if qt > 0:
                emit_fineav_chunks(qt - 1, range(0, cuts[0]))
            emit_fine_g(qt, 0)
            if qt > 0:
                emit_fineav_chunks(qt - 1, range(cuts[0], cuts[1]))
            emit_fine_g(qt, 1)
            if qt > 0:
                emit_fineav_chunks(qt - 1, range(cuts[1], cuts[2]))
            emit_fine_g(qt, 2)
            if qt > 0:
                emit_fineav_chunks(qt - 1, range(cuts[2], cuts[3]))
            emit_fine_g(qt, 3)
            if qt > 0:
                emit_fineav_chunks(qt - 1, range(cuts[3], nup))
                emit_slide_av(qt - 1)
            emit_compav(qt)
        emit_slide(NT - 1)
        emit_fineav_chunks(NT - 1, range(0, (1152 + 128 * (NT - 1)) // 128))
        emit_slide_av(NT - 1)



def _p5(b, ps):
    nc, s, di = b.nc, b.s, b.di
    sb = nc.sbuf_tensor
    attn = ps.enter_context(sb("attnsb", [128, H * QH], BF16))
    wo = ps.enter_context(sb("wob", [128, 2 * 16 * 512], BF16))
    w2s = ps.enter_context(sb("w2sb", [128, 4 * 512], BF16))
    gs = ps.enter_context(sb("gsb", [128, 4 * QH], BF16))
    Fb = ps.enter_context(sb("Fb", [128, 8 * 512], F32))
    xb = ps.enter_context(sb("xb5", [128, 2 * 512], F32))
    ob = ps.enter_context(sb("ob5", [128, 2 * 512], F32))
    rc = ps.enter_context(sb("rc5", [128, 512], F32))
    B2s = ps.enter_context(sb("B2sb", [16, D], BF16))
    psD = [ps.enter_context(nc.psum_tensor(f"p5psD{i}", [128, 512], F32))
           for i in range(8)]

    s.op("gp", lambda h: h.dma_start(out=B2s[:], in_=di["B2T"]),
         writes=("B2s",), tag="p5w")
    for kvh in range(KVH):
        s.op("sy", lambda h, kvh=kvh: h.dma_start(
            out=attn[:, kvh * G * QH:(kvh + 1) * G * QH], in_=b.a_scr[kvh]),
            writes=("attn",), tag="p5attn")
    for o in range(8):
        for hh in range(NO):
            wk = f"w2l{hh % 4}"
            gk = f"gl{hh % 4}"
            s.op("gp", lambda h, o=o, hh=hh: h.dma_start(
                out=w2s[:, (hh % 4) * 512:(hh % 4) * 512 + 512],
                in_=di["w2T"][o][hh]),
                writes=(wk,), tag=wk)
            s.op("sy", lambda h, hh=hh: h.dma_start(
                out=gs[:, (hh % 4) * QH:(hh % 4) * QH + QH], in_=b.g_scr[hh]),
                writes=(gk,), tag=gk)
            builds = [lambda h, hh=hh, tt=tt: h.matmul(
                psD[tt][:], gs[:, (hh % 4) * QH + tt * 128:
                               (hh % 4) * QH + (tt + 1) * 128],
                w2s[:, (hh % 4) * 512:(hh % 4) * 512 + 512],
                start=(hh == 0), stop=False) for tt in range(NT)]
            s.op("te", builds, reads=(wk, gk),
                 writes=tuple(f"psD{tt}" for tt in range(NT)))
        builds = [lambda h, o=o, tt=tt: h.matmul(
            psD[tt][:], b.t2b[:, tt * 128:(tt + 1) * 128],
            B2s[:, o * 512:(o + 1) * 512], start=False, stop=True)
            for tt in range(NT)]
        s.op("te", builds, reads=("t2b", "B2s"),
             writes=tuple(f"psD{tt}" for tt in range(NT)))
        for tt in range(NT):
            s.op("se", lambda h, tt=tt: h.activation(
                Fb[:, tt * 512:(tt + 1) * 512], psD[tt][:], AF.Silu),
                reads=(f"psD{tt}",), writes=(f"F{tt}",))
        for ah in range(2):
            wk = f"wo{ah}"
            s.op("gp", lambda h, o=o, ah=ah: h.dma_start(
                out=wo[:, ah * 8192: ah * 8192 + 8192].rearrange(
                    "p (a n) -> p a n", n=512),
                in_=di["woT"][o][ah * 16:(ah + 1) * 16].rearrange(
                    "a p n -> p a n")),
                writes=(wk,), tag=wk)
            for tt in range(NT):
                builds = [lambda h, a=a, ah=ah, tt=tt: h.matmul(
                    psD[tt][:],
                    attn[:, (ah * 16 + a) * QH + tt * 128:
                         (ah * 16 + a) * QH + (tt + 1) * 128],
                    wo[:, ah * 8192 + a * 512: ah * 8192 + (a + 1) * 512],
                    start=(ah == 0 and a == 0), stop=(ah == 1 and a == 15))
                    for a in range(16)]
                s.op("te", builds, reads=(wk, "attn", f"F{tt}"),
                     writes=(f"psD{tt}",))
        for tt in range(NT):
            xk = f"x5{tt % 2}"
            s.op("sy", lambda h, o=o, tt=tt: h.dma_start(
                out=xb[:, (tt % 2) * 512:(tt % 2) * 512 + 512],
                in_=di["xo"][tt * 128:(tt + 1) * 128, o * 512:(o + 1) * 512]),
                writes=(xk,), tag=xk)
            s.op("ve", lambda h, tt=tt: h.tensor_tensor(
                rc[:], psD[tt][:], Fb[:, tt * 512:(tt + 1) * 512], ALU.mult),
                reads=(f"psD{tt}", f"F{tt}"), writes=("rc",))
            s.op("ve", lambda h, tt=tt: h.tensor_tensor(
                ob[:, (tt % 2) * 512:(tt % 2) * 512 + 512], rc[:],
                xb[:, (tt % 2) * 512:(tt % 2) * 512 + 512], ALU.add),
                reads=("rc", xk), writes=(f"ob{tt % 2}",))
            s.op("sy", lambda h, o=o, tt=tt: h.dma_start(
                out=b.out[tt * 128:(tt + 1) * 128, o * 512:(o + 1) * 512],
                in_=ob[:, (tt % 2) * 512:(tt % 2) * 512 + 512]),
                reads=(f"ob{tt % 2}",), writes=(), tag=f"o5{tt % 2}")


def _emit_debug(b):
    s = b.s
    for name, ap in b.dbg.items():
        src = getattr(b, "dbgsrc", {}).get(name)
        if src is None:
            continue
        sap, reads = src
        s.barrier(engs=("sy",))
        s.op("sy", lambda h, ap=ap, sap=sap: h.dma_start(out=ap, in_=sap),
             reads=reads, writes=())



def kernel(**inputs):
    in_maps = stage_inputs(inputs)
    nc = build_program()
    res = run_bass_kernel_spmd(nc, in_maps, core_ids=list(range(NCORES)))
    out = np.empty((B, S, D), dtype=np.float32)
    for core in range(NCORES):
        bb, h = core // 2, core % 2
        out[bb, h * QH:(h + 1) * QH, :] = res.results[core]["out"]
    return out


# version 09
# speedup vs baseline: 1.1153x; 1.0572x over previous
import contextlib

import numpy as np
import ml_dtypes

import concourse.bass as bass
import concourse.mybir as mybir
from concourse.bass_utils import run_bass_kernel_spmd

F32 = mybir.dt.float32
BF16 = mybir.dt.bfloat16
AF = mybir.ActivationFunctionType
ALU = mybir.AluOpType
AX = mybir.AxisListType.X
BFDT = ml_dtypes.bfloat16

B, S, D = 4, 2048, 4096
H, KVH, G, DH = 32, 8, 4, 128
HID = 11008
NO = HID // 128
CB = 16
NBLK = S // CB
NCK = NBLK + 1
WIN = 64
NWTOK = 1152
QH = 1024
NT = QH // 128
SCALE = DH ** -0.5
NEG = float(np.float32(BFDT(-1e30)))
V30 = -NEG
LN1E10 = float(np.log(1e-10))
LORA_SCALE = 2.0
NCORES = 8



def _tables_for_core(h):
    qoff = QH * h
    f32 = np.float32
    inv = 1.0 / 10000.0 ** (np.arange(0, DH, 2, dtype=np.float64) / DH)

    def rope_tabs(positions):
        ang = positions[None, :] * inv[:, None]
        cosf = np.repeat(np.cos(ang), 2, axis=0).astype(f32)
        sinf = np.repeat(np.sin(ang), 2, axis=0).astype(f32)
        sinf[0::2, :] *= -1.0
        return np.ascontiguousarray(cosf), np.ascontiguousarray(sinf)

    cosk, sink = rope_tabs(np.arange(S, dtype=np.float64))
    cosq, sinq = rope_tabs(np.arange(qoff, qoff + QH, dtype=np.float64))
    cosw, sinw = rope_tabs(np.arange(qoff - WIN, qoff - WIN + NWTOK,
                                     dtype=np.float64))

    CM = np.zeros((NT, 128, NCK), dtype=f32)
    qpos = qoff + 128 * np.arange(NT)[:, None] + np.arange(128)[None, :]
    ends = 16 * np.arange(NBLK) + 15
    CM[:, :, :NBLK] = np.where(ends[None, None, :] < qpos[:, :, None], 0.0, NEG)

    SM = np.full((NT, 128, 192), NEG, dtype=f32)
    r = np.arange(128)[None, :, None]
    c = np.arange(192)[None, None, :]
    keypos = (qoff + 128 * np.arange(NT))[:, None, None] - WIN + c
    SM[(c >= r) & (c <= r + WIN) & (keypos >= 0)] = 0.0

    T2 = np.zeros((128, 3200), dtype=BFDT)
    for p in range(128):
        T2[p, qoff + 1024 + p] = BFDT(1.0)
    return dict(cosk=cosk, sink=sink, cosq=cosq, sinq=sinq, cosw=cosw, sinw=sinw,
                CM=np.ascontiguousarray(CM), SM=np.ascontiguousarray(SM), T2=T2)


def _shared_consts():
    ident = np.eye(128, dtype=BFDT)
    psw = np.zeros((128, 128), dtype=BFDT)
    for k in range(128):
        psw[k, k ^ 1] = BFDT(1.0)
    onesb = np.ones((1, 128), dtype=BFDT)
    EL = np.zeros((32, 512), dtype=BFDT)
    for bb in range(32):
        EL[bb, bb * 16:(bb + 1) * 16] = BFDT(1.0)
    DL = np.zeros((128, 128), dtype=BFDT)
    v30b = BFDT(V30)
    p = np.arange(128)
    rr = np.arange(128)
    m = (rr[None, :] // 16 == p[:, None] // 16) & (rr[None, :] % 16 >= p[:, None] % 16)
    DL[m] = v30b
    return dict(ident=ident, psw=psw, onesb=onesb, EL=EL, DL=DL)


def _stage_weights(I):
    f32 = np.float32
    ac = np.ascontiguousarray
    w = {}
    w["wqkvT"] = ac(I["L0_attn_w_qkv"].T.reshape(D, 48, 128).transpose(1, 0, 2))
    w["woT"] = ac(I["L0_attn_w_o"].T.reshape(32, 128, 8, 512).transpose(2, 0, 1, 3))
    w["w1T"] = ac(I["L0_ffn_w1"].T.reshape(D, NO, 128).transpose(1, 0, 2))
    w["w3T"] = ac(I["L0_ffn_w3"].T.reshape(D, NO, 128).transpose(1, 0, 2))
    w["w2T"] = ac(I["L0_ffn_w2"].T.reshape(NO, 128, 8, 512).transpose(2, 0, 1, 3))
    w["kw1T"] = ac(I["L0_attn_k_w1"].T.reshape(16, 128, 16, 128).transpose(2, 0, 1, 3))
    w["vw1T"] = ac(I["L0_attn_v_w1"].T.reshape(16, 128, 16, 128).transpose(2, 0, 1, 3))
    w["kw2T"] = ac(I["L0_attn_k_w2"].T.reshape(16, 128, 128))
    w["vw2T"] = ac(I["L0_attn_v_w2"].T.reshape(16, 128, 128))
    w["kb1"] = ac(I["L0_attn_k_b1"].reshape(16, 128).T.astype(f32))
    w["vb1"] = ac(I["L0_attn_v_b1"].reshape(16, 128).T.astype(f32))
    w["kb2"] = ac(I["L0_attn_k_b2"].reshape(128, 1).astype(f32))
    w["vb2"] = ac(I["L0_attn_v_b2"].reshape(128, 1).astype(f32))
    w["kposT"] = ac(I["L0_attn_k_pos"].transpose(2, 0, 1).reshape(128, 128))
    w["vposT"] = ac(I["L0_attn_v_pos"].transpose(2, 0, 1).reshape(128, 128))
    w["memk"] = ac(I["L0_attn_mem_kv"][0, :, 0, :].reshape(KVH, 128, 1))
    w["memv"] = ac(I["L0_attn_mem_kv"][1, :, 0, :])
    w["wgT"] = ac(I["L0_attn_w_gate"].T.reshape(32, 128, 96))
    w["bg"] = ac(I["L0_attn_b_gate"].reshape(1, 96))
    w["A1T"] = ac(I["L0_ffn_lora_A1"].T.reshape(32, 128, 16))
    w["A3T"] = ac(I["L0_ffn_lora_A3"].T.reshape(32, 128, 16))
    w["A2T"] = ac(I["L0_ffn_lora_A2"].T.reshape(NO, 128, 16))
    w["B1T"] = ac(I["L0_ffn_lora_B1"].T)
    w["B3T"] = ac(I["L0_ffn_lora_B3"].T)
    w["B2T"] = ac(I["L0_ffn_lora_B2"].T)
    return w


def stage_inputs(inputs):
    x = np.asarray(inputs["L0_attn_hidden_states"], dtype=np.float32)
    w = _stage_weights(inputs)
    consts = _shared_consts()
    tabs = [_tables_for_core(h) for h in (0, 1)]
    in_maps = []
    for core in range(NCORES):
        bb, h = core // 2, core % 2
        qoff = QH * h
        xw = np.zeros((NWTOK, D), dtype=np.float32)
        lo, hi = qoff - WIN, qoff - WIN + NWTOK
        slo, shi = max(lo, 0), min(hi, S)
        xw[slo - lo: shi - lo, :] = x[bb, slo:shi, :]
        m = dict(w)
        m.update(consts)
        m.update(tabs[h])
        m["xf"] = np.ascontiguousarray(x[bb])
        m["xo"] = np.ascontiguousarray(x[bb, qoff:qoff + QH])
        m["xw"] = xw
        in_maps.append(m)
    return in_maps



ENGS = ("te", "ve", "se", "gp", "sy")
DMA_ENGS = ("gp", "sy")


class Sched:

    def __init__(self):
        self.q = {e: [] for e in ENGS}
        self.cnt = {}
        for e in ENGS:
            self.cnt[e] = 0
        self.sem = {}
        self.tags = []
        self.buf = {}
        self.hwm = {e: {} for e in ENGS}

    def _need_to_waits(self, eng, need):
        out = []
        for s2, n in sorted(need.items()):
            if n > self.hwm[eng].get(s2, 0):
                self.hwm[eng][s2] = n
                out.append((s2, n))
        return out

    def _waits(self, eng, reads, writes, sreads):
        need = {}

        def add(sname, n):
            if sname.startswith("dma_"):
                n = self.cnt[sname]
            need[sname] = max(need.get(sname, 0), n)

        strict = eng in ("ve", "se")
        for key in list(reads) + list(sreads):
            ent = self.buf.get(key)
            if ent is None:
                continue
            we, wn, haz = ent["w"]
            if we is None:
                continue
            if we != eng or haz or strict or key in sreads:
                add(we, wn)
        for key in writes:
            ent = self.buf.get(key)
            if ent is None:
                continue
            we, wn, haz = ent["w"]
            if we is not None and (we != eng or strict):
                add(we, wn)
            for re_, rn in ent["r"]:
                if re_ != eng or strict:
                    add(re_, rn)
        return self._need_to_waits(eng, need)

    def op(self, eng, build, reads=(), writes=(), sreads=(), hazard=False,
           tag=None):
        builds = build if isinstance(build, (list, tuple)) else [build]
        is_dma = eng in DMA_ENGS
        if is_dma:
            assert tag is not None, "DMA op needs tag"
            sname = "dma_" + tag
            if sname not in self.cnt:
                self.cnt[sname] = 0
                self.tags.append(sname)
        else:
            sname = eng
        waits = self._waits(eng, reads, writes, sreads)
        self.cnt[sname] += 16 if is_dma else 1
        n = self.cnt[sname]

        def fn(h, _w=waits, _b=builds, _s=sname, _dma=is_dma):
            for s2, v in _w:
                h.wait_ge(self.sem[s2], v)
            ins = None
            for bb in _b:
                ins = bb(h)
            ins.then_inc(self.sem[_s], 16 if _dma else 1)
        self.q[eng].append(fn)
        for key in writes:
            self.buf[key] = {"w": (sname, n, hazard), "r": []}
        for key in reads:
            ent = self.buf.setdefault(key, {"w": (None, 0, False), "r": []})
            ent["r"].append((sname, n))
        return n

    def raw_wait(self, eng, s2, n=None):
        v = self.cnt[s2] if n is None else n
        if v > self.hwm[eng].get(s2, 0):
            self.hwm[eng][s2] = v
            self.q[eng].append(lambda h, _s2=s2, _v=v: h.wait_ge(self.sem[_s2], _v))

    def barrier(self, engs=ENGS):
        on = [e for e in ENGS if e not in DMA_ENGS] + self.tags
        for e in engs:
            for s2 in on:
                if s2 != e and self.cnt.get(s2, 0) > 0:
                    self.raw_wait(e, s2)



def _dram_inputs(nc):
    t = {}

    def di(name, shape, dtype=F32):
        t[name] = nc.dram_tensor(name, list(shape), dtype, kind="ExternalInput").ap()

    di("xf", (S, D)); di("xo", (QH, D)); di("xw", (NWTOK, D))
    di("wqkvT", (48, D, 128)); di("woT", (8, 32, 128, 512))
    di("w1T", (NO, D, 128)); di("w3T", (NO, D, 128)); di("w2T", (8, NO, 128, 512))
    di("kw1T", (16, 16, 128, 128)); di("vw1T", (16, 16, 128, 128))
    di("kw2T", (16, 128, 128)); di("vw2T", (16, 128, 128))
    di("kb1", (128, 16)); di("vb1", (128, 16))
    di("kb2", (128, 1)); di("vb2", (128, 1))
    di("kposT", (128, 128)); di("vposT", (128, 128))
    di("memk", (KVH, 128, 1)); di("memv", (8, 128))
    di("wgT", (32, 128, 96)); di("bg", (1, 96))
    di("A1T", (32, 128, 16)); di("A3T", (32, 128, 16)); di("A2T", (NO, 128, 16))
    di("B1T", (16, HID)); di("B3T", (16, HID)); di("B2T", (16, D))
    di("cosk", (128, S)); di("sink", (128, S))
    di("cosq", (128, QH)); di("sinq", (128, QH))
    di("cosw", (128, NWTOK)); di("sinw", (128, NWTOK))
    di("CM", (NT, 128, NCK)); di("SM", (NT, 128, 192))
    di("T2", (128, 3200), BF16)
    di("ident", (128, 128), BF16); di("psw", (128, 128), BF16)
    di("onesb", (1, 128), BF16); di("EL", (32, 512), BF16); di("DL", (128, 128), BF16)
    return t


class Bld:
    pass


def build_program(upto=9, debug=()):
    nc = bass.Bass("TRN2", target_bir_lowering=False, debug=False)
    b = Bld()
    b.nc = nc
    b.s = Sched()
    b.di = _dram_inputs(nc)
    b.out = nc.dram_tensor("out", [QH, D], F32, kind="ExternalOutput").ap()
    b.q_scr = nc.dram_tensor("q_scr", [KVH, 128, G * QH], BF16, kind="Internal").ap()
    b.g_scr = nc.dram_tensor("g_scr", [NO, 128, QH], BF16, kind="Internal").ap()
    b.a_scr = nc.dram_tensor("a_scr", [KVH, 128, G * QH], BF16, kind="Internal").ap()

    es = contextlib.ExitStack()
    with es:
        sbuf = nc.sbuf_tensor
        b.ckf = es.enter_context(sbuf("ckf", [128, KVH * NCK], BF16))
        b.cvt1 = es.enter_context(sbuf("cvt1", [128, KVH * 128], BF16))
        b.cvt2 = es.enter_context(sbuf("cvt2", [1, KVH * 128], BF16))
        b.gates = es.enter_context(sbuf("gates", [128, NT * 96], F32))
        b.t2b = es.enter_context(sbuf("t2b", [16, QH], BF16))
        b.ident = es.enter_context(sbuf("sb_idn", [128, 128], BF16))
        b.psw = es.enter_context(sbuf("sb_psw", [128, 128], BF16))
        b.onesb = es.enter_context(sbuf("sb_onesb", [1, 128], BF16))
        b.el = es.enter_context(sbuf("sb_el", [32, 512], BF16))
        b.dl = es.enter_context(sbuf("sb_dl", [128, 128], BF16))
        b.stat = es.enter_context(sbuf("stat", [128, 64], F32))

        b.dumps = dict(debug) if not isinstance(debug, dict) else debug
        _p_consts(b)
        with contextlib.ExitStack() as xs:
            b.xn = xs.enter_context(sbuf("xn", [128, 32 * QH], BF16))
            with contextlib.ExitStack() as ps:
                _p0(b, ps)
            if upto >= 1:
                b.s.barrier()
                with contextlib.ExitStack() as ps:
                    _p4a(b, ps)
            if upto >= 2:
                b.s.barrier()
                with contextlib.ExitStack() as ps:
                    _p4b(b, ps)
            _emit_dumps(b, only_scope="p04")
        if upto >= 3:
            b.s.barrier()
            with contextlib.ExitStack() as kvs:
                b.kf = kvs.enter_context(sbuf("kf", [128, KVH * S], BF16))
                b.vf = kvs.enter_context(sbuf("vf", [128, KVH * S], BF16))
                b.kwin = kvs.enter_context(sbuf("kwin", [128, KVH * NWTOK], BF16))
                b.vwin = kvs.enter_context(sbuf("vwin", [128, KVH * NWTOK], BF16))
                with contextlib.ExitStack() as ps:
                    _p1(b, ps)
                if upto >= 4:
                    b.s.barrier()
                    with contextlib.ExitStack() as ps:
                        _p2(b, ps)
                _emit_dumps(b, only_scope="p12")
                if upto >= 5:
                    b.s.barrier()
                    with contextlib.ExitStack() as ps:
                        _p3(b, ps)
                        _emit_dumps(b, only_scope="p3")
        if upto >= 6:
            b.s.barrier()
            with contextlib.ExitStack() as ps:
                _p5(b, ps)
        _emit_dumps(b, only_scope="dram")
        _replay(b)
    return nc


def _emit_dumps(b, only_scope):
    s = b.s
    todo = [(n, sp) for n, (sc, sp) in b.dumps.items() if sc == only_scope]
    if not todo:
        return
    s.barrier(engs=("sy",))
    for name, src_fn in todo:
        ap, shape, dt = src_fn(b)
        dst = b.nc.dram_tensor("dump_" + name, list(shape), dt,
                               kind="ExternalOutput").ap()
        s.op("sy", lambda h, ap=ap, dst=dst: h.dma_start(out=dst, in_=ap),
             reads=(), writes=(), tag="dump")


def _replay(b):
    s = b.s
    with contextlib.ExitStack() as es:
        for e in ENGS:
            s.sem[e] = es.enter_context(b.nc.semaphore(f"sem_{e}"))
        for t in s.tags:
            s.sem[t] = es.enter_context(b.nc.semaphore(t))
        block = es.enter_context(b.nc.Block())

        @block.tensor
        def _(h):
            for fn in s.q["te"]:
                fn(h)

        @block.vector
        def _(h):
            for fn in s.q["ve"]:
                fn(h)

        @block.scalar
        def _(h):
            for fn in s.q["se"]:
                fn(h)

        @block.gpsimd
        def _(h):
            for fn in s.q["gp"]:
                fn(h)

        @block.sync
        def _(h):
            for fn in s.q["sy"]:
                fn(h)
            for t in s.tags:
                h.wait_ge(s.sem[t], s.cnt[t])


def _p_consts(b):
    s, di = b.s, b.di
    for dst, src in ((b.ident, "ident"), (b.psw, "psw"), (b.onesb, "onesb"),
                     (b.el, "EL"), (b.dl, "DL")):
        s.op("sy", lambda h, d=dst, sr=src: h.dma_start(out=d[:], in_=di[sr]),
             writes=(src,), tag="consts")
    s.op("ve", lambda h: h.memset(b.stat[:, 60:61], -1000.0), writes=("c1000",))



def _ln_tiles(b, src_ap, ntile, sink_fn, xbuf, xnt):
    s, st = b.s, b.stat
    for i in range(ntile):
        sl = (i % 2) * D
        xk = f"lnx{i % 2}"
        s.op("sy", lambda h, i=i, sl=sl: h.dma_start(
            out=xbuf[:, sl:sl + D], in_=src_ap[i * 128:(i + 1) * 128, :]),
            reads=(), writes=(xk,), tag=xk)
        xs = xbuf[:, sl:sl + D]
        s.op("se", lambda h, xs=xs: h.activation(xnt[:], xs, AF.Square,
                                                 accum_out=st[:, 1:2]),
             reads=(xk,), writes=("xnt", "ss"), hazard=True)
        s.op("ve", lambda h, xs=xs: h.reduce_sum(st[:, 0:1], xs, axis=AX),
             reads=(xk,), writes=("s1",), hazard=True)
        s.op("ve", lambda h: h.tensor_scalar(st[:, 2:3], st[:, 0:1], 1.0 / D,
                                             None, ALU.mult),
             reads=("s1",), writes=("mean",))
        s.op("ve", lambda h: h.tensor_scalar(st[:, 4:5], st[:, 1:2], 1.0 / D,
                                             None, ALU.mult),
             reads=("ss",), writes=("ex2",))
        s.op("ve", lambda h: h.tensor_tensor(st[:, 3:4], st[:, 2:3], st[:, 2:3],
                                             ALU.mult),
             reads=("mean",), writes=("m2",))
        s.op("ve", lambda h: h.tensor_tensor(st[:, 5:6], st[:, 4:5], st[:, 3:4],
                                             ALU.subtract),
             reads=("ex2", "m2"), writes=("var",))
        s.op("ve", lambda h: h.tensor_scalar(st[:, 6:7], st[:, 5:6], 1e-5, None,
                                             ALU.add),
             reads=("var",), writes=("vpe",))
        s.op("se", lambda h: h.activation(st[:, 7:8], st[:, 6:7], AF.Sqrt),
             reads=("vpe",), writes=("std",))
        s.op("ve", lambda h: h.reciprocal(st[:, 8:9], st[:, 7:8]),
             reads=("std",), writes=("rstd",), hazard=True)
        s.op("ve", lambda h, xs=xs: h.tensor_scalar(xnt[:], xs, st[:, 2:3],
                                                    st[:, 8:9], ALU.subtract,
                                                    ALU.mult),
             reads=(xk, "mean", "rstd"), writes=("xnt",),
             sreads=("mean", "rstd"))
        sink_fn(i)


def _xn_transpose_sink(b, xnt, pst, dst, dst_stride, dstkey, base_col=0):
    s = b.s

    def sink(i):
        for jj in range(8):
            pk = f"pst{jj % 2}"
            builds = []
            for cc in range(4):
                c = jj * 4 + cc
                builds.append(lambda h, c=c, cc=cc, jj=jj: h.transpose(
                    pst[jj % 2][:, cc * 128:(cc + 1) * 128],
                    xnt[:, c * 128:(c + 1) * 128], b.ident[:]))
            s.op("te", builds, reads=("xnt", "ident"), writes=(pk,))
            dv = dst[:].rearrange("p (c t) -> p c t", t=dst_stride)
            s.op("ve", lambda h, jj=jj, i=i, dv=dv: h.tensor_copy(
                dv[:, jj * 4:(jj + 1) * 4,
                   base_col + i * 128: base_col + (i + 1) * 128],
                pst[jj % 2][:].rearrange("p (c t) -> p c t", t=128)),
                reads=(pk,), writes=(dstkey,))
    return sink



def _p0(b, ps):
    nc, s = b.nc, b.s
    xbuf = ps.enter_context(nc.sbuf_tensor("p0xb", [128, 2 * D], F32))
    xnt = ps.enter_context(nc.sbuf_tensor("p0xnt", [128, D], BF16))
    pst = [ps.enter_context(nc.psum_tensor(f"p0pst{i}", [128, 512], BF16))
           for i in range(2)]
    sink = _xn_transpose_sink(b, xnt, pst, b.xn, QH, "xndst")
    _ln_tiles(b, b.di["xo"], NT, sink, xbuf, xnt)



def _p4a(b, ps):
    nc, s, di = b.nc, b.s, b.di
    wq = ps.enter_context(nc.sbuf_tensor("wq", [128, 2 * D], BF16))
    qo = ps.enter_context(nc.sbuf_tensor("qo", [128, 2 * QH], BF16))
    wg = ps.enter_context(nc.sbuf_tensor("wg", [128, 32 * 96], BF16))
    bgb = ps.enter_context(nc.sbuf_tensor("bgb", [1, 96], BF16))
    psq = [ps.enter_context(nc.psum_tensor(f"psq{i}", [128, 512], F32))
           for i in range(2)]
    psg = ps.enter_context(nc.psum_tensor("psg", [128, 96], F32))

    s.op("gp", lambda h: h.dma_start(
        out=wg[:].rearrange("p (c n) -> p c n", n=96),
        in_=di["wgT"].rearrange("c p n -> p c n")),
        writes=("wg",), tag="p4aw")
    s.op("gp", lambda h: h.dma_start(out=bgb[:], in_=di["bg"]), writes=("bgb",),
         tag="p4aw")
    for o in range(32):
        wk = f"wq{o % 2}"
        s.op("gp", lambda h, o=o: h.dma_start(
            out=wq[:, (o % 2) * D:(o % 2) * D + D].rearrange(
                "p (c n) -> p c n", n=128),
            in_=di["wqkvT"][o].rearrange("(c p) n -> p c n", p=128)),
            writes=(wk,), tag=wk)
        wqs = wq[:, (o % 2) * D:(o % 2) * D + D].rearrange("p (c n) -> p c n", n=128)
        for half in range(2):
            pk = f"psq{half}"
            builds = [lambda h, c=c, half=half, wqs=wqs: h.matmul(
                psq[half][:], wqs[:, c, :],
                b.xn[:, c * QH + half * 512: c * QH + half * 512 + 512],
                start=(c == 0), stop=(c == 31)) for c in range(32)]
            s.op("te", builds, reads=(wk, "xndst"), writes=(pk,))
            qk = f"qo{o % 2}"
            s.op("se", lambda h, o=o, half=half: h.activation(
                qo[:, (o % 2) * QH + half * 512:(o % 2) * QH + half * 512 + 512],
                psq[half][:], AF.Copy, bias=0.0, scale=SCALE),
                reads=(pk,), writes=(qk + f"h{half}",))
        s.op("sy", lambda h, o=o: h.dma_start(
            out=b.q_scr[o // 4][:, (o % 4) * QH:(o % 4 + 1) * QH],
            in_=qo[:, (o % 2) * QH:(o % 2) * QH + QH]),
            reads=(f"qo{o % 2}h0", f"qo{o % 2}h1"), writes=(),
            tag=f"qsp{o % 2}")
    for tt in range(NT):
        builds = [lambda h, c=c, tt=tt: h.matmul(
            psg[:], b.xn[:, c * QH + tt * 128: c * QH + (tt + 1) * 128],
            wg[:, c * 96:(c + 1) * 96], start=(c == 0), stop=False)
            for c in range(32)]
        builds.append(lambda h: h.matmul(psg[:], b.onesb[:, 0:128], bgb[:],
                                         start=False, stop=True))
        s.op("te", builds, reads=("wg", "bgb", "xndst", "onesb"), writes=("psg",))
        s.op("se", lambda h, tt=tt: h.activation(
            b.gates[:, tt * 96:(tt + 1) * 96], psg[:], AF.Sigmoid),
            reads=("psg",), writes=("gates",))



def _p4b(b, ps):
    nc, s, di = b.nc, b.s, b.di
    wf = ps.enter_context(nc.sbuf_tensor("wf", [128, 8 * D], BF16))
    gob = ps.enter_context(nc.sbuf_tensor("gob", [128, 4 * QH], BF16))
    a1s = ps.enter_context(nc.sbuf_tensor("a1s", [128, 32 * 16], BF16))
    a3s = ps.enter_context(nc.sbuf_tensor("a3s", [128, 32 * 16], BF16))
    a2s = ps.enter_context(nc.sbuf_tensor("a2s", [128, NO * 16], BF16))
    b1s = ps.enter_context(nc.sbuf_tensor("b1s", [16, HID], BF16))
    b3s = ps.enter_context(nc.sbuf_tensor("b3s", [16, HID], BF16))
    t1b = ps.enter_context(nc.sbuf_tensor("t1b", [16, QH], BF16))
    t3b = ps.enter_context(nc.sbuf_tensor("t3b", [16, QH], BF16))
    sil = ps.enter_context(nc.sbuf_tensor("sil", [128, QH], F32))
    ps1 = [ps.enter_context(nc.psum_tensor(f"ph1{i}", [128, 512], F32))
           for i in range(2)]
    ps3 = [ps.enter_context(nc.psum_tensor(f"ph3{i}", [128, 512], F32))
           for i in range(2)]
    pt2 = ps.enter_context(nc.psum_tensor("pt2", [16, QH], F32))

    for nm, dst in (("A1T", a1s), ("A3T", a3s), ("A2T", a2s)):
        s.op("gp", lambda h, nm=nm, dst=dst: h.dma_start(
            out=dst[:].rearrange("p (c n) -> p c n", n=16),
            in_=di[nm].rearrange("c p n -> p c n")),
            writes=(nm,), tag="p4bw")
    for nm, dst in (("B1T", b1s), ("B3T", b3s)):
        s.op("gp", lambda h, nm=nm, dst=dst: h.dma_start(out=dst[:], in_=di[nm]),
             writes=(nm,), tag="p4bw")
    for a_s, akey, tb, tk, pp in ((a1s, "A1T", t1b, "t1b", ps1),
                                  (a3s, "A3T", t3b, "t3b", ps3)):
        builds = []
        for hh in range(2):
            for c in range(32):
                builds.append(lambda h, c=c, hh=hh, a_s=a_s, pp=pp: h.matmul(
                    pp[hh][0:16, :], a_s[:, c * 16:(c + 1) * 16],
                    b.xn[:, c * QH + hh * 512: c * QH + hh * 512 + 512],
                    start=(c == 0), stop=(c == 31)))
        s.op("te", builds, reads=(akey, "xndst"), writes=("ps10", "ps11")
             if pp is ps1 else ("ps30", "ps31"))
        for hh in range(2):
            s.op("ve", lambda h, tb=tb, hh=hh, pp=pp: h.tensor_scalar(
                tb[:, hh * 512: hh * 512 + 512], pp[hh][0:16, :], LORA_SCALE,
                None, ALU.mult),
                reads=(("ps10", "ps11") if pp is ps1 else ("ps30", "ps31"))[hh:hh + 1],
                writes=(tk + str(hh),))

    for o in range(NO):
        for wsl, nm in ((0, "w1T"), (1, "w3T")):
            wk = f"wf{wsl}{o % 4}"
            s.op("gp", lambda h, o=o, wsl=wsl, nm=nm: h.dma_start(
                out=wf[:, (4 * wsl + o % 4) * D:(4 * wsl + o % 4) * D + D]
                    .rearrange("p (c n) -> p c n", n=128),
                in_=di[nm][o].rearrange("(c p) n -> p c n", p=128)),
            writes=(wk,), tag=wk)
        w1s = wf[:, (o % 4) * D:(o % 4) * D + D].rearrange("p (c n) -> p c n", n=128)
        w3s = wf[:, (4 + o % 4) * D:(4 + o % 4) * D + D].rearrange(
            "p (c n) -> p c n", n=128)
        for half in range(2):
            hs = half * 512
            bl1 = [lambda h, c=c, half=half, w1s=w1s: h.matmul(
                ps1[half][:], w1s[:, c, :],
                b.xn[:, c * QH + half * 512: c * QH + half * 512 + 512],
                start=(c == 0), stop=False) for c in range(32)]
            bl1.append(lambda h, o=o, half=half: h.matmul(
                ps1[half][:], b1s[:, o * 128:(o + 1) * 128],
                t1b[:, half * 512: half * 512 + 512], start=False, stop=True))
            s.op("te", bl1, reads=(f"wf0{o % 4}", "xndst", "B1T", "t1b0", "t1b1"),
                 writes=(f"ps1{half}",))
            bl3 = [lambda h, c=c, half=half, w3s=w3s: h.matmul(
                ps3[half][:], w3s[:, c, :],
                b.xn[:, c * QH + half * 512: c * QH + half * 512 + 512],
                start=(c == 0), stop=False) for c in range(32)]
            bl3.append(lambda h, o=o, half=half: h.matmul(
                ps3[half][:], b3s[:, o * 128:(o + 1) * 128],
                t3b[:, half * 512: half * 512 + 512], start=False, stop=True))
            s.op("te", bl3, reads=(f"wf1{o % 4}", "xndst", "B3T", "t3b0", "t3b1"),
                 writes=(f"ps3{half}",))
            s.op("se", lambda h, half=half: h.activation(
                sil[:, half * 512: half * 512 + 512], ps1[half][:], AF.Silu),
                reads=(f"ps1{half}",), writes=(f"sil{half}",))
            gk = f"gob{o % 4}"
            s.op("ve", lambda h, o=o, half=half: h.tensor_tensor(
                gob[:, (o % 4) * QH + half * 512:(o % 4) * QH + half * 512 + 512],
                sil[:, half * 512: half * 512 + 512], ps3[half][:], ALU.mult),
                reads=(f"sil{half}", f"ps3{half}"), writes=(gk + f"h{half}",))
        s.op("sy", lambda h, o=o: h.dma_start(
            out=b.g_scr[o], in_=gob[:, (o % 4) * QH:(o % 4) * QH + QH]),
            reads=(f"gob{o % 4}h0", f"gob{o % 4}h1"), writes=(),
            tag=f"gsp{o % 2}")
        if o >= 2:
            od = o - 2
            bt2 = [lambda h, od=od, hh=hh: h.matmul(
                pt2[:, hh * 512: hh * 512 + 512], a2s[:, od * 16:(od + 1) * 16],
                gob[:, (od % 4) * QH + hh * 512:(od % 4) * QH + hh * 512 + 512],
                start=(od == 0), stop=False) for hh in range(2)]
            s.op("te", bt2, reads=("A2T", f"gob{od % 4}h0", f"gob{od % 4}h1"),
                 writes=())
    for od in (NO - 2, NO - 1):
        bt2 = [lambda h, od=od, hh=hh: h.matmul(
            pt2[:, hh * 512: hh * 512 + 512], a2s[:, od * 16:(od + 1) * 16],
            gob[:, (od % 4) * QH + hh * 512:(od % 4) * QH + hh * 512 + 512],
            start=False, stop=(od == NO - 1)) for hh in range(2)]
        s.op("te", bt2, reads=("A2T", f"gob{od % 4}h0", f"gob{od % 4}h1"),
             writes=() if od < NO - 1 else ("pt2",))
    s.op("ve", lambda h: h.tensor_scalar(b.t2b[:], pt2[:], LORA_SCALE, None,
                                         ALU.mult),
         reads=("pt2",), writes=("t2b",))



def _p1(b, ps):
    nc, s, di = b.nc, b.s, b.di
    xbuf = ps.enter_context(nc.sbuf_tensor("p1xb", [128, 2 * D], F32))
    xnt = ps.enter_context(nc.sbuf_tensor("p1xnt", [128, D], BF16))
    xnkv = ps.enter_context(nc.sbuf_tensor("xnkv", [128, 32 * 512], BF16))
    wkv = ps.enter_context(nc.sbuf_tensor("wkv", [128, 2 * D], BF16))
    pst = [ps.enter_context(nc.psum_tensor(f"p1pst{i}", [128, 512], BF16))
           for i in range(2)]
    psp = [ps.enter_context(nc.psum_tensor(f"p1psp{i}", [128, 512], F32))
           for i in range(2)]
    groups = [(di["xf"], gg * 512, 512, "glob") for gg in range(4)]
    groups += [(di["xw"], gg * 512, 512, "win") for gg in range(2)]
    groups += [(di["xw"], 1024, 128, "win")]
    for gi, (src, tok0, ntok, kind) in enumerate(groups):
        nt = ntok // 128
        sink = _xn_transpose_sink(b, xnt, pst, xnkv, 512, "xnkvdst")
        _ln_tiles(b, src[tok0:tok0 + ntok], nt, sink, xbuf, xnt)
        for o in range(16):
            wk = f"wkv{o % 2}"
            s.op("gp", lambda h, o=o: h.dma_start(
                out=wkv[:, (o % 2) * D:(o % 2) * D + D].rearrange(
                    "p (c n) -> p c n", n=128),
                in_=di["wqkvT"][32 + o].rearrange("(c p) n -> p c n", p=128)),
                writes=(wk,), tag=wk)
            wks = wkv[:, (o % 2) * D:(o % 2) * D + D].rearrange(
                "p (c n) -> p c n", n=128)
            builds = [lambda h, c=c, wks=wks, ntok=ntok, o=o: h.matmul(
                psp[o % 2][:, 0:ntok],
                wks[:, c, :], xnkv[:, c * 512: c * 512 + ntok],
                start=(c == 0), stop=(c == 31)) for c in range(32)]
            s.op("te", builds, reads=(wk, "xnkvdst"), writes=(f"pkv{o % 2}",))
            kvh = o % 8
            if kind == "glob":
                dst = (b.kf if o < 8 else b.vf)[:, kvh * S + tok0: kvh * S + tok0 + ntok]
                dkey = "kf" if o < 8 else "vf"
            else:
                dst = (b.kwin if o < 8 else b.vwin)[
                    :, kvh * NWTOK + tok0: kvh * NWTOK + tok0 + ntok]
                dkey = "kwin" if o < 8 else "vwin"
            s.op("se", lambda h, dst=dst, o=o, ntok=ntok: h.activation(
                dst, psp[o % 2][:, 0:ntok], AF.Copy),
                reads=(f"pkv{o % 2}",), writes=(dkey,))



def _rope_inplace(b, buf, bufkey, ncol_per_head, cos_sb, coskey, sin_sb,
                  psr, f32a, f32b):
    s = b.s
    ntile = (ncol_per_head + 511) // 512
    for k in range(KVH):
        for u in range(ntile):
            w = min(512, ncol_per_head - u * 512)
            c0 = k * ncol_per_head + u * 512
            t0 = u * 512
            sl = buf[:, c0:c0 + w]
            pk = f"ropeps{u % 2}"
            s.op("te", lambda h, sl=sl, u=u, w=w: h.matmul(
                psr[u % 2][:, 0:w], b.psw[:], sl, start=True, stop=True),
                reads=(bufkey, "psw"), writes=(pk,))
            s.op("se", lambda h, sl=sl, w=w: h.activation(
                f32a[:, 0:w], sl, AF.Copy),
                reads=(bufkey,), writes=("ropea",))
            s.op("ve", lambda h, w=w, t0=t0: h.tensor_tensor(
                f32a[:, 0:w], f32a[:, 0:w], cos_sb[:, t0:t0 + w], ALU.mult),
                reads=("ropea", coskey), writes=("ropea",))
            s.op("ve", lambda h, w=w, u=u, t0=t0: h.tensor_tensor(
                f32b[:, 0:w], psr[u % 2][:, 0:w], sin_sb[:, t0:t0 + w],
                ALU.mult),
                reads=(pk, coskey), writes=("ropeb",))
            s.op("ve", lambda h, sl=sl, w=w: h.tensor_tensor(
                sl, f32a[:, 0:w], f32b[:, 0:w], ALU.add),
                reads=("ropea", "ropeb"), writes=(bufkey,))


def _p2(b, ps):
    nc, s, di = b.nc, b.s, b.di
    cosk = ps.enter_context(nc.sbuf_tensor("coskb", [128, S], F32))
    sinkb = ps.enter_context(nc.sbuf_tensor("sinkb", [128, S], F32))
    cosw = ps.enter_context(nc.sbuf_tensor("coswb", [128, NWTOK], F32))
    sinw = ps.enter_context(nc.sbuf_tensor("sinwb", [128, NWTOK], F32))
    posb = ps.enter_context(nc.sbuf_tensor("posb", [128, 2 * 128], F32))
    b1b = ps.enter_context(nc.sbuf_tensor("b1b", [128, 2 * 16], F32))
    b2b = ps.enter_context(nc.sbuf_tensor("b2b", [128, 2], F32))
    w1b = ps.enter_context(nc.sbuf_tensor("w1b", [128, 2 * 2048], BF16))
    w2b = ps.enter_context(nc.sbuf_tensor("w2b", [128, 2 * 2048], BF16))
    cki = ps.enter_context(nc.sbuf_tensor("cki", [128, 4 * 16 * 128], BF16))
    hid = ps.enter_context(nc.sbuf_tensor("hid", [128, 4 * 16 * 128], BF16))
    f32a = ps.enter_context(nc.sbuf_tensor("rpa", [128, 512], F32))
    f32b = ps.enter_context(nc.sbuf_tensor("rpb", [128, 512], F32))
    cvfb = ps.enter_context(nc.sbuf_tensor("cvfb", [128, 128], BF16))
    psc = [ps.enter_context(nc.psum_tensor(f"p2psc{i}", [128, 512], F32))
           for i in range(2)]
    psc2 = ps.enter_context(nc.psum_tensor("p2psc2", [128, 128], F32))
    psr = [ps.enter_context(nc.psum_tensor(f"p2psr{i}", [128, 512], F32))
           for i in range(2)]
    pst = [ps.enter_context(nc.psum_tensor(f"p2pst{i}", [128, 512], BF16))
           for i in range(2)]

    for nm, dst in (("cosk", cosk), ("sink", sinkb), ("cosw", cosw),
                    ("sinw", sinw)):
        s.op("sy", lambda h, nm=nm, dst=dst: h.dma_start(out=dst[:], in_=di[nm]),
             writes=(nm,), tag="p2tab")
    for bld, wkey in (
        (lambda h: h.dma_start(out=posb[:, 0:128], in_=di["kposT"]), "posb"),
        (lambda h: h.dma_start(out=posb[:, 128:256], in_=di["vposT"]), "posb"),
        (lambda h: h.dma_start(out=b1b[:, 0:16], in_=di["kb1"]), "b1b"),
        (lambda h: h.dma_start(out=b1b[:, 16:32], in_=di["vb1"]), "b1b"),
        (lambda h: h.dma_start(out=b2b[:, 0:1], in_=di["kb2"]), "b2b"),
        (lambda h: h.dma_start(out=b2b[:, 1:2], in_=di["vb2"]), "b2b"),
    ):
        s.op("sy", bld, writes=(wkey,), tag="p2tab")
    for side, nm in ((0, "kw2T"), (1, "vw2T")):
        s.op("gp", lambda h, side=side, nm=nm: h.dma_start(
            out=w2b[:, side * 2048: side * 2048 + 2048].rearrange(
                "p (o n) -> p o n", n=128),
            in_=di[nm].rearrange("o p n -> p o n")),
            writes=("w2b",), tag="p2w2")
    for k in range(KVH):
        s.op("gp", lambda h, k=k: h.dma_start(
            out=b.ckf[:, k * NCK + 128: k * NCK + 129],
            in_=di["memk"][k]),
            writes=("ckf",), tag="p2w2")
        s.op("gp", lambda h, k=k: h.dma_start(
            out=b.cvt2[:, k * 128:(k + 1) * 128], in_=di["memv"][k:k + 1, :]),
            writes=("cvt2",), tag="p2w2")

    for side in range(2):
        src_ = b.kf if side == 0 else b.vf
        skey = "kf" if side == 0 else "vf"
        w1nm = "kw1T" if side == 0 else "vw1T"
        for kh in range(2):
            for ki in range(4):
                k = 4 * kh + ki
                for c in range(16):
                    s.op("ve", lambda h, k=k, c=c, ki=ki, side=side,
                         src_=src_: h.tensor_scalar(
                        cki[:, ki * 2048 + c * 128: ki * 2048 + (c + 1) * 128],
                        src_[:, k * S + c: k * S + S: 16],
                        posb[:, side * 128 + k * 16 + c:
                             side * 128 + k * 16 + c + 1],
                        None, ALU.add),
                        reads=(skey, "posb"), writes=(f"cki{ki}",))
            if side == 0 and kh == 0 and getattr(b, "dbg_cki", None) is not None:
                s.barrier(engs=("sy",))
                s.op("sy", lambda h: h.dma_start(out=b.dbg_cki, in_=cki[:]),
                     reads=("cki0", "cki1", "cki2", "cki3"), writes=(),
                     tag="dump")
            for o in range(16):
                wk = f"w1b{o % 2}"
                s.op("gp", lambda h, o=o, w1nm=w1nm: h.dma_start(
                    out=w1b[:, (o % 2) * 2048:(o % 2) * 2048 + 2048].rearrange(
                        "p (c n) -> p c n", n=128),
                    in_=di[w1nm][o].rearrange("c p n -> p c n")),
                    writes=(wk,), tag=wk)
                for ki in range(4):
                    builds = [lambda h, c=c, o=o, ki=ki: h.matmul(
                        psc[ki % 2][:, 0:128],
                        w1b[:, (o % 2) * 2048 + c * 128:
                            (o % 2) * 2048 + (c + 1) * 128],
                        cki[:, ki * 2048 + c * 128: ki * 2048 + (c + 1) * 128],
                        start=(c == 0), stop=(c == 15)) for c in range(16)]
                    s.op("te", builds, reads=(wk, f"cki{ki}"),
                         writes=(f"p2c{ki % 2}",))
                    s.op("se", lambda h, o=o, ki=ki, side=side: h.activation(
                        hid[:, ki * 2048 + o * 128: ki * 2048 + (o + 1) * 128],
                        psc[ki % 2][:, 0:128], AF.Relu,
                        bias=b1b[:, side * 16 + o: side * 16 + o + 1]),
                        reads=(f"p2c{ki % 2}", "b1b"), writes=(f"hid{ki}",))
            if side == 0 and kh == 0 and getattr(b, "dbg_hid", None) is not None:
                s.barrier(engs=("sy",))
                s.op("sy", lambda h: h.dma_start(out=b.dbg_hid, in_=hid[:]),
                     reads=("hid0", "hid1", "hid2", "hid3"), writes=(),
                     tag="dump")
            for ki in range(4):
                k = 4 * kh + ki
                builds = [lambda h, o=o, ki=ki, side=side: h.matmul(
                    psc2[:, 0:128],
                    w2b[:, side * 2048 + o * 128: side * 2048 + (o + 1) * 128],
                    hid[:, ki * 2048 + o * 128: ki * 2048 + (o + 1) * 128],
                    start=(o == 0), stop=(o == 15)) for o in range(16)]
                s.op("te", builds, reads=("w2b", f"hid{ki}"), writes=("p2out",))
                if side == 0:
                    s.op("ve", lambda h, k=k: h.tensor_scalar(
                        b.ckf[:, k * NCK: k * NCK + 128], psc2[:, 0:128],
                        b2b[:, 0:1], None, ALU.add),
                        reads=("p2out", "b2b"), writes=("ckf",), sreads=("b2b",))
                else:
                    s.op("ve", lambda h: h.tensor_scalar(
                        cvfb[:], psc2[:, 0:128],
                        b2b[:, 1:2], None, ALU.add),
                        reads=("p2out", "b2b"), writes=("cvfb",), sreads=("b2b",))
                    s.op("te", lambda h: h.transpose(
                        pst[0][:, 0:128], cvfb[:], b.ident[:]),
                        reads=("cvfb", "ident"), writes=("p2t0",))
                    s.op("ve", lambda h, k=k: h.tensor_copy(
                        b.cvt1[:, k * 128:(k + 1) * 128], pst[0][:, 0:128]),
                        reads=("p2t0",), writes=("cvt1",))

    _rope_inplace(b, b.kf, "kf", S, cosk, "cosk", sinkb, psr, f32a, f32b)
    _rope_inplace(b, b.kwin, "kwin", NWTOK, cosw, "cosw", sinw, psr, f32a, f32b)

    for buf, bufkey, ncol, ntile in ((b.vf, "vf", S, 16),
                                     (b.vwin, "vwin", NWTOK, 9)):
        for k in range(KVH):
            for u0 in range(0, ntile, 4):
                nb = min(4, ntile - u0)
                c0 = k * ncol + u0 * 128
                pk = f"p2t{(u0 // 4) % 2}"
                builds = [lambda h, c0=c0, j=j, u0=u0, buf=buf: h.transpose(
                    pst[(u0 // 4) % 2][:, j * 128:(j + 1) * 128],
                    buf[:, c0 + j * 128: c0 + (j + 1) * 128], b.ident[:])
                    for j in range(nb)]
                s.op("te", builds, reads=(bufkey, "ident"), writes=(pk,))
                s.op("ve", lambda h, c0=c0, nb=nb, u0=u0, buf=buf: h.tensor_copy(
                    buf[:, c0: c0 + nb * 128],
                    pst[(u0 // 4) % 2][:, 0: nb * 128]),
                    reads=(pk,), writes=(bufkey,))


def _p3(b, ps):
    nc, s, di = b.nc, b.s, b.di
    st = b.stat
    sb = nc.sbuf_tensor
    qb = ps.enter_context(sb("qb", [128, G * QH], BF16))
    rq = ps.enter_context(sb("rq", [128, G * QH], BF16))
    pf = ps.enter_context(sb("pf", [128, G * 2 * 2048], BF16))
    pc = ps.enter_context(sb("pcb", [128, G * NCK], BF16))
    psl = ps.enter_context(sb("pslb", [128, G * 192], BF16))
    smb = ps.enter_context(sb("smb", [128, G * NCK], F32))
    ssb = ps.enter_context(sb("ssb", [128, 192], F32))
    imps = ps.enter_context(sb("imps", [128, 128], F32))
    ya = ps.enter_context(sb("ya", [128, 128], F32))
    yb = ps.enter_context(sb("yb", [128, 128], F32))
    geb = ps.enter_context(sb("geb", [128, 128], F32))
    fm = ps.enter_context(sb("fm", [128, 128], BF16))
    fmq = ps.enter_context(sb("fmq", [32, 512], BF16))
    zdump = ps.enter_context(sb("zdump", [128, 128], BF16))
    ptA = ps.enter_context(sb("ptA", [128, 2 * 512], BF16))
    ptM = ps.enter_context(sb("ptM", [1, 512], BF16))
    ptB = ps.enter_context(sb("ptB", [64, 512], BF16))
    asp = ps.enter_context(sb("asp", [128, 2 * 512], BF16))
    cosq = ps.enter_context(sb("cosqb", [128, QH], F32))
    sinq = ps.enter_context(sb("sinqb", [128, QH], F32))
    cmb = ps.enter_context(sb("cmb", [128, NT * NCK], F32))
    smvb = ps.enter_context(sb("smvb", [128, NT * 192], F32))
    t2t = ps.enter_context(sb("t2tb", [128, 3200], BF16))
    rpa = ps.enter_context(sb("rpa3", [128, 512], F32))
    rpb = ps.enter_context(sb("rpb3", [128, 512], F32))
    psC = ps.enter_context(nc.psum_tensor("p3psC", [128, 2048], F32))
    psO = ps.enter_context(nc.psum_tensor("p3psO", [128, 512], F32))
    psT = [ps.enter_context(nc.psum_tensor(f"p3psT{i}", [128, 1024], BF16))
           for i in range(2)]
    psR = ps.enter_context(nc.psum_tensor("p3psR", [128, 512], F32))

    for bld, wkey in (
        (lambda h: h.dma_start(out=cosq[:], in_=di["cosq"]), "cosq"),
        (lambda h: h.dma_start(out=sinq[:], in_=di["sinq"]), "cosq"),
        (lambda h: h.dma_start(out=cmb[:].rearrange("p (t n) -> p t n", n=NCK),
                               in_=di["CM"].rearrange("t p n -> p t n")), "cm"),
        (lambda h: h.dma_start(out=smvb[:].rearrange("p (t n) -> p t n", n=192),
                               in_=di["SM"].rearrange("t p n -> p t n")), "smv"),
        (lambda h: h.dma_start(out=t2t[:], in_=di["T2"]), "t2t"),
    ):
        s.op("sy", bld, writes=(wkey,), tag="p3tab")

    PSCK = ("psCg0", "psCg1", "psCg2", "psCg3")
    for kvh in range(KVH):
        s.op("sy", lambda h, kvh=kvh: h.dma_start(out=qb[:], in_=b.q_scr[kvh]),
             reads=(), writes=("qld",), tag="qld")
        for g in range(G):
            for uu in range(2):
                c0 = g * QH + uu * 512
                t0 = uu * 512
                pk = "ropeps0" if uu == 0 else "ropeps1"
                pp = psR[:, 0:512]
                s.op("te", lambda h, c0=c0: h.matmul(
                    psR[:, 0:512], b.psw[:], qb[:, c0:c0 + 512],
                    start=True, stop=True),
                    reads=("qld", "psw"), writes=("psR",))
                s.op("se", lambda h, c0=c0: h.activation(
                    rpa[:], qb[:, c0:c0 + 512], AF.Copy),
                    reads=("qld",), writes=("ropea",))
                s.op("ve", lambda h, t0=t0: h.tensor_tensor(
                    rpa[:], rpa[:], cosq[:, t0:t0 + 512], ALU.mult),
                    reads=("ropea", "cosq"), writes=("ropea",))
                s.op("ve", lambda h, t0=t0: h.tensor_tensor(
                    rpb[:], psR[:, 0:512], sinq[:, t0:t0 + 512], ALU.mult),
                    reads=("psR", "cosq"), writes=("ropeb",))
                s.op("ve", lambda h, c0=c0: h.tensor_tensor(
                    rq[:, c0:c0 + 512], rpa[:], rpb[:], ALU.add),
                    reads=("ropea", "ropeb"), writes=("rq",))

        PSCKALL = ("psCg0", "psCg1", "psCg2", "psCg3")

        def pfbase(g, qt):
            return (g * 2 + qt % 2) * 2048

        def emit_comp(qt):
            for g in range(G):
                s.op("te", lambda h, kvh=kvh, g=g, qt=qt: h.matmul(
                    psC[:, g * 512: g * 512 + NCK],
                    qb[:, g * QH + qt * 128: g * QH + (qt + 1) * 128],
                    b.ckf[:, kvh * NCK: (kvh + 1) * NCK],
                    start=True, stop=True),
                    reads=("qld", "ckf"), writes=(f"psCg{g}",))
                s.op("ve", lambda h, g=g, qt=qt: h.tensor_tensor(
                    smb[:, g * NCK:(g + 1) * NCK],
                    psC[:, g * 512: g * 512 + NCK],
                    cmb[:].rearrange("p (t n) -> p t n", n=NCK)[:, qt, :],
                    ALU.add),
                    reads=(f"psCg{g}", "cm"), writes=(f"smb{g}",))
            s.op("ve", lambda h: h.reduce_max(
                st[:, 16:20], smb[:].rearrange("p (g x) -> p g x", x=NCK),
                axis=AX, negate=True),
                reads=("smb0", "smb1", "smb2", "smb3"), writes=("nmc",),
                hazard=True)
            for g in range(G):
                s.op("se", lambda h, g=g: h.activation(
                    pc[:, g * NCK:(g + 1) * NCK], smb[:, g * NCK:(g + 1) * NCK],
                    AF.Exp, bias=st[:, 16 + g:17 + g],
                    accum_out=st[:, 20 + g:21 + g]),
                    reads=(f"smb{g}", "nmc"), writes=(f"pc{g}", f"csum{g}"))
            s.op("ve", lambda h: h.reciprocal(st[:, 24:28], st[:, 20:24]),
                 reads=("csum0", "csum1", "csum2", "csum3"), writes=("rcpc",),
                 hazard=True)
            gc0 = qt * 96 + 12 * kvh
            s.op("ve", lambda h, gc0=gc0: h.tensor_tensor(
                st[:, 28:32], st[:, 24:28],
                b.gates[:, gc0:gc0 + 12:3], ALU.mult),
                reads=("rcpc", "gates"), writes=("rsc",))
            for g in range(G):
                s.op("ve", lambda h, g=g: h.tensor_scalar(
                    pc[:, g * NCK:(g + 1) * NCK], pc[:, g * NCK:(g + 1) * NCK],
                    st[:, 28 + g:29 + g], None, ALU.mult),
                    reads=(f"pc{g}",), sreads=("rsc",), writes=(f"pc{g}",))

        def emit_slide(qt):
            for g in range(G):
                s.op("te", lambda h, kvh=kvh, g=g, qt=qt: h.matmul(
                    psC[:, g * 512: g * 512 + 192],
                    rq[:, g * QH + qt * 128: g * QH + (qt + 1) * 128],
                    b.kwin[:, kvh * NWTOK + qt * 128:
                           kvh * NWTOK + qt * 128 + 192],
                    start=True, stop=True),
                    reads=("rq", "kwin"), writes=(f"psCg{g}",))
                s.op("ve", lambda h, g=g, qt=qt: h.tensor_tensor(
                    ssb[:], psC[:, g * 512: g * 512 + 192],
                    smvb[:].rearrange("p (t n) -> p t n", n=192)[:, qt, :],
                    ALU.add),
                    reads=(f"psCg{g}", "smv"), writes=("ssb",))
                s.op("ve", lambda h: h.reduce_max(
                    st[:, 48:49], ssb[:], axis=AX, negate=True),
                    reads=("ssb",), writes=("snm",), hazard=True)
                s.op("se", lambda h, g=g: h.activation(
                    psl[:, g * 192:(g + 1) * 192], ssb[:], AF.Exp,
                    bias=st[:, 48:49], accum_out=st[:, 49:50]),
                    reads=("ssb", "snm"), writes=(f"psl{g}", "ssum"))
                s.op("ve", lambda h: h.reciprocal(st[:, 50:51], st[:, 49:50]),
                     reads=("ssum",), writes=("srcp",), hazard=True)
                gcol = qt * 96 + (4 * kvh + g) * 3 + 2
                s.op("ve", lambda h, gcol=gcol: h.tensor_tensor(
                    st[:, 51:52], st[:, 50:51], b.gates[:, gcol:gcol + 1],
                    ALU.mult),
                    reads=("srcp", "gates"), writes=("srs",))
                s.op("ve", lambda h, g=g: h.tensor_scalar(
                    psl[:, g * 192:(g + 1) * 192],
                    psl[:, g * 192:(g + 1) * 192],
                    st[:, 51:52], None, ALU.mult),
                    reads=(f"psl{g}",), sreads=("srs",), writes=(f"psl{g}",))

        def emit_qkpre(qt):
            Lt = 1152 + 128 * qt
            nch = (Lt + 511) // 512
            for c in range(nch):
                N = min(512, Lt - 512 * c)
                s.op("te", [lambda h, qt=qt, c=c, N=N, kvh=kvh: h.matmul(
                    psC[:, c * 512: c * 512 + N],
                    rq[:, qt * 128: (qt + 1) * 128],
                    b.kf[:, kvh * S + c * 512: kvh * S + c * 512 + N],
                    start=True, stop=False)],
                    reads=("rq", "kf"), writes=(f"psCg{c}",))

        def emit_sel(qt):
                s.op("ve", lambda h: h.tensor_tensor(
                    ya[:], smb[:, 0:128], smb[:, NCK:NCK + 128], ALU.add),
                    reads=("smb0", "smb1"), writes=("ya",))
                s.op("ve", lambda h: h.tensor_tensor(
                    yb[:], smb[:, 2 * NCK:2 * NCK + 128],
                    smb[:, 3 * NCK:3 * NCK + 128], ALU.add),
                    reads=("smb2", "smb3"), writes=("yb",))
                s.op("ve", lambda h: h.tensor_tensor(imps[:], ya[:], yb[:], ALU.add),
                     reads=("ya", "yb"), writes=("imps",))
                s.op("ve", lambda h: h.reduce_max(st[:, 32:33], imps[:], axis=AX),
                     reads=("imps",), writes=("mx",), hazard=True)
                s.op("ve", lambda h: h.tensor_scalar(
                    st[:, 33:34], st[:, 32:33], 0.25, -1000.0, ALU.mult, ALU.max),
                    reads=("mx",), sreads=("mx",), writes=("M",))
                s.op("ve", lambda h: h.tensor_scalar(
                    st[:, 34:35], st[:, 33:34], -1.0, None, ALU.mult),
                    reads=("M",), sreads=("M",), writes=("nM",))
                s.op("se", lambda h: h.activation(
                    zdump[:], imps[:], AF.Exp, bias=st[:, 34:35], scale=0.25,
                    accum_out=st[:, 35:36]),
                    reads=("imps", "nM"), writes=("zs", "zdumpb"))
                s.op("se", lambda h: h.activation(
                    st[:, 36:37], st[:, 33:34], AF.Exp, bias=st[:, 60:61],
                    scale=-1.0),
                    reads=("M", "c1000"), writes=("e2",))
                s.op("ve", lambda h: h.tensor_tensor(
                    st[:, 37:38], st[:, 35:36], st[:, 36:37], ALU.add),
                    reads=("zs", "e2"), writes=("z",))
                s.op("se", lambda h: h.activation(st[:, 38:39], st[:, 37:38], AF.Ln),
                     reads=("z",), writes=("lnz",))
                s.op("ve", lambda h: h.tensor_tensor(
                    st[:, 39:40], st[:, 33:34], st[:, 38:39], ALU.add),
                    reads=("M", "lnz"), writes=("mlz",))
                s.op("ve", lambda h: h.tensor_scalar(
                    st[:, 40:41], st[:, 39:40], LN1E10, 4.0, ALU.add, ALU.mult),
                    reads=("mlz",), sreads=("mlz",), writes=("t2c",))
                cur, nxt = imps, ya
                curk, nxtk = "imps", "ya"
                for it in range(3):
                    s.op("ve", lambda h, cur=cur: h.reduce_max(
                        st[:, 41:42], cur[:], axis=AX),
                        reads=(curk,), writes=("mi",), hazard=True)
                    s.op("ve", lambda h, cur=cur: h.tensor_scalar(
                        geb[:], cur[:], st[:, 41:42], None, ALU.is_ge),
                        reads=(curk,), sreads=("mi",), writes=("geb",))
                    s.op("ve", lambda h, cur=cur, nxt=nxt: h.scalar_tensor_tensor(
                        nxt[:], geb[:], -1e33, cur[:], ALU.mult, ALU.add),
                        reads=("geb", curk), writes=(nxtk,))
                    cur, nxt = nxt, (yb if nxt is ya else ya)
                    curk, nxtk = nxtk, ("yb" if nxtk == "ya" else "ya")
                s.op("ve", lambda h, cur=cur: h.reduce_max(
                    st[:, 42:43], cur[:], axis=AX),
                    reads=(curk,), writes=("q4",), hazard=True)
                s.op("ve", lambda h: h.tensor_scalar(
                    geb[:], imps[:], st[:, 42:43], None, ALU.is_ge),
                    reads=("imps",), sreads=("q4",), writes=("geb",))
                s.op("ve", lambda h: h.tensor_scalar(
                    ya[:], imps[:], st[:, 40:41], None, ALU.is_gt),
                    reads=("imps",), sreads=("t2c",), writes=("ya",))
                s.op("ve", lambda h: h.tensor_tensor(yb[:], geb[:], ya[:], ALU.mult),
                     reads=("geb", "ya"), writes=("yb",))
                s.op("ve", lambda h: h.tensor_scalar(
                    fm[:], yb[:], 1.0, V30, ALU.subtract, ALU.mult),
                    reads=("yb",), writes=("fm",))
                builds = [lambda h, c=c: h.transpose(
                    psT[0][0:32, c * 128:(c + 1) * 128], fm[:, c * 32:(c + 1) * 32],
                    b.ident[:]) for c in range(4)]
                s.op("te", builds, reads=("fm", "ident"), writes=("psT0",))
                s.op("ve", lambda h: h.tensor_copy(fmq[:], psT[0][0:32, 0:512]),
                     reads=("psT0",), writes=("fmq",))

        def emit_fine_g(qt, g):
            Lt = 1152 + 128 * qt
            nch = (Lt + 511) // 512
            STT = 1024 - 128 * qt
            pb = pfbase(g, qt)
            for c in range(nch):
                N = min(512, Lt - 512 * c)
                builds = []
                if g > 0:
                    builds.append(
                        lambda h, g=g, qt=qt, c=c, N=N, kvh=kvh: h.matmul(
                            psC[:, c * 512: c * 512 + N],
                            rq[:, g * QH + qt * 128: g * QH + (qt + 1) * 128],
                            b.kf[:, kvh * S + c * 512: kvh * S + c * 512 + N],
                            start=True, stop=False))
                builds += [
                    lambda h, g=g, c=c, N=N: h.matmul(
                        psC[:, c * 512: c * 512 + N],
                        fmq[:, c * 128:(c + 1) * 128], b.el[:, 0:N],
                        start=False, stop=False),
                    lambda h, c=c, N=N, STT=STT: h.matmul(
                        psC[:, c * 512: c * 512 + N], b.dl[:],
                        t2t[:, STT + 512 * c: STT + 512 * c + N],
                        start=False, stop=True),
                ]
                s.op("te", builds, reads=("rq", "kf", "fmq", "el", "dl", "t2t"),
                     writes=(f"psCg{c}",))
                s.op("ve", lambda h, c=c, N=N: h.reduce_max(
                    st[:, 52 + c:53 + c], psC[:, c * 512: c * 512 + N], axis=AX),
                    reads=(f"psCg{c}",), writes=(f"pm{c}",), hazard=True)
            s.op("ve", lambda h, nch=nch: h.reduce_max(
                st[:, 44:45], st[:, 52:52 + nch], axis=AX, negate=True),
                reads=tuple(f"pm{c}" for c in range(nch)), writes=("nmf",),
                hazard=True)
            s.op("se", lambda h, g=g, Lt=Lt, pb=pb: h.activation(
                pf[:, pb: pb + Lt], psC[:, 0:Lt], AF.Exp,
                bias=st[:, 44:45], accum_out=st[:, 45:46]),
                reads=PSCKALL + ("nmf",), writes=(f"pf{g}{qt % 2}", "fsum"))
            s.op("ve", lambda h: h.reciprocal(st[:, 46:47], st[:, 45:46]),
                 reads=("fsum",), writes=("frcp",), hazard=True)
            gcol = qt * 96 + (4 * kvh + g) * 3 + 1
            s.op("ve", lambda h, gcol=gcol: h.tensor_tensor(
                st[:, 47:48], st[:, 46:47], b.gates[:, gcol:gcol + 1],
                ALU.mult),
                reads=("frcp", "gates"), writes=("frs",))
            s.op("ve", lambda h, g=g, Lt=Lt, pb=pb: h.tensor_scalar(
                pf[:, pb: pb + Lt], pf[:, pb: pb + Lt],
                st[:, 47:48], None, ALU.mult),
                reads=(f"pf{g}{qt % 2}",), sreads=("frs",),
                writes=(f"pf{g}{qt % 2}",))

        def emit_fineav_chunks(qt, us):
            par = qt % 2
            pfk = tuple(f"pf{g}{par}" for g in range(G))
            for u in us:
                builds = [lambda h, g=g, u=u, qt=qt: h.transpose(
                    psT[u % 2][:, g * 128:(g + 1) * 128],
                    pf[:, pfbase(g, qt) + u * 128: pfbase(g, qt) + (u + 1) * 128],
                    b.ident[:]) for g in range(G)]
                s.op("te", builds, reads=pfk + ("ident",),
                     writes=(f"psT{u % 2}",))
                if u % 2 == 0:
                    s.op("ve", lambda h, u=u: h.tensor_copy(
                        ptA[:, (u % 2) * 512:(u % 2) * 512 + 512],
                        psT[u % 2][:, 0:512]),
                        reads=(f"psT{u % 2}",), writes=(f"ptA{u % 2}",))
                else:
                    s.op("se", lambda h, u=u: h.activation(
                        ptA[:, (u % 2) * 512:(u % 2) * 512 + 512],
                        psT[u % 2][:, 0:512], AF.Copy),
                        reads=(f"psT{u % 2}",), writes=(f"ptA{u % 2}",))
                s.op("te", lambda h, kvh=kvh, u=u: h.matmul(
                    psO[:], b.vf[:, kvh * S + u * 128: kvh * S + (u + 1) * 128],
                    ptA[:, (u % 2) * 512:(u % 2) * 512 + 512],
                    start=False, stop=False),
                    reads=(f"ptA{u % 2}", "vf"), writes=("psO",))

        def emit_slide_av(qt):
            builds = [lambda h, g=g: h.transpose(
                psT[0][:, g * 128:(g + 1) * 128],
                psl[:, g * 192: g * 192 + 128], b.ident[:]) for g in range(G)]
            s.op("te", builds, reads=("psl0", "psl1", "psl2", "psl3", "ident"),
                 writes=("psT0",))
            builds = [lambda h, g=g: h.transpose(
                psT[1][0:64, g * 128:(g + 1) * 128],
                psl[:, g * 192 + 128: g * 192 + 192], b.ident[:])
                for g in range(G)]
            s.op("te", builds, reads=("psl0", "psl1", "psl2", "psl3", "ident"),
                 writes=("psT1",))
            s.op("ve", lambda h: h.tensor_copy(ptA[:, 0:512], psT[0][:, 0:512]),
                 reads=("psT0",), writes=("ptA0",))
            s.op("se", lambda h: h.activation(ptB[:], psT[1][0:64, 0:512],
                                              AF.Copy),
                 reads=("psT1",), writes=("ptB",))
            s.op("te", [lambda h, kvh=kvh, qt=qt: h.matmul(
                psO[:],
                b.vwin[:, kvh * NWTOK + qt * 128: kvh * NWTOK + (qt + 1) * 128],
                ptA[:, 0:512], start=False, stop=False),
                lambda h, kvh=kvh, qt=qt: h.matmul(
                psO[:],
                b.vwin[0:64, kvh * NWTOK + (qt + 1) * 128:
                       kvh * NWTOK + (qt + 2) * 128],
                ptB[:], start=False, stop=True)],
                reads=("ptA0", "ptB", "vwin"), writes=("psO",))
            s.op("ve", lambda h, qt=qt: h.tensor_copy(
                asp[:, (qt % 2) * 512:(qt % 2) * 512 + 512], psO[:]),
                reads=("psO",), writes=(f"asp{qt % 2}",))
            s.op("sy", lambda h, kvh=kvh, qt=qt: h.dma_start(
                out=b.a_scr[kvh].rearrange("p (g t) -> p g t", t=QH)
                [:, :, qt * 128:(qt + 1) * 128],
                in_=asp[:, (qt % 2) * 512:(qt % 2) * 512 + 512].rearrange(
                    "p (g t) -> p g t", t=128)),
                reads=(f"asp{qt % 2}",), writes=(), tag=f"asp{qt % 2}")

        def emit_compav(qt):
            builds = [lambda h, g=g: h.transpose(
                psT[0][:, g * 128:(g + 1) * 128], pc[:, g * NCK: g * NCK + 128],
                b.ident[:]) for g in range(G)]
            s.op("te", builds, reads=("pc0", "pc1", "pc2", "pc3", "ident"),
                 writes=("psT0",))
            builds = [lambda h, g=g: h.transpose(
                psT[1][0:1, g * 128:(g + 1) * 128],
                pc[:, g * NCK + 128: g * NCK + 129], b.ident[:])
                for g in range(G)]
            s.op("te", builds, reads=("pc0", "pc1", "pc2", "pc3", "ident"),
                 writes=("psT1",))
            s.op("ve", lambda h: h.tensor_copy(ptA[:, 0:512], psT[0][:, 0:512]),
                 reads=("psT0",), writes=("ptA0",))
            s.op("se", lambda h: h.activation(ptM[:], psT[1][0:1, 0:512],
                                              AF.Copy),
                 reads=("psT1",), writes=("ptM",))
            s.op("te", [lambda h, kvh=kvh: h.matmul(
                psO[:], b.cvt1[:, kvh * 128:(kvh + 1) * 128], ptA[:, 0:512],
                start=True, stop=False),
                lambda h, kvh=kvh: h.matmul(
                psO[:], b.cvt2[:, kvh * 128:(kvh + 1) * 128], ptM[:],
                start=False, stop=False)],
                reads=("ptA0", "ptM", "cvt1", "cvt2"), writes=("psO",))

        for qt in range(NT):
            Ltp = 1152 + 128 * (qt - 1)
            nup = Ltp // 128 if qt > 0 else 0
            emit_comp(qt)
            if qt > 0:
                emit_slide(qt - 1)
            emit_qkpre(qt)
            emit_sel(qt)
            cuts = [int(nup * f) for f in (0.4, 0.55, 0.7, 0.85)] + [nup]
            if qt > 0:
                emit_fineav_chunks(qt - 1, range(0, cuts[0]))
            emit_fine_g(qt, 0)
            if qt > 0:
                emit_fineav_chunks(qt - 1, range(cuts[0], cuts[1]))
            emit_fine_g(qt, 1)
            if qt > 0:
                emit_fineav_chunks(qt - 1, range(cuts[1], cuts[2]))
            emit_fine_g(qt, 2)
            if qt > 0:
                emit_fineav_chunks(qt - 1, range(cuts[2], cuts[3]))
            emit_fine_g(qt, 3)
            if qt > 0:
                emit_fineav_chunks(qt - 1, range(cuts[3], nup))
                emit_slide_av(qt - 1)
            emit_compav(qt)
        emit_slide(NT - 1)
        emit_fineav_chunks(NT - 1, range(0, (1152 + 128 * (NT - 1)) // 128))
        emit_slide_av(NT - 1)



def _p5(b, ps):
    nc, s, di = b.nc, b.s, b.di
    sb = nc.sbuf_tensor
    attn = ps.enter_context(sb("attnsb", [128, H * QH], BF16))
    wo = ps.enter_context(sb("wob", [128, 2 * 16 * 512], BF16))
    w2s = ps.enter_context(sb("w2sb", [128, 4 * 512], BF16))
    gs = ps.enter_context(sb("gsb", [128, 4 * QH], BF16))
    Fb = ps.enter_context(sb("Fb", [128, 8 * 512], F32))
    xb = ps.enter_context(sb("xb5", [128, 2 * 512], F32))
    ob = ps.enter_context(sb("ob5", [128, 2 * 512], F32))
    rc = ps.enter_context(sb("rc5", [128, 512], F32))
    B2s = ps.enter_context(sb("B2sb", [16, D], BF16))
    psD = [ps.enter_context(nc.psum_tensor(f"p5psD{i}", [128, 512], F32))
           for i in range(8)]

    s.op("gp", lambda h: h.dma_start(out=B2s[:], in_=di["B2T"]),
         writes=("B2s",), tag="p5w")
    for kvh in range(KVH):
        s.op("sy", lambda h, kvh=kvh: h.dma_start(
            out=attn[:, kvh * G * QH:(kvh + 1) * G * QH], in_=b.a_scr[kvh]),
            writes=("attn",), tag="p5attn")
    for o in range(8):
        for hh in range(NO):
            wk = f"w2l{hh % 4}"
            gk = f"gl{hh % 4}"
            s.op("gp", lambda h, o=o, hh=hh: h.dma_start(
                out=w2s[:, (hh % 4) * 512:(hh % 4) * 512 + 512],
                in_=di["w2T"][o][hh]),
                writes=(wk,), tag=wk)
            s.op("sy", lambda h, hh=hh: h.dma_start(
                out=gs[:, (hh % 4) * QH:(hh % 4) * QH + QH], in_=b.g_scr[hh]),
                writes=(gk,), tag=gk)
            builds = [lambda h, hh=hh, tt=tt: h.matmul(
                psD[tt][:], gs[:, (hh % 4) * QH + tt * 128:
                               (hh % 4) * QH + (tt + 1) * 128],
                w2s[:, (hh % 4) * 512:(hh % 4) * 512 + 512],
                start=(hh == 0), stop=False) for tt in range(NT)]
            s.op("te", builds, reads=(wk, gk),
                 writes=tuple(f"psD{tt}" for tt in range(NT)))
        builds = [lambda h, o=o, tt=tt: h.matmul(
            psD[tt][:], b.t2b[:, tt * 128:(tt + 1) * 128],
            B2s[:, o * 512:(o + 1) * 512], start=False, stop=True)
            for tt in range(NT)]
        s.op("te", builds, reads=("t2b", "B2s"),
             writes=tuple(f"psD{tt}" for tt in range(NT)))
        for tt in range(NT):
            s.op("se", lambda h, tt=tt: h.activation(
                Fb[:, tt * 512:(tt + 1) * 512], psD[tt][:], AF.Silu),
                reads=(f"psD{tt}",), writes=(f"F{tt}",))
        for ah in range(2):
            wk = f"wo{ah}"
            s.op("gp", lambda h, o=o, ah=ah: h.dma_start(
                out=wo[:, ah * 8192: ah * 8192 + 8192].rearrange(
                    "p (a n) -> p a n", n=512),
                in_=di["woT"][o][ah * 16:(ah + 1) * 16].rearrange(
                    "a p n -> p a n")),
                writes=(wk,), tag=wk)
            for tt in range(NT):
                builds = [lambda h, a=a, ah=ah, tt=tt: h.matmul(
                    psD[tt][:],
                    attn[:, (ah * 16 + a) * QH + tt * 128:
                         (ah * 16 + a) * QH + (tt + 1) * 128],
                    wo[:, ah * 8192 + a * 512: ah * 8192 + (a + 1) * 512],
                    start=(ah == 0 and a == 0), stop=(ah == 1 and a == 15))
                    for a in range(16)]
                s.op("te", builds, reads=(wk, "attn", f"F{tt}"),
                     writes=(f"psD{tt}",))
        for tt in range(NT):
            xk = f"x5{tt % 2}"
            s.op("sy", lambda h, o=o, tt=tt: h.dma_start(
                out=xb[:, (tt % 2) * 512:(tt % 2) * 512 + 512],
                in_=di["xo"][tt * 128:(tt + 1) * 128, o * 512:(o + 1) * 512]),
                writes=(xk,), tag=xk)
            s.op("ve", lambda h, tt=tt: h.tensor_tensor(
                rc[:], psD[tt][:], Fb[:, tt * 512:(tt + 1) * 512], ALU.mult),
                reads=(f"psD{tt}", f"F{tt}"), writes=("rc",))
            s.op("ve", lambda h, tt=tt: h.tensor_tensor(
                ob[:, (tt % 2) * 512:(tt % 2) * 512 + 512], rc[:],
                xb[:, (tt % 2) * 512:(tt % 2) * 512 + 512], ALU.add),
                reads=("rc", xk), writes=(f"ob{tt % 2}",))
            s.op("sy", lambda h, o=o, tt=tt: h.dma_start(
                out=b.out[tt * 128:(tt + 1) * 128, o * 512:(o + 1) * 512],
                in_=ob[:, (tt % 2) * 512:(tt % 2) * 512 + 512]),
                reads=(f"ob{tt % 2}",), writes=(), tag=f"o5{tt % 2}")


def _emit_debug(b):
    s = b.s
    for name, ap in b.dbg.items():
        src = getattr(b, "dbgsrc", {}).get(name)
        if src is None:
            continue
        sap, reads = src
        s.barrier(engs=("sy",))
        s.op("sy", lambda h, ap=ap, sap=sap: h.dma_start(out=ap, in_=sap),
             reads=reads, writes=())



def kernel(**inputs):
    in_maps = stage_inputs(inputs)
    nc = build_program()
    res = run_bass_kernel_spmd(nc, in_maps, core_ids=list(range(NCORES)))
    out = np.empty((B, S, D), dtype=np.float32)
    for core in range(NCORES):
        bb, h = core // 2, core % 2
        out[bb, h * QH:(h + 1) * QH, :] = res.results[core]["out"]
    return out
